# version 24
# baseline (speedup 1.0000x reference)
import math
from contextlib import ExitStack, contextmanager
import numpy as np
import concourse.bass as bass
import concourse.mybir as mybir

F32 = mybir.dt.float32
BF16 = mybir.dt.bfloat16
U32 = mybir.dt.uint32
AF = mybir.ActivationFunctionType
OP = mybir.AluOpType
AX = mybir.AxisListType

T = 4096
D = 1024
NH = 16
DH = 64
NP = 8
NT = T // 128
ND = D // 128
LAM = 1.0507009873554805
ALPHA = 1.6732632423543772
LA = LAM * ALPHA
LN_LA = math.log(LA)
SCALE = DH ** -0.5
OSCALE = T ** -0.5
LN_EPS = 1e-5

ENGS = ("tensor", "vector", "scalar", "gpsimd", "sync")


class AS:

    def __init__(self, nc, block, stack):
        self.nc, self.block, self.stack = nc, block, stack
        self.sems = {}
        for e in ENGS:
            self.sems[(e, "c")] = stack.enter_context(nc.semaphore(f"as_{e}_c"))
        self.cnt = {k: 0 for k in self.sems}
        self.queue = {e: [] for e in ENGS}
        self.waited = {e: {} for e in ENGS}
        self.last_w = {}
        self.readers = {}
        self.flushed = False

    def dstream(self, e, stream):
        key = (e, "d", stream)
        if key not in self.sems:
            self.sems[key] = self.stack.enter_context(
                self.nc.semaphore(f"as_{e}_d_{stream}"))
            self.cnt[key] = 0
        return key

    def op(self, e, fn, reads=(), writes=(), xreads=(), dma=False, stream=None):
        if dma:
            mykey = self.dstream(e, stream or e)
            inc = 16
        else:
            mykey = (e, "c")
            inc = 1
        deps = {}

        def add_dep(dk, dc):
            if dk == mykey and not dma and e == "tensor":
                return
            deps[dk] = max(deps.get(dk, 0), dc)

        for r in tuple(reads) + tuple(writes) + tuple(xreads):
            lw = self.last_w.get(r)
            if lw is not None:
                add_dep(*lw)
        for r in tuple(writes) + tuple(xreads):
            for rd in self.readers.get(r, ()):
                add_dep(*rd)
        mycnt = self.cnt[mykey] + inc
        self.cnt[mykey] = mycnt
        waits = []
        w = self.waited[e]
        for dk, dc in deps.items():
            if w.get(dk, 0) < dc:
                w[dk] = dc
                waits.append((dk, dc))
        sems = self.sems
        me = sems[mykey]

        def closure(eng, waits=waits, fn=fn, me=me, inc=inc):
            for dk, dc in waits:
                eng.wait_ge(sems[dk], dc)
            try:
                instr = fn(eng)
            except Exception:
                import sys
                print("FAILING OP waits:", waits, "inc:", inc, file=sys.stderr)
                raise
            instr.then_inc(me, inc)

        self.queue[e].append(closure)
        for r in tuple(writes):
            self.last_w[r] = (mykey, mycnt)
            self.readers[r] = []
        for r in tuple(reads) + tuple(xreads):
            self.readers.setdefault(r, []).append((mykey, mycnt))
        return (mykey, mycnt)

    def fence(self):
        snap = [(k, c) for k, c in self.cnt.items() if c > 0]
        self.closed = getattr(self, "closed", set())
        for e in ENGS:
            waits = []
            for k, c in snap:
                if k == (e, "c"):
                    continue
                if self.waited[e].get(k, 0) < c:
                    self.waited[e][k] = c
                    waits.append((k, c))
            if waits:
                def wfn(eng, waits=waits):
                    for k, c in waits:
                        eng.wait_ge(self.sems[k], c)
                self.queue[e].append(wfn)
        for k, c in snap:
            if len(k) == 3:
                self.closed.add(k)

    def flush(self):
        assert not self.flushed
        self.flushed = True
        for key, cnt in list(self.cnt.items()):
            if len(key) == 3 and cnt > 0:
                e = key[0]
                if self.waited[e].get(key, 0) < cnt:
                    self.waited[e][key] = cnt
                    self.queue[e].append(
                        lambda eng, key=key, cnt=cnt: eng.wait_ge(self.sems[key], cnt))
        for e in ENGS:
            q = self.queue[e]
            if not q:
                continue

            def runner(eng, q=q):
                for c in q:
                    c(eng)

            getattr(self.block, e)(runner)


class Ctx:
    def __init__(self, nc, block, stack):
        self.nc, self.block, self.stack = nc, block, stack

    def sbuf(self, name, shape, dtype, stack=None):
        return (stack or self.stack).enter_context(self.nc.sbuf_tensor(name, shape, dtype))

    def psum(self, name, shape, dtype=F32, stack=None):
        return (stack or self.stack).enter_context(self.nc.psum_tensor(name, shape, dtype))


def emit_ident(ctx, au):
    identB = ctx.sbuf("identB", [128, 128], BF16)
    identF = ctx.sbuf("identF", [128, 128], F32)
    au.op("gpsimd", lambda g: g.iota(identB[:, :], pattern=[[1, 128]], base=0,
                                     channel_multiplier=-1, allow_small_or_imprecise_dtypes=True),
          writes=[("identB",)])
    au.op("gpsimd", lambda g: g.iota(identF[:, :], pattern=[[1, 128]], base=0,
                                     channel_multiplier=-1, allow_small_or_imprecise_dtypes=True),
          writes=[("identF",)])
    au.op("vector", lambda v: v.tensor_scalar(identB[:, :], identB[:, :], 0.0, None, OP.is_equal),
          reads=[("identB",)], writes=[("identB",)])
    au.op("vector", lambda v: v.tensor_scalar(identF[:, :], identF[:, :], 0.0, None, OP.is_equal),
          reads=[("identF",)], writes=[("identF",)])
    ccol = ctx.sbuf("ccol", [128, 2], F32)
    au.op("vector", lambda v: v.memset(ccol[:, 0:1], LN_LA), writes=[("ccol",)])
    ctx.identB, ctx.identF, ctx.ccol = identB, identF, ccol


def emit_tm_to_fm(ctx, au, pstack, src_dram, dst_fm, nt, nd, tag, also_bf16_dram=None,
                  src_key=None):
    identB = ctx.identB
    xtile = ctx.sbuf(f"tt_{tag}", [128, 2, nd * 128], BF16)
    tps = [ctx.psum(f"tp_{tag}A", [128, 1024], BF16, stack=pstack),
           ctx.psum(f"tp_{tag}B", [128, 1024], BF16, stack=pstack)]
    for c in range(nt):
        b = c % 2
        au.op("gpsimd", lambda g, c=c, b=b: g.dma_start(
            out=xtile[:, b, :], in_=src_dram[c * 128:(c + 1) * 128, :]),
            reads=[(src_key, c)] if src_key else [], writes=[(f"tt_{tag}", b)], dma=True,
            stream=f"tt{tag}{b}")
        if also_bf16_dram is not None:
            au.op("gpsimd", lambda g, c=c, b=b: g.dma_start(
                out=also_bf16_dram[c * 128:(c + 1) * 128, :], in_=xtile[:, b, :]),
                reads=[(f"tt_{tag}", b)], writes=[(f"{tag}bd", c)], dma=True,
                stream=f"st{tag}{b}")
        for dp in range(nd // 2):
            pb = (c * (nd // 2) + dp) % 2

            def mmfn(t, c=c, dp=dp, pb=pb, b=b):
                t.transpose(tps[pb][:, 0:128], xtile[:, b, (2 * dp) * 128:(2 * dp + 1) * 128],
                            identB[:, :])
                return t.transpose(tps[pb][:, 128:256],
                                   xtile[:, b, (2 * dp + 1) * 128:(2 * dp + 2) * 128], identB[:, :])

            au.op("tensor", mmfn, reads=[(f"tt_{tag}", b), ("identB",)],
                  writes=[(f"tp_{tag}", pb)])

            def evfn(v, c=c, dp=dp, pb=pb):
                v.tensor_copy(dst_fm[:, 2 * dp, c * 128:(c + 1) * 128], tps[pb][:, 0:128])
                return v.tensor_copy(dst_fm[:, 2 * dp + 1, c * 128:(c + 1) * 128],
                                     tps[pb][:, 128:256])

            au.op("vector", evfn, reads=[(f"tp_{tag}", pb)],
                  writes=[(f"fm_{tag}", 2 * dp), (f"fm_{tag}", 2 * dp + 1)])


def emit_ln_inplace(ctx, au, rt, rkeys, lnbufs, outkeys=(), tag=0):
    sqs, mu, ssq, rstd, mb = lnbufs
    au.op("vector", lambda v: v.reduce_sum(mu[:, :], rt, axis=AX.X),
          reads=rkeys, writes=[("ln_mu", tag)])
    au.op("scalar", lambda s: s.activation(sqs[:, :], rt, AF.Square, accum_out=ssq[:, :]),
          reads=rkeys, writes=[("ln_ssq", tag), ("ln_sqs", tag), ("ln_sqs_sh",)])
    au.op("vector", lambda v: v.tensor_scalar(mu[:, :], mu[:, :], 1.0 / D, None, OP.mult),
          reads=[("ln_mu", tag)], writes=[("ln_mu", tag)])
    au.op("vector", lambda v: v.tensor_tensor(rstd[:, :], mu[:, :], mu[:, :], OP.mult),
          reads=[("ln_mu", tag)], writes=[("ln_rstd", tag)])
    au.op("vector", lambda v: v.scalar_tensor_tensor(rstd[:, :], ssq[:, :], 1.0 / D, rstd[:, :],
                                                     OP.mult, OP.subtract),
          reads=[("ln_ssq", tag), ("ln_rstd", tag)], writes=[("ln_rstd", tag)])
    au.op("vector", lambda v: v.tensor_scalar(rstd[:, :], rstd[:, :], 0.0, LN_EPS, OP.max, OP.add),
          reads=[("ln_rstd", tag)], writes=[("ln_rstd", tag)])
    au.op("scalar", lambda s: s.activation(rstd[:, :], rstd[:, :], AF.Sqrt),
          reads=[("ln_rstd", tag)], writes=[("ln_rstd", tag)])
    au.op("vector", lambda v: v.reciprocal(rstd[:, :], rstd[:, :]),
          reads=[("ln_rstd", tag)], writes=[("ln_rstd", tag)])
    au.op("vector", lambda v: v.scalar_tensor_tensor(mb[:, :], mu[:, :], -1.0, rstd[:, :],
                                                     OP.mult, OP.mult),
          reads=[("ln_mu", tag), ("ln_rstd", tag)], writes=[("ln_mb", tag)])
    au.op("scalar", lambda s: s.activation(rt, rt, AF.Identity, bias=mb[:, :], scale=rstd[:, :]),
          reads=list(rkeys) + [("ln_mb", tag), ("ln_rstd", tag)], writes=list(rkeys) + list(outkeys))


def emit_p0(ctx, au, x_dram, wqk_dram, bqk_dram, xb_dram, attnT_dram, r1_dram, stages='ABCDE'):
    nc = ctx.nc
    identB, identF = ctx.identB, ctx.identF

    xT = ctx.sbuf("xT", [128, ND, T], BF16)
    wqkT = ctx.sbuf("wqkT", [128, ND, 2048], BF16)
    b_row = ctx.sbuf("b_row", [1, 2048], F32)
    nkbar_row = ctx.sbuf("nkbar_row", [1, 2048], F32)
    xbar = ctx.sbuf("xbar", [128, ND], F32)
    xbar_b = ctx.sbuf("xbar_b", [128, ND], BF16)
    qb_col = ctx.sbuf("qb_col", [128, NP], F32)
    kb_col = ctx.sbuf("kb_col", [128, NP], F32)
    ov_col = ctx.sbuf("ov_col", [128, NP], F32)

    with ExitStack() as ps_ab:
        xtile = ctx.sbuf("tt_x", [128, 2, D], BF16)
        for c in range(NT):
            b = c % 2
            au.op("gpsimd", lambda g, c=c, b=b: g.dma_start(
                out=xtile[:, b, :], in_=x_dram[c * 128:(c + 1) * 128, :]),
                writes=[("tt_x", b)], dma=True, stream=f"ttx{b}")
            au.op("gpsimd", lambda g, c=c, b=b: g.dma_start(
                out=xb_dram[c * 128:(c + 1) * 128, :], in_=xtile[:, b, :]),
                reads=[("tt_x", b)], writes=[("xbd", c)], dma=True, stream=f"stx{b}")
            au.op("sync", lambda sy, c=c: sy.dma_start(
                out=xT[:, :, c * 128:(c + 1) * 128], in_=xb_dram[c * 128:(c + 1) * 128, :],
                transpose=True),
                reads=[("xbd", c)], writes=[("fm_x", c % 2)] + [("fm_xd", d) for d in range(ND)],
                dma=True, stream="xtr")
        XT = [("fm_xd", d) for d in range(ND)]

        wnat = ctx.sbuf("wnat", [128, 2, 4, D], BF16)
        au.op("gpsimd", lambda g: g.dma_start(out=b_row[:, :], in_=bqk_dram[None, :]),
              writes=[("b_row",)], dma=True, stream="bload")
        wps = [ctx.psum("wpsA", [128, 1024], BF16, stack=ps_ab),
               ctx.psum("wpsB", [128, 1024], BF16, stack=ps_ab)]
        for rnd in range(4):
            rb = rnd % 2
            au.op("gpsimd", lambda g, rnd=rnd, rb=rb: g.dma_start(
                out=wnat[:, rb, :, :],
                in_=wqk_dram.rearrange("(jt p) d -> p jt d", p=128)[:, rnd * 4:(rnd + 1) * 4, :]),
                writes=[("wnat", rb)], dma=True, stream=f"wload{rb}")
            for jl in range(4):
                jt = rnd * 4 + jl
                for dp in range(ND // 2):
                    pb = (jt * (ND // 2) + dp) % 2

                    def mmfn(t, jl=jl, dp=dp, pb=pb, rb=rb):
                        t.transpose(wps[pb][:, 0:128],
                                    wnat[:, rb, jl, (2 * dp) * 128:(2 * dp + 1) * 128],
                                    identB[:, :])
                        return t.transpose(wps[pb][:, 128:256],
                                           wnat[:, rb, jl, (2 * dp + 1) * 128:(2 * dp + 2) * 128],
                                           identB[:, :])

                    au.op("tensor", mmfn, reads=[("wnat", rb), ("identB",)], writes=[("wps", pb)])

                    def evfn(v, jt=jt, dp=dp, pb=pb):
                        v.tensor_copy(wqkT[:, 2 * dp, jt * 128:(jt + 1) * 128], wps[pb][:, 0:128])
                        return v.tensor_copy(wqkT[:, 2 * dp + 1, jt * 128:(jt + 1) * 128],
                                             wps[pb][:, 128:256])

                    au.op("vector", evfn, reads=[("wps", pb)], writes=[("wqkT",)])

        if 'C' not in stages:
            return []
        def xbarfn(v):
            for d in range(ND):
                i = v.reduce_sum(xbar[:, d:d + 1], xT[:, d, :], axis=AX.X)
            return i

        au.op("vector", xbarfn, reads=XT, writes=[("xbar",)])
        au.op("vector", lambda v: v.tensor_scalar(xbar[:, :], xbar[:, :], 1.0 / T, None, OP.mult),
              reads=[("xbar",)], writes=[("xbar",)])
        au.op("vector", lambda v: v.tensor_copy(xbar_b[:, :], xbar[:, :]),
              reads=[("xbar",)], writes=[("xbar_b",)])
        au.op("vector", lambda v: v.tensor_scalar(ov_col[:, :], xbar[:, :], -LA * T * OSCALE,
                                                  None, OP.mult),
              reads=[("xbar",)], writes=[("ov_col",)])
        kps1 = ctx.psum("kps1", [1, 512], F32, stack=ps_ab)
        bps = ctx.psum("bps", [128, 512], F32, stack=ps_ab)
        for c in range(4):
            def kbarfn(t, c=c):
                for d in range(ND):
                    i = t.matmul(kps1[:, :], xbar_b[:, d:d + 1], wqkT[:, d, c * 512:(c + 1) * 512],
                                 start=(d == 0), stop=(d == ND - 1))
                return i

            au.op("tensor", kbarfn, reads=[("xbar_b",), ("wqkT",)], writes=[("kps1",)])
            au.op("scalar", lambda s, c=c: s.activation(nkbar_row[:, c * 512:(c + 1) * 512],
                                                        kps1[:, :], AF.Identity, scale=-1.0),
                  reads=[("kps1",)], writes=[("nkbar_row",)])

        def biasfn(t):
            for ph in range(NP):
                t.transpose(bps[:, ph * 2:ph * 2 + 1],
                            b_row[0:1, ph * 256: ph * 256 + 256: 2], identF[0:1, 0:1])
                i = t.transpose(bps[:, ph * 2 + 1:ph * 2 + 2],
                                nkbar_row[0:1, ph * 256 + 1: ph * 256 + 256: 2], identF[0:1, 0:1])
            return i

        au.op("tensor", biasfn, reads=[("b_row",), ("nkbar_row",), ("identF",)],
              writes=[("bps",)])

        def biasev(v):
            for ph in range(NP):
                v.tensor_copy(qb_col[:, ph:ph + 1], bps[:, ph * 2:ph * 2 + 1])
                i = v.tensor_copy(kb_col[:, ph:ph + 1], bps[:, ph * 2 + 1:ph * 2 + 2])
            return i

        au.op("vector", biasev, reads=[("bps",)], writes=[("qb_col",), ("kb_col",)])

    TQC = 512
    NTQ = T // TQC
    with ExitStack() as ps_d:
        kpair = ctx.sbuf("kpair", [128, 2, T], BF16)
        qsl = ctx.sbuf("qsl", [128, 2, TQC], BF16)
        vpair = ctx.sbuf("vpair", [128, 1, NT, 128], BF16)
        Eb = ctx.sbuf("Eb", [128, 3, 2, TQC], BF16)
        A1b = ctx.sbuf("A1b", [128, 3, 2, TQC], BF16)
        A2b = ctx.sbuf("A2b", [128, 3, 2, TQC], BF16)
        othf = ctx.sbuf("othf", [128, 2, TQC], BF16)

        qps = ctx.psum("qps", [128, 512], F32, stack=ps_d)
        sps = [ctx.psum("spsA", [128, 2, 512], F32, stack=ps_d),
               ctx.psum("spsB", [128, 2, 512], F32, stack=ps_d),
               ctx.psum("spsC", [128, 2, 512], F32, stack=ps_d)]
        opsum = ctx.psum("opsum", [128, 512], F32, stack=ps_d)
        kpps = [qps, qps]

        XBD = [("xbd", c) for c in range(NT)]
        for ph in range(NP):
            bp = 0
            au.op("gpsimd", lambda g, ph=ph, bp=bp: g.dma_start(
                out=vpair[:, bp, :, :],
                in_=xb_dram.rearrange("(tk p) d -> p tk d", p=128)[:, :, ph * 128:(ph + 1) * 128]),
                reads=XBD, writes=[("vpair", bp)], dma=True, stream="vp0")
            for tc in range(T // 512):
                kb = tc % 2

                def kproj(t, ph=ph, tc=tc, kb=kb):
                    for d in range(ND):
                        st0, sp = (d == 0), (d == ND - 1)
                        t.matmul(kpps[kb][0:64, :], wqkT[:, d, ph * 256 + 1: ph * 256 + 128: 2],
                                 xT[:, d, tc * 512:(tc + 1) * 512], start=st0, stop=sp,
                                 tile_position=(0, 0), skip_group_check=True)
                        i = t.matmul(kpps[kb][64:128, :],
                                     wqkT[:, d, ph * 256 + 129: ph * 256 + 256: 2],
                                     xT[:, d, tc * 512:(tc + 1) * 512], start=st0, stop=sp,
                                     tile_position=(0, 64), skip_group_check=True)
                    return i

                au.op("tensor", kproj, reads=XT + [("wqkT",)], writes=[("qps",)])
                au.op("scalar", lambda s, bp=bp, ph=ph, tc=tc, kb=kb: s.activation(
                    kpair[:, bp, tc * 512:(tc + 1) * 512], kpps[kb][:, :], AF.Identity,
                    bias=kb_col[:, ph:ph + 1], scale=1.0),
                    reads=[("kb_col",)], xreads=[("qps",)], writes=[("kpair", bp, tc)])
            for tq in range(NTQ):
                bq = tq % 2

                def qproj(t, ph=ph, tq=tq):
                    col0 = tq * TQC
                    for d in range(ND):
                        st0, sp = (d == 0), (d == ND - 1)
                        t.matmul(qps[0:64, :], wqkT[:, d, ph * 256: ph * 256 + 128: 2],
                                 xT[:, d, col0:col0 + 512], start=st0, stop=sp,
                                 tile_position=(0, 0), skip_group_check=True)
                        i = t.matmul(qps[64:128, :],
                                     wqkT[:, d, ph * 256 + 128: ph * 256 + 256: 2],
                                     xT[:, d, col0:col0 + 512], start=st0, stop=sp,
                                     tile_position=(0, 64), skip_group_check=True)
                    return i

                au.op("tensor", qproj, reads=XT + [("wqkT",)], writes=[("qps",)])
                au.op("scalar", lambda s, bq=bq, ph=ph: s.activation(
                    qsl[:, bq, :], qps[:, :], AF.Identity,
                    bias=qb_col[:, ph:ph + 1], scale=1.0),
                    reads=[("qb_col",)], xreads=[("qps",)], writes=[("qsl", bq)])
                def mk_sim(tk):
                    bk = tk % 3

                    def simfn(t, bp=bp, bq=bq, tk=tk, bk=bk):
                        for h in range(2):
                            i = t.matmul(sps[bk][:, h, :],
                                         kpair[h * 64:(h + 1) * 64, bp, tk * 128:(tk + 1) * 128],
                                         qsl[h * 64:(h + 1) * 64, bq, :],
                                         start=True, stop=True, tile_position=(h * 64, 0),
                                         skip_group_check=True)
                        return i

                    au.op("tensor", simfn, reads=[("kpair", bp, tk // 4), ("qsl", bq)],
                          writes=[("sps", bk)])

                def mk_av(tk):
                    bk = tk % 3

                    def avfn(t, bp=bp, tk=tk, bk=bk):
                        for h in range(2):
                            st0 = (tk == 0)
                            t.matmul(opsum[h * 64:(h + 1) * 64, :],
                                     vpair[:, bp, tk, h * 64:(h + 1) * 64],
                                     A1b[:, bk, h, :],
                                     start=st0, stop=False, tile_position=(0, h * 64),
                                     skip_group_check=True)
                            i = t.matmul(opsum[h * 64:(h + 1) * 64, :],
                                         vpair[:, bp, tk, h * 64:(h + 1) * 64],
                                         A2b[:, bk, h, :],
                                         start=False, stop=(tk == NT - 1), tile_position=(0, h * 64),
                                         skip_group_check=True)
                        return i

                    au.op("tensor", avfn,
                          reads=[("A1b", bk), ("A2b", bk), ("vpair", bp)],
                          writes=[("opsum",)] if (tk == 0 or tk == NT - 1) else [("opsum_acc",)])

                mk_sim(0)
                for tk in range(NT):
                    bk = tk % 3
                    au.op("scalar", lambda s, bk=bk: s.activation(
                        Eb[:, bk, :, :].rearrange("p a b -> p (a b)"),
                        sps[bk].ap().rearrange("p a b -> p (a b)"), AF.Exp,
                        bias=ctx.ccol[:, 0:1], scale=SCALE),
                        reads=[("ccol",)], xreads=[("sps", bk)], writes=[("Eb", bk)])
                    if tk % 6 == 5:
                        au.op("scalar", lambda s, bk=bk: s.activation(
                            A1b[:, bk, :, :].rearrange("p a b -> p (a b)"),
                            sps[bk].ap().rearrange("p a b -> p (a b)"), AF.Relu,
                            scale=LAM * SCALE),
                            reads=[], xreads=[("sps", bk)], writes=[("A1b", bk)])
                    else:
                        au.op("vector", lambda v, bk=bk: v.tensor_scalar(
                            A1b[:, bk, :, :].rearrange("p a b -> p (a b)"),
                            sps[bk].ap().rearrange("p a b -> p (a b)"),
                            LAM * SCALE, 0.0, OP.mult, OP.max),
                            reads=[], xreads=[("sps", bk)], writes=[("A1b", bk)])
                    au.op("vector", lambda v, bk=bk: v.tensor_scalar(
                        A2b[:, bk, :, :].rearrange("p a b -> p (a b)"),
                        Eb[:, bk, :, :].rearrange("p a b -> p (a b)"), LA, None, OP.min),
                        reads=[("Eb", bk)], writes=[("A2b", bk)])
                    if tk + 1 < NT:
                        mk_sim(tk + 1)
                    if tk >= 2:
                        mk_av(tk - 2)
                mk_av(NT - 2)
                mk_av(NT - 1)
                bq2 = tq % 2
                au.op("scalar", lambda s, ph=ph, bq2=bq2: s.activation(
                    othf[:, bq2, :], opsum[:, :], AF.Identity,
                    bias=ov_col[:, ph:ph + 1], scale=OSCALE),
                    reads=[("ov_col",)], xreads=[("opsum",)], writes=[("othf", bq2)])
                au.op("sync", lambda sy, ph=ph, tq=tq, bq2=bq2: sy.dma_start(
                    out=attnT_dram[ph, :, tq * TQC:(tq + 1) * TQC], in_=othf[:, bq2, :]),
                    reads=[("othf", bq2)], writes=[("attnTd", ph, tq)], dma=True, stream=f"ost{bq2}")

    if 'E' not in stages:
        return []
    with ExitStack() as ps_e:
        atile = ctx.sbuf("atile", [128, 2, NP, 128], BF16)
        rtile = ctx.sbuf("rtile", [128, 2, D], F32)
        xf32 = ctx.sbuf("xf32", [128, 2, D], F32)
        sqs = ctx.sbuf("sqs", [128, 2, D], BF16)
        mu = ctx.sbuf("mu", [128, 2], F32)
        ssq = ctx.sbuf("ssq", [128, 2], F32)
        rstd = ctx.sbuf("rstd", [128, 2], F32)
        mb = ctx.sbuf("mb", [128, 2], F32)
        ttp = [ctx.psum("ttpA", [128, 1024], BF16, stack=ps_e),
               ctx.psum("ttpB", [128, 1024], BF16, stack=ps_e)]
        lnbufs = [(sqs[:, b, :], mu[:, b:b + 1], ssq[:, b:b + 1], rstd[:, b:b + 1],
                   mb[:, b:b + 1]) for b in range(2)]

        for c in range(NT):
            b = c % 2
            tqc = c // 4
            au.op("gpsimd", lambda g, c=c, b=b: g.dma_start(
                out=xf32[:, b, :], in_=x_dram[c * 128:(c + 1) * 128, :]),
                writes=[("xf32", b)], dma=True, stream=f"xf{b}")
            au.op("gpsimd", lambda g, c=c, b=b: g.dma_start(
                out=atile[:, b, :, :],
                in_=attnT_dram.rearrange("h p t -> p h t")[:, :, c * 128:(c + 1) * 128]),
                reads=[("attnTd", ph, tqc) for ph in range(NP)],
                writes=[("atile", b)], dma=True, stream=f"at{b}")
            for dp in range(ND // 2):
                pb = dp % 2

                def ttfn(t, c=c, dp=dp, pb=pb, b=b):
                    t.transpose(ttp[pb][:, 0:128], atile[:, b, 2 * dp, :], identB[:, :])
                    return t.transpose(ttp[pb][:, 128:256], atile[:, b, 2 * dp + 1, :], identB[:, :])

                au.op("tensor", ttfn, reads=[("atile", b), ("identB",)], writes=[("ttp", pb)])

                def addfn(v, dp=dp, pb=pb, b=b):
                    v.tensor_add(rtile[:, b, (2 * dp) * 128:(2 * dp + 1) * 128],
                                 xf32[:, b, (2 * dp) * 128:(2 * dp + 1) * 128], ttp[pb][:, 0:128])
                    return v.tensor_add(rtile[:, b, (2 * dp + 1) * 128:(2 * dp + 2) * 128],
                                        xf32[:, b, (2 * dp + 1) * 128:(2 * dp + 2) * 128],
                                        ttp[pb][:, 128:256])

                au.op("vector", addfn, reads=[("ttp", pb), ("xf32", b)],
                      writes=[("rtile", b, dp)])
            rkeys = [("rtile", b, dp) for dp in range(ND // 2)]
            emit_ln_inplace(ctx, au, rtile[:, b, :], rkeys, lnbufs[b], tag=b)
            au.op("sync", lambda sy, c=c, b=b: sy.dma_start(
                out=r1_dram[c * 128:(c + 1) * 128, :], in_=rtile[:, b, :]),
                reads=rkeys, writes=[("r1d", c)], dma=True, stream=f"r1st{b}")
    return [("r1d", c) for c in range(NT)]


BIGIDX = 4000000.0
KCAP = 1024
NE = 8


def _breg(ctx, g):
    if not hasattr(ctx, "breg"):
        ctx.breg = g.to_reg(KCAP - 1)
    return ctx.breg


def emit_p1(ctx, au, r1_dram, gw_dram, w1_dram, w2_dram, cx_dram, y_dram, r2_dram):
    nc = ctx.nc
    identB, identF = ctx.identB, ctx.identF
    from concourse.bass import IndirectOffsetOnAxis

    mprob = ctx.sbuf("mprob", [128, NT, NE], F32)
    idxu = ctx.sbuf("idxu", [128, NT, NE], U32)
    _p1s = ExitStack()
    probs = ctx.sbuf("probs", [128, NT, NE], F32, stack=_p1s)
    probsE = ctx.sbuf("probsE", [8, NT * 128], F32, stack=_p1s)
    masksA = ctx.sbuf("masksA", [128, NT, NE], BF16, stack=_p1s)
    gwT = ctx.sbuf("gwT", [128, ND, NE], F32, stack=_p1s)
    gwsb = ctx.sbuf("gwsb", [8, D], F32, stack=_p1s)
    Umat = ctx.sbuf("Umat", [128, 128], BF16, stack=_p1s)
    onescolB = ctx.sbuf("onescolB", [128, 1], BF16, stack=_p1s)
    onesrowF = ctx.sbuf("onesrowF", [1, 128], F32, stack=_p1s)
    cnts = ctx.sbuf("cnts", [1, NT * NE], F32, stack=_p1s)
    offs = ctx.sbuf("offs", [1, 2, NT * NE], F32, stack=_p1s)
    offs_ex = ctx.sbuf("offs_ex", [1, NT * NE], F32, stack=_p1s)
    tau = ctx.sbuf("tau", [8, 8], F32, stack=_p1s)
    taurow = ctx.sbuf("taurow", [1, NE], F32, stack=_p1s)
    taubc = ctx.sbuf("taubc", [128, NE], F32, stack=_p1s)
    r1t = ctx.sbuf("r1t", [128, 2, D], F32, stack=_p1s)
    rtT = ctx.sbuf("rtT", [128, 2, ND, 128], F32, stack=_p1s)
    scs = ctx.sbuf("scs", [128, 2, NE], F32, stack=_p1s)
    smx = ctx.sbuf("smx", [128, 2, 4], F32, stack=_p1s)
    maskf = ctx.sbuf("maskf", [128, 2, NE], F32, stack=_p1s)
    post = ctx.sbuf("post", [128, 2, NE], F32, stack=_p1s)
    idxf = ctx.sbuf("idxf", [128, 2, NE], F32, stack=_p1s)
    rep = ctx.sbuf("rep", [128, NT * 128], F32, stack=_p1s)
    scr = ctx.sbuf("scr", [128, NT * 128], BF16, stack=_p1s)
    sel2 = ctx.sbuf("sel2", [128, 8], F32, stack=_p1s)
    selrepT = ctx.sbuf("selrepT", [8, 128], F32, stack=_p1s)
    iot8 = ctx.sbuf("iot8", [128, 8], F32, stack=_p1s)
    srt8 = ctx.sbuf("srt8", [8, 128], F32, stack=_p1s)
    bcols = ctx.sbuf("bcols", [128, 10], F32, stack=_p1s)
    bsb = ctx.sbuf("bsb", [8, 1], F32, stack=_p1s)
    shtb = ctx.sbuf("shtb", [128, 8, 16], BF16, stack=_p1s)
    predb = ctx.sbuf("predb", [128, 1], BF16, stack=_p1s)

    au.op("gpsimd", lambda g: g.iota(Umat[:, :], pattern=[[1, 128]], base=0,
                                     channel_multiplier=-1, allow_small_or_imprecise_dtypes=True),
          writes=[("Umat",)])
    au.op("vector", lambda v: v.tensor_scalar(Umat[:, :], Umat[:, :], 0.0, None, OP.is_ge),
          reads=[("Umat",)], writes=[("Umat",)])
    au.op("gpsimd", lambda g: g.iota(onescolB[:, :], pattern=[[0, 1]], base=1,
                                     channel_multiplier=0, allow_small_or_imprecise_dtypes=True),
          writes=[("onescolB",)])
    au.op("gpsimd", lambda g: g.iota(onesrowF[:, :], pattern=[[0, 128]], base=1,
                                     channel_multiplier=0, allow_small_or_imprecise_dtypes=True),
          writes=[("onesrowF",)])
    au.op("gpsimd", lambda g: g.dma_start(out=gwsb[:, :], in_=gw_dram[:, :]),
          writes=[("gwsb",)], dma=True, stream="p1gw")


    with ExitStack() as psA:
        tpg = ctx.psum("tpg", [128, 512], F32, stack=psA)
        scp = ctx.psum("scp", [128, 512], F32, stack=psA)
        tpp = [ctx.psum("tppA", [128, 512], F32, stack=psA),
               ctx.psum("tppB", [128, 512], F32, stack=psA)]
        pep = ctx.psum("pep", [128, 512], F32, stack=psA)
        aux1 = ctx.psum("aux1", [128, 512], F32, stack=psA)
        aux2 = ctx.psum("aux2", [128, 512], F32, stack=psA)
        aux3 = ctx.psum("aux3", [128, 512], F32, stack=psA)

        def gwtfn(t):
            for d in range(ND):
                i = t.transpose(tpg[:, d * 8:(d + 1) * 8], gwsb[:, d * 128:(d + 1) * 128],
                                identF[0:8, 0:8])
            return i
        au.op("tensor", gwtfn, reads=[("gwsb",), ("identF",)], writes=[("tpg",)])

        def gwevf(v):
            for d in range(ND):
                i = v.tensor_copy(gwT[:, d, :], tpg[:, d * 8:(d + 1) * 8])
            return i
        au.op("vector", gwevf, reads=[("tpg",)], writes=[("gwT",)])

        for c in range(NT):
            b = c % 2
            au.op("gpsimd", lambda g, c=c, b=b: g.dma_start(
                out=r1t[:, b, :], in_=r1_dram[c * 128:(c + 1) * 128, :]),
                writes=[("r1t", b)], dma=True, stream=f"p1r1{b}")
            for dp in range(ND // 2):
                def tpfn(t, b=b, dp=dp):
                    t.transpose(tpp[b][:, 0:128], r1t[:, b, (2 * dp) * 128:(2 * dp + 1) * 128],
                                identF[:, :])
                    return t.transpose(tpp[b][:, 128:256],
                                       r1t[:, b, (2 * dp + 1) * 128:(2 * dp + 2) * 128],
                                       identF[:, :])
                au.op("tensor", tpfn, reads=[("r1t", b), ("identF",)], writes=[("tpp", b)])

                def tpev(v, b=b, dp=dp):
                    v.tensor_copy(rtT[:, b, 2 * dp, :], tpp[b][:, 0:128])
                    return v.tensor_copy(rtT[:, b, 2 * dp + 1, :], tpp[b][:, 128:256])
                au.op("vector", tpev, reads=[("tpp", b)], writes=[("rtT", b, dp)])
            rT = [("rtT", b, dp) for dp in range(ND // 2)]

            def scfn(t, b=b):
                for d in range(ND):
                    i = t.matmul(scp[:, 0:NE], rtT[:, b, d, :], gwT[:, d, :],
                                 start=(d == 0), stop=(d == ND - 1))
                return i
            au.op("tensor", scfn, reads=rT + [("gwT",)], writes=[("scp",)])
            au.op("vector", lambda v, b=b: v.reduce_max(smx[:, b, 0:1], scp[:, 0:NE], axis=AX.X),
                  xreads=[("scp",)], writes=[("smx", b, 0)])
            au.op("vector", lambda v, b=b: v.tensor_scalar(smx[:, b, 1:2], smx[:, b, 0:1],
                                                           -1.0, None, OP.mult),
                  reads=[("smx", b, 0)], writes=[("smx", b, 1)])
            au.op("scalar", lambda s, b=b: s.activation(scs[:, b, :], scp[:, 0:NE], AF.Exp,
                                                        bias=smx[:, b, 1:2], scale=1.0),
                  reads=[("smx", b, 1)], xreads=[("scp",)], writes=[("scs", b)])
            au.op("vector", lambda v, b=b: v.reduce_sum(smx[:, b, 2:3], scs[:, b, :], axis=AX.X),
                  reads=[("scs", b)], writes=[("smx", b, 2)])
            au.op("vector", lambda v, b=b: v.reciprocal(smx[:, b, 3:4], smx[:, b, 2:3]),
                  reads=[("smx", b, 2)], writes=[("smx", b, 3)])
            au.op("vector", lambda v, c=c, b=b: v.tensor_scalar(probs[:, c, :], scs[:, b, :],
                                                                smx[:, b, 3:4], None, OP.mult),
                  reads=[("scs", b), ("smx", b, 3)], writes=[("probs", c)])
            au.op("tensor", lambda t, c=c: t.transpose(pep[0:8, 0:128], probs[:, c, :],
                                                       identF[:, :]),
                  reads=[("probs", c), ("identF",)], writes=[("pep",)])
            au.op("vector", lambda v, c=c: v.tensor_copy(probsE[0:8, c * 128:(c + 1) * 128],
                                                         pep[0:8, 0:128]),
                  reads=[("pep",)], writes=[("probsE", c)])

        PRE = [("probsE", c) for c in range(NT)]
        au.op("gpsimd", lambda g: g.iota(bcols[:, 0:1], pattern=[[0, 1]], base=0,
                                         channel_multiplier=1,
                                         allow_small_or_imprecise_dtypes=True),
              writes=[("bc", 0)])
        au.op("gpsimd", lambda g: g.iota(iot8[:, :], pattern=[[-16, 8]], base=0,
                                         channel_multiplier=1,
                                         allow_small_or_imprecise_dtypes=True),
              writes=[("iot8",)])
        au.op("gpsimd", lambda g: g.iota(srt8[:, :], pattern=[[1, 128]], base=0,
                                         channel_multiplier=-16,
                                         allow_small_or_imprecise_dtypes=True),
              writes=[("srt8",)])
        au.op("vector", lambda v: v.tensor_scalar(sel2[:, :], iot8[:, :], 0.0, None, OP.is_ge),
              reads=[("iot8",)], writes=[("sel2",)])
        au.op("vector", lambda v: v.tensor_scalar(iot8[:, :], iot8[:, :], 16.0, 0.0,
                                                  OP.is_ge, OP.add,
                                                  accum_out=bcols[:, 1:2]),
              reads=[("iot8",)], writes=[("iot8",), ("bc", 1)])
        au.op("vector", lambda v: v.tensor_sub(sel2[:, :], sel2[:, :], iot8[:, :]),
              reads=[("sel2",), ("iot8",)], writes=[("sel2",)])
        au.op("vector", lambda v: v.scalar_tensor_tensor(bcols[:, 2:3], bcols[:, 1:2], -16.0,
                                                         bcols[:, 0:1], OP.mult, OP.add),
              reads=[("bc", 1), ("bc", 0)], writes=[("bc", 2)])
        au.op("vector", lambda v: v.tensor_scalar(selrepT[:, :], srt8[:, :], 0.0, None, OP.is_ge),
              reads=[("srt8",)], writes=[("selrepT",)])
        au.op("vector", lambda v: v.tensor_scalar(srt8[:, :], srt8[:, :], 16.0, None, OP.is_ge),
              reads=[("srt8",)], writes=[("srt8",)])
        au.op("vector", lambda v: v.tensor_sub(selrepT[:, :], selrepT[:, :], srt8[:, :]),
              reads=[("selrepT",)] + [("srt8",)], writes=[("selrepT",)])
        for cc in range(8):
            ax = aux1 if cc % 2 == 0 else aux2
            axk = ("aux1",) if cc % 2 == 0 else ("aux2",)
            au.op("tensor", lambda t, cc=cc, ax=ax: t.matmul(
                ax[:, :], selrepT[0:8, :], probsE[0:8, cc * 512:(cc + 1) * 512],
                start=True, stop=True),
                reads=[("selrepT",)] + PRE, writes=[axk])
            au.op("vector", lambda v, cc=cc, ax=ax: v.tensor_copy(
                rep[:, cc * 512:(cc + 1) * 512], ax[:, :]),
                reads=[axk], writes=[("rep", cc)])
        REPK = [("rep", cc) for cc in range(8)]
        au.op("gpsimd", lambda g: g.iota(shtb[:, :, :], pattern=[[16, 8], [1, 16]], base=0,
                                         channel_multiplier=-1,
                                         allow_small_or_imprecise_dtypes=True),
              writes=[("shtb",)])
        au.op("vector", lambda v: v.tensor_scalar(shtb[:, :, :], shtb[:, :, :], -1.0, None,
                                                  OP.is_equal),
              reads=[("shtb",)], writes=[("shtb",)])
        au.op("vector", lambda v: v.memset(shtb[:, :, 15:16], 0.0),
              reads=[("shtb",)], writes=[("shtb",)])
        au.op("vector", lambda v: v.memset(bcols[:, 3:4], 0.0), writes=[("bc", 3)])
        au.op("vector", lambda v: v.memset(bcols[:, 4:5], 1.0 / 16.0), writes=[("bc", 4)])
        shtf = shtb.ap().rearrange("p a b -> p (a b)")
        for it in range(7):
            au.op("scalar", lambda s: s.activation(bcols[:, 6:7], bcols[:, 2:3], AF.Identity,
                                                   bias=bcols[:, 3:4], scale=bcols[:, 4:5]),
                  reads=[("bc", 2), ("bc", 3), ("bc", 4)], writes=[("bc", 6)])
            au.op("vector", lambda v: v.tensor_scalar(scr[:, :], rep[:, :], bcols[:, 6:7],
                                                      0.0, OP.is_ge, OP.add,
                                                      accum_out=bcols[:, 7:8]),
                  reads=REPK + [("bc", 6)], writes=[("scr",), ("bc", 7)])
            au.op("vector", lambda v: v.tensor_scalar(bcols[:, 8:9], bcols[:, 7:8], float(KCAP),
                                                      None, OP.is_ge),
                  reads=[("bc", 7)], writes=[("bc", 8)])
            au.op("vector", lambda v: v.tensor_scalar(predb[:, :], bcols[:, 7:8], float(KCAP),
                                                      None, OP.is_ge),
                  reads=[("bc", 7)], writes=[("predb",)])
            au.op("tensor", lambda t: t.matmul(aux3[:, 0:1], shtf, predb[:, :],
                                               start=True, stop=True),
                  reads=[("shtb",), ("predb",)], writes=[("aux3",)])
            au.op("vector", lambda v: v.tensor_copy(bcols[:, 5:6], aux3[:, 0:1]),
                  reads=[("aux3",)], writes=[("bc", 5)])
            au.op("vector", lambda v: v.tensor_sub(bcols[:, 1:2], bcols[:, 8:9], bcols[:, 5:6]),
                  reads=[("bc", 8), ("bc", 5)], writes=[("bc", 1)])
            au.op("vector", lambda v: v.tensor_mul(bcols[:, 9:10], bcols[:, 1:2], bcols[:, 6:7]),
                  reads=[("bc", 1), ("bc", 6)], writes=[("bc", 9)])
            au.op("tensor", lambda t: t.matmul(aux1[0:8, 0:1], sel2[:, :], bcols[:, 9:10],
                                               start=True, stop=True),
                  reads=[("sel2",), ("bc", 9)], writes=[("aux1",)])
            au.op("vector", lambda v: v.tensor_copy(bsb[:, :], aux1[0:8, 0:1]),
                  reads=[("aux1",)], writes=[("bsb",)])
            au.op("tensor", lambda t: t.matmul(aux2[:, 0:1], selrepT[0:8, :], bsb[0:8, :],
                                               start=True, stop=True),
                  reads=[("selrepT",), ("bsb",)], writes=[("aux2",)])
            au.op("vector", lambda v: v.tensor_copy(bcols[:, 3:4], aux2[:, 0:1]),
                  reads=[("aux2",)], writes=[("bc", 3)])
            au.op("vector", lambda v: v.tensor_scalar(bcols[:, 4:5], bcols[:, 4:5], 1.0 / 16.0,
                                                      None, OP.mult),
                  reads=[("bc", 4)], writes=[("bc", 4)])
        au.op("tensor", lambda t: t.matmul(aux3[0:8, 0:1], sel2[:, :], bcols[:, 3:4],
                                           start=True, stop=True),
              reads=[("sel2",), ("bc", 3)], writes=[("aux3",)])
        au.op("vector", lambda v: v.tensor_scalar(tau[:, 0:1], aux3[0:8, 0:1], 1.0 / 16.0,
                                                  None, OP.mult),
              reads=[("aux3",)], writes=[("tau", 0)])
        au.op("tensor", lambda t: t.transpose(pep[0:1, 0:8], tau[0:8, 0:1], identF[0:8, 0:8]),
              reads=[("tau", 0), ("identF",)], writes=[("pep",)])
        au.op("vector", lambda v: v.tensor_copy(taurow[:, :], pep[0:1, 0:8]),
              reads=[("pep",)], writes=[("taurow",)])
        au.op("tensor", lambda t: t.matmul(aux1[:, 0:8], onesrowF[:, :], taurow[:, :],
                                           start=True, stop=True),
              reads=[("onesrowF",), ("taurow",)], writes=[("aux1",)])
        au.op("vector", lambda v: v.tensor_copy(taubc[:, :], aux1[:, 0:8]),
              reads=[("aux1",)], writes=[("taubc",)])

        for c in range(NT):
            b = c % 2
            au.op("vector", lambda v, c=c, b=b: v.tensor_tensor(maskf[:, b, :], probs[:, c, :],
                                                                taubc[:, :], OP.is_ge),
                  reads=[("probs", c), ("taubc",)], writes=[("maskf", b)])
            au.op("vector", lambda v, c=c, b=b: v.tensor_mul(mprob[:, c, :], probs[:, c, :],
                                                             maskf[:, b, :]),
                  reads=[("probs", c), ("maskf", b)], writes=[("mprob", c)])
            au.op("vector", lambda v, c=c, b=b: v.tensor_copy(masksA[:, c, :], maskf[:, b, :]),
                  reads=[("maskf", b)], writes=[("masksA", c)])
            au.op("tensor", lambda t, c=c: t.matmul(aux2[0:1, 0:8], onescolB[:, :], masksA[:, c, :],
                                                    start=True, stop=True),
                  reads=[("masksA", c), ("onescolB",)], writes=[("aux2",)])
            au.op("vector", lambda v, c=c: v.tensor_copy(cnts[0:1, c * NE:(c + 1) * NE],
                                                         aux2[0:1, 0:8]),
                  reads=[("aux2",)], writes=[("cnts", c)])
        CN = [("cnts", c) for c in range(NT)]
        au.op("vector", lambda v: v.tensor_copy(offs[:, 0, :], cnts[:, :]),
              reads=CN, writes=[("offs", 0)])
        par = 0
        for sh in (8, 16, 32, 64, 128):
            def shfn(v, sh=sh, par=par):
                v.tensor_copy(offs[:, 1 - par, 0:sh], offs[:, par, 0:sh])
                return v.tensor_add(offs[:, 1 - par, sh:NT * NE], offs[:, par, sh:NT * NE],
                                    offs[:, par, 0:NT * NE - sh])
            au.op("vector", shfn, reads=[("offs", par)], writes=[("offs", 1 - par)])
            par = 1 - par
        au.op("vector", lambda v, par=par: v.tensor_sub(offs_ex[:, :], offs[:, par, :], cnts[:, :]),
              reads=[("offs", par)] + CN, writes=[("offs_ex",)])

        for c in range(NT):
            b = c % 2

            def posfn(t, c=c):
                t.matmul(aux3[:, 0:8], Umat[:, :], masksA[:, c, :], start=True, stop=False)
                return t.matmul(aux3[:, 0:8], onesrowF[:, :], offs_ex[0:1, c * NE:(c + 1) * NE],
                                start=False, stop=True)
            au.op("tensor", posfn, reads=[("masksA", c), ("Umat",), ("offs_ex",), ("onesrowF",)],
                  writes=[("aux3",)])
            au.op("vector", lambda v, b=b: v.tensor_copy(post[:, b, :], aux3[:, 0:8]),
                  reads=[("aux3",)], writes=[("post", b)])
            au.op("vector", lambda v, c=c, b=b: v.scalar_tensor_tensor(
                idxf[:, b, :], masksA[:, c, :], -BIGIDX, post[:, b, :], OP.mult, OP.add),
                reads=[("masksA", c), ("post", b)], writes=[("idxf", b)])
            au.op("vector", lambda v, b=b: v.tensor_scalar(idxf[:, b, :], idxf[:, b, :],
                                                           BIGIDX - 1.0, None, OP.add),
                  reads=[("idxf", b)], writes=[("idxf", b)])
            au.op("vector", lambda v, c=c, b=b: v.tensor_copy(idxu[:, c, :], idxf[:, b, :]),
                  reads=[("idxf", b)], writes=[("idxu", c)])
            for e in range(NE):
                au.op("gpsimd", lambda g, c=c, b=b, e=e: g.indirect_dma_start(
                    out=cx_dram.rearrange("e k d -> (e k) d"),
                    out_offset=IndirectOffsetOnAxis(ap=idxu[:, c, e:e + 1], axis=0),
                    in_=r1t[:, b, :], in_offset=None, element_offset=e * KCAP * D,
                    bounds_check=_breg(ctx, g), oob_is_err=False),
                    reads=[("idxu", c), ("r1t", b)], writes=[("cxd", e, c)],
                    dma=True, stream=f"p1sc{b}{e}")
    au.fence()
    _p1s.close()
    return probs, mprob, idxu


def emit_p1_experts(ctx, au, w1_dram, w2_dram, cx_dram, y_dram, r1_dram, idxu):
    from concourse.bass import IndirectOffsetOnAxis
    identB = ctx.identB
    r1b = ctx.sbuf("r1b", [128, NT, D], BF16)
    cxsb = ctx.sbuf("cxsb", [128, 1, 8, KCAP], BF16)
    cxT = ctx.sbuf("cxT", [128, ND, KCAP], BF16)
    hsb = ctx.sbuf("hsb", [128, 32, KCAP], BF16)
    w1c = ctx.sbuf("w1c", [128, 2, ND, 512], BF16)
    w2c = ctx.sbuf("w2c", [128, 2, 2, D], BF16)
    ysb = ctx.sbuf("ysb", [128, 4, D], F32)
    au.op("gpsimd", lambda g: g.dma_start(
        out=r1b[:, :, :], in_=r1_dram.rearrange("(c p) d -> p c d", p=128)),
        writes=[("r1b",)], dma=True, stream="p1r1b")

    def emit_scatters(e):
        for c in range(NT):
            au.op("gpsimd", lambda g, c=c, e=e: g.indirect_dma_start(
                out=cx_dram.rearrange("e k d -> (e k) d"),
                out_offset=IndirectOffsetOnAxis(ap=idxu[:, c, e:e + 1], axis=0),
                in_=r1b[:, c, :], in_offset=None, element_offset=e * KCAP * D,
                bounds_check=_breg(ctx, g), oob_is_err=False),
                reads=[("idxu", c), ("r1b",)], writes=[("cxd", e, c % 4)],
                dma=True, stream=f"p1sc{e % 2}")

    CXD = lambda e: [("cxd", e, j) for j in range(4)]
    emit_scatters(0)
    for e in range(NE):
        with ExitStack() as psE:
            ctp = [ctx.psum(f"ctpA{e}", [128, 1024], BF16, stack=psE),
                   ctx.psum(f"ctpB{e}", [128, 1024], BF16, stack=psE)]
            hps = [ctx.psum(f"hpsA{e}", [128, 2, 512], F32, stack=psE),
                   ctx.psum(f"hpsB{e}", [128, 2, 512], F32, stack=psE)]
            if e + 1 < NE:
                emit_scatters(e + 1)
            for d in range(ND):
                au.op("sync", lambda sy, e=e, d=d: sy.dma_start(
                    out=cxT[:, d, :], in_=cx_dram[e][:, d * 128:(d + 1) * 128], transpose=True),
                    reads=CXD(e) + ([("yev", e - 1)] if e > 0 else []),
                    writes=[("cxT", d)], dma=True, stream="p1ctr")
            CXT = [("cxT", d) for d in range(ND)]
            for fc in range(8):
                wb = fc % 2
                au.op("gpsimd", lambda g, e=e, fc=fc, wb=wb: g.dma_start(
                    out=w1c[:, wb, :, :],
                    in_=w1_dram[e].rearrange("(dt p) f -> p dt f", p=128)[:, :,
                                                                         fc * 512:(fc + 1) * 512]),
                    writes=[("w1c", wb)], dma=True, stream=f"p1w1{wb}")
                for fcc in range(4):
                    hb = fcc % 2

                    def hmm(t, fc=fc, fcc=fcc, wb=wb, hb=hb):
                        for d in range(ND):
                            for tc in range(2):
                                i = t.matmul(hps[hb][:, tc, :],
                                             w1c[:, wb, d, fcc * 128:(fcc + 1) * 128],
                                             cxT[:, d, tc * 512:(tc + 1) * 512],
                                             start=(d == 0), stop=(d == ND - 1))
                        return i
                    au.op("tensor", hmm, reads=CXT + [("w1c", wb)], writes=[("hps", hb)])
                    ft = fc * 4 + fcc
                    au.op("scalar", lambda s, ft=ft, hb=hb: s.activation(
                        hsb[:, ft, :], hps[hb].ap().rearrange("p a b -> p (a b)"), AF.Gelu),
                        xreads=[("hps", hb)], writes=[("hsb", ft)])
            HS = [("hsb", ft) for ft in range(32)]
        with ExitStack() as psY:
            yps = [ctx.psum(f"yps{e}_{j}", [128, 2, 512], F32, stack=psY) for j in range(4)]
            for half in range(2):
                for fp_ in range(16):
                    wb = fp_ % 2
                    au.op("gpsimd", lambda g, e=e, fp_=fp_, wb=wb: g.dma_start(
                        out=w2c[:, wb, :, :],
                        in_=w2_dram[e, fp_ * 256:(fp_ + 1) * 256, :].rearrange(
                            "(a p) d -> p a d", p=128)),
                        writes=[("w2c", wb)], dma=True, stream=f"p1w2{wb}")
                    for fq in range(2):
                        ft = fp_ * 2 + fq

                        def ymm(t, half=half, ft=ft, wb=wb, fq=fq):
                            for j in range(4):
                                tj = half * 4 + j
                                for dc in range(2):
                                    i = t.matmul(yps[j][:, dc, :],
                                                 hsb[:, ft, tj * 128:(tj + 1) * 128],
                                                 w2c[:, wb, fq, dc * 512:(dc + 1) * 512],
                                                 start=(ft == 0), stop=(ft == 31))
                            return i
                        au.op("tensor", ymm,
                              reads=[("hsb", ft), ("w2c", wb)] + (HS if ft == 0 else []),
                              writes=[("yps",)] if (ft == 0 or ft == 31) else [("yacc",)])
                for j in range(4):
                    tj = half * 4 + j
                    au.op("vector", lambda v, j=j: v.tensor_copy(
                        ysb[:, j, :], yps[j].ap().rearrange("p a b -> p (a b)")),
                        xreads=[("yps",)], writes=[("ysbuf", j), ("yev", e)])
                for j in range(4):
                    tj = half * 4 + j
                    au.op("sync", lambda sy, e=e, tj=tj, j=j: sy.dma_start(
                        out=y_dram[e, tj * 128:(tj + 1) * 128, :], in_=ysb[:, j, :]),
                        reads=[("ysbuf", j)], writes=[("yd", e, tj)], dma=True,
                        stream=f"p1y{j}")


def emit_p1_tail(ctx, au, r1_dram, y_dram, r2_dram, mprob, idxu, after_tile=None):
    from concourse.bass import IndirectOffsetOnAxis
    rtile = ctx.sbuf("p1rt", [128, 2, D], F32)
    acc2 = ctx.sbuf("p1acc2", [128, 2, D], F32)
    tmpg = ctx.sbuf("tmpg", [128, 4, D], F32)
    sqs = ctx.sbuf("p1sqs", [128, 2, D], BF16)
    mu = ctx.sbuf("p1mu", [128, 2], F32)
    ssq = ctx.sbuf("p1ssq", [128, 2], F32)
    rstd = ctx.sbuf("p1rstd", [128, 2], F32)
    mb = ctx.sbuf("p1mb", [128, 2], F32)
    lnbufs = [(sqs[:, b, :], mu[:, b:b + 1], ssq[:, b:b + 1], rstd[:, b:b + 1],
               mb[:, b:b + 1]) for b in range(2)]
    au.op("vector", lambda v: v.memset(tmpg[:, :, :], 0.0),
          writes=[("tmpg", 0), ("tmpg", 1), ("tmpg", 2), ("tmpg", 3)])
    YD = [("yd", e, tj) for e in range(NE) for tj in range(8)]
    for c in range(NT):
        b = c % 2
        au.op("gpsimd", lambda g, c=c, b=b: g.dma_start(
            out=rtile[:, b, :], in_=r1_dram[c * 128:(c + 1) * 128, :]),
            writes=[("p1rt", b), ("rk", b, 0), ("rk", b, 1), ("rk", b, 2), ("rk", b, 3)],
            dma=True, stream=f"p1f{b}")
        for e in range(NE):
            ge = e % 2
            au.op("gpsimd", lambda g, c=c, e=e, ge=ge: g.indirect_dma_start(
                out=tmpg[:, ge, :], out_offset=None,
                in_=y_dram.rearrange("e k d -> (e k) d"),
                in_offset=IndirectOffsetOnAxis(ap=idxu[:, c, e:e + 1], axis=0),
                element_offset=e * KCAP * D,
                bounds_check=_breg(ctx, g), oob_is_err=False),
                reads=[("idxu", c)] + (YD if c == 0 and e == 0 else [("yd", e, tj) for tj in range(8)]),
                writes=[("tmpg", ge)], dma=True, stream=f"p1g{e}")
            au.op("vector", lambda v, c=c, e=e, ge=ge, b=b: v.scalar_tensor_tensor(
                rtile[:, b, :], tmpg[:, ge, :], mprob[:, c, e:e + 1], rtile[:, b, :],
                OP.mult, OP.add),
                reads=[("tmpg", ge), ("mprob", c), ("p1rt", b), ("rk", b, 0), ("rk", b, 1),
                       ("rk", b, 2), ("rk", b, 3)],
                writes=[("rk", b, 0), ("rk", b, 1), ("rk", b, 2), ("rk", b, 3)])
        rkeys = [("rk", b, j) for j in range(4)]
        emit_ln_inplace(ctx, au, rtile[:, b, :], rkeys, lnbufs[b], tag=10 + b)
        au.op("sync", lambda sy, c=c, b=b: sy.dma_start(
            out=r2_dram[c * 128:(c + 1) * 128, :], in_=rtile[:, b, :]),
            reads=rkeys, writes=[("r2d", c)], dma=True, stream=f"p1r2{b}")
        if after_tile is not None:
            after_tile(c, rtile[:, b, :], rkeys)
    return [("r2d", c) for c in range(NT)]


def emit_p2(ctx, au, r2_dram, wq_dram, wk_dram, wv_dram, wo_dram, bo_dram,
            ct_dram, st_dram, qtm_dram, ktm_dram, vtm_dram, r3_dram, tail_hook_reg=None):
    nc = ctx.nc
    identB, identF = ctx.identB, ctx.identF

    wosb = ctx.sbuf("wosb", [128, ND, D], BF16)
    r2t = ctx.sbuf("r2t", [128, 2, D], F32)
    nmhat = ctx.sbuf("nmhat", [128, 1], F32)
    bobc = ctx.sbuf("bobc", [128, D], F32)
    tob = ctx.sbuf("tob", [128, 2, D], F32)
    sqs2 = ctx.sbuf("p2sqs", [128, 2, D], BF16)
    mu2 = ctx.sbuf("p2mu", [128, 2], F32)
    ssq2 = ctx.sbuf("p2ssq", [128, 2], F32)
    rstd2 = ctx.sbuf("p2rstd", [128, 2], F32)
    mb2 = ctx.sbuf("p2mb", [128, 2], F32)
    lnb2 = [(sqs2[:, b, :], mu2[:, b:b + 1], ssq2[:, b:b + 1], rstd2[:, b:b + 1],
             mb2[:, b:b + 1]) for b in range(2)]
    onesrF = ctx.sbuf("onesrF", [1, 128], F32)
    onescB2 = ctx.sbuf("onescB2", [128, 1], BF16)
    onesrB = ctx.sbuf("onesrB", [1, 128], BF16)
    sbA = ExitStack()
    wqsb = ctx.sbuf("wqsb", [128, ND, D], BF16, stack=sbA)
    wksb = ctx.sbuf("wksb", [128, ND, D], BF16, stack=sbA)
    wvsb = ctx.sbuf("wvsb", [128, ND, D], BF16, stack=sbA)
    rtTb = ctx.sbuf("rtTb", [128, 2, ND, 128], BF16, stack=sbA)
    ctab = ctx.sbuf("ctab", [128, 2, 512], BF16, stack=sbA)
    stab = ctx.sbuf("stab", [128, 2, 512], BF16, stack=sbA)
    rsc1 = ctx.sbuf("rsc1", [128, 512], BF16, stack=sbA)
    rsc2 = ctx.sbuf("rsc2", [128, 512], BF16, stack=sbA)
    qkraw = ctx.sbuf("qkraw", [128, D], BF16, stack=sbA)
    qtt = ctx.sbuf("qtt", [128, 2, D], BF16, stack=sbA)
    ktt = ctx.sbuf("ktt", [128, 2, D], BF16, stack=sbA)
    vtt = ctx.sbuf("vtt", [128, 2, D], BF16, stack=sbA)
    sqsc = ctx.sbuf("sqsc", [128, D], BF16, stack=sbA)
    nrm = ctx.sbuf("nrm", [128, 4], F32, stack=sbA)
    sc11 = ctx.sbuf("sc11", [1, 8], F32, stack=sbA)
    borow = ctx.sbuf("borow", [1, D], F32, stack=sbA)

    au.op("gpsimd", lambda g: g.iota(onesrF[:, :], pattern=[[0, 128]], base=1,
                                     channel_multiplier=0, allow_small_or_imprecise_dtypes=True),
          writes=[("onesrF",)])
    au.op("gpsimd", lambda g: g.iota(onescB2[:, :], pattern=[[0, 1]], base=1,
                                     channel_multiplier=0, allow_small_or_imprecise_dtypes=True),
          writes=[("onescB2",)])
    au.op("gpsimd", lambda g: g.iota(onesrB[:, :], pattern=[[0, 128]], base=1,
                                     channel_multiplier=0, allow_small_or_imprecise_dtypes=True),
          writes=[("onesrB",)])
    for nm, dst, key in (("wq", wqsb, "wqsb"), ("wk", wksb, "wksb"),
                         ("wv", wvsb, "wvsb"), ("wo", wosb, "wosb")):
        srcd = {"wq": wq_dram, "wk": wk_dram, "wv": wv_dram, "wo": wo_dram}[nm]
        au.op("gpsimd", lambda g, srcd=srcd, dst=dst: g.dma_start(
            out=dst[:, :, :], in_=srcd.rearrange("(dt p) c -> p dt c", p=128)),
            writes=[(key,)], dma=True, stream=f"p2w{nm}")
    au.op("gpsimd", lambda g: g.dma_start(out=borow[:, :], in_=bo_dram[None, :]),
          writes=[("borow",)], dma=True, stream="p2bo")
    au.op("vector", lambda v: v.memset(nrm[:, 1:3], 0.0), writes=[("nrm", 1), ("nrm", 2)])

    psA = ExitStack()
    tpa = ctx.psum("p2tpa", [128, 512], F32, stack=psA)
    qpp = ctx.psum("qpp", [128, 2, 512], F32, stack=psA)
    kpp = ctx.psum("kpp", [128, 2, 512], F32, stack=psA)
    vpp = ctx.psum("vpp", [128, 2, 512], F32, stack=psA)
    aux = ctx.psum("p2aux", [128, 512], F32, stack=psA)

    au.op("tensor", lambda t: t.matmul(aux[:, 0:512], onesrF[:, :], borow[0:1, 0:512],
                                       start=True, stop=True),
          reads=[("onesrF",), ("borow",)], writes=[("p2aux",)])
    au.op("vector", lambda v: v.tensor_copy(bobc[:, 0:512], aux[:, 0:512]),
          reads=[("p2aux",)], writes=[("bobc", 0)])
    au.op("tensor", lambda t: t.matmul(aux[:, 0:512], onesrF[:, :], borow[0:1, 512:1024],
                                       start=True, stop=True),
          reads=[("onesrF",), ("borow",)], writes=[("p2aux",)])
    au.op("vector", lambda v: v.tensor_copy(bobc[:, 512:1024], aux[:, 0:512]),
          reads=[("p2aux",)], writes=[("bobc", 1)])

    def tileA(c, src_ap, src_keys):
        b = c % 2
        au.op("gpsimd", lambda g, c=c, b=b: g.dma_start(
            out=ctab[:, b, :], in_=ct_dram[c * 128:(c + 1) * 128, :]),
            writes=[("ctab", b)], dma=True, stream=f"p2ct{b}")
        au.op("gpsimd", lambda g, c=c, b=b: g.dma_start(
            out=stab[:, b, :], in_=st_dram[c * 128:(c + 1) * 128, :]),
            writes=[("stab", b)], dma=True, stream=f"p2st{b}")
        for dp in range(ND // 2):
            def tpfn(t, b=b, dp=dp, src_ap=src_ap):
                t.transpose(tpa[:, 0:128], src_ap[:, (2 * dp) * 128:(2 * dp + 1) * 128],
                            identF[:, :])
                return t.transpose(tpa[:, 128:256],
                                   src_ap[:, (2 * dp + 1) * 128:(2 * dp + 2) * 128],
                                   identF[:, :])
            au.op("tensor", tpfn, reads=list(src_keys) + [("identF",)], writes=[("p2tpa",)])

            def tpev(v, b=b, dp=dp):
                v.tensor_copy(rtTb[:, b, 2 * dp, :], tpa[:, 0:128])
                return v.tensor_copy(rtTb[:, b, 2 * dp + 1, :], tpa[:, 128:256])
            au.op("vector", tpev, reads=[("p2tpa",)], writes=[("rtTb", b, dp)])
        rT = [("rtTb", b, dp) for dp in range(ND // 2)]
        for nmW, wsb, wkey, pp, ppk in (("q", wqsb, "wqsb", qpp, "qpp"),
                                        ("k", wksb, "wksb", kpp, "kpp"),
                                        ("v", wvsb, "wvsb", vpp, "vpp")):
            def pfn(t, b=b, wsb=wsb, pp=pp):
                for d in range(ND):
                    for qc in range(2):
                        i = t.matmul(pp[:, qc, :], rtTb[:, b, d, :],
                                     wsb[:, d, qc * 512:(qc + 1) * 512],
                                     start=(d == 0), stop=(d == ND - 1))
                return i
            au.op("tensor", pfn, reads=rT + [(wkey,)], writes=[(ppk,)])
        au.op("scalar", lambda s, b=b: s.activation(
            vtt[:, b, :], vpp.ap().rearrange("p a x -> p (a x)"), AF.Identity),
            xreads=[("vpp",)], writes=[("vtt", b)])
        au.op("sync", lambda sy, c=c, b=b: sy.dma_start(
            out=vtm_dram[c * 128:(c + 1) * 128, :], in_=vtt[:, b, :]),
            reads=[("vtt", b)], writes=[("vtmd", c)], dma=True, stream=f"p2vs{b}")
        for nmW, pp, ppk, ttt, tk, sqk, macc in (
                ("q", qpp, "qpp", qtt, "qtt", 0, 1), ("k", kpp, "kpp", ktt, "ktt", 0, 2)):
            au.op("vector", lambda v, pp=pp: v.tensor_copy(
                qkraw[:, :], pp.ap().rearrange("p a x -> p (a x)")),
                xreads=[(ppk,)], writes=[("qkraw",)])
            qkf = qkraw.ap()
            x1 = qkf[:, 0:1024:2]
            x2 = qkf[:, 1:1024:2]
            au.op("vector", lambda v, x1=x1, b=b: v.tensor_mul(rsc1[:, :], x1, ctab[:, b, :]),
                  reads=[("ctab", b), ("qkraw",)], writes=[("rsc1",)])
            au.op("vector", lambda v, x2=x2, b=b: v.tensor_mul(rsc2[:, :], x2, stab[:, b, :]),
                  reads=[("stab", b), ("qkraw",)], writes=[("rsc2",)])
            au.op("vector", lambda v, ttt=ttt, b=b: v.tensor_sub(
                ttt[:, b, 0:1024:2], rsc1[:, :], rsc2[:, :]),
                reads=[("rsc1",), ("rsc2",)], writes=[(tk, b, 0)])
            au.op("vector", lambda v, x1=x1, b=b: v.tensor_mul(rsc1[:, :], x1, stab[:, b, :]),
                  reads=[("stab", b), ("qkraw",)], writes=[("rsc1",)])
            au.op("vector", lambda v, x2=x2, b=b: v.tensor_mul(rsc2[:, :], x2, ctab[:, b, :]),
                  reads=[("ctab", b), ("qkraw",)], writes=[("rsc2",)])
            au.op("vector", lambda v, ttt=ttt, b=b: v.tensor_add(
                ttt[:, b, 1:1024:2], rsc1[:, :], rsc2[:, :]),
                reads=[("rsc1",), ("rsc2",)], writes=[(tk, b, 1)])
            au.op("scalar", lambda s, ttt=ttt, b=b: s.activation(
                sqsc[:, :], ttt[:, b, :], AF.Square, accum_out=nrm[:, 0:1]),
                reads=[(tk, b, 0), (tk, b, 1)], writes=[("nrm", 0), ("sqsc",)])
            au.op("vector", lambda v, macc=macc: v.tensor_max(
                nrm[:, macc:macc + 1], nrm[:, macc:macc + 1], nrm[:, 0:1]),
                reads=[("nrm", 0), ("nrm", macc)], writes=[("nrm", macc)])
            dstd = qtm_dram if tk == "qtt" else ktm_dram
            au.op("sync", lambda sy, c=c, b=b, ttt=ttt, dstd=dstd: sy.dma_start(
                out=dstd[c * 128:(c + 1) * 128, :], in_=ttt[:, b, :]),
                reads=[(tk, b, 0), (tk, b, 1)], writes=[(tk + "d", c)], dma=True,
                stream=f"p2{tk}s{b}")

    if tail_hook_reg is not None:
        tail_hook_reg(tileA)
    else:
        for c in range(NT):
            b = c % 2
            au.op("gpsimd", lambda g, c=c, b=b: g.dma_start(
                out=r2t[:, b, :], in_=r2_dram[c * 128:(c + 1) * 128, :]),
                reads=[("r2d", c)], writes=[("r2t", b)], dma=True, stream=f"p2r{b}")
            tileA(c, r2t[:, b, :], [("r2t", b)])

    def _cont():
        def finishA():
            au.op("tensor", lambda t: t.transpose(aux[0:1, 0:128], nrm[:, 1:2], identF[:, :]),
                  reads=[("nrm", 1), ("identF",)], writes=[("p2aux",)])
            au.op("vector", lambda v: v.reduce_max(sc11[0:1, 0:1], aux[0:1, 0:128], axis=AX.X),
                  xreads=[("p2aux",)], writes=[("sc11", 0)])
            au.op("tensor", lambda t: t.transpose(aux[0:1, 0:128], nrm[:, 2:3], identF[:, :]),
                  reads=[("nrm", 2), ("identF",)], writes=[("p2aux",)])
            au.op("vector", lambda v: v.reduce_max(sc11[0:1, 1:2], aux[0:1, 0:128], axis=AX.X),
                  xreads=[("p2aux",)], writes=[("sc11", 1)])
            au.op("vector", lambda v: v.tensor_mul(sc11[0:1, 2:3], sc11[0:1, 0:1], sc11[0:1, 1:2]),
                  reads=[("sc11", 0), ("sc11", 1)], writes=[("sc11", 2)])
            au.op("scalar", lambda s: s.activation(sc11[0:1, 3:4], sc11[0:1, 2:3], AF.Sqrt),
                  reads=[("sc11", 2)], writes=[("sc11", 3)])
            au.op("vector", lambda v: v.tensor_scalar(sc11[0:1, 4:5], sc11[0:1, 3:4], -SCALE, None,
                                                      OP.mult),
                  reads=[("sc11", 3)], writes=[("sc11", 4)])
            au.op("tensor", lambda t: t.matmul(aux[:, 0:1], onesrF[:, :], sc11[0:1, 4:5],
                                               start=True, stop=True),
                  reads=[("sc11", 4), ("onesrF",)], writes=[("p2aux",)])
            au.op("vector", lambda v: v.tensor_copy(nmhat[:, :], aux[:, 0:1]),
                  reads=[("p2aux",)], writes=[("nmhat",)])
            psA.close()

        finishA()
        sbA.close()
        au.fence()
        vhp = ctx.sbuf("vhp", [128, 2, NT, 128], BF16)
        kTp = ctx.sbuf("kTp", [128, 2, T], BF16)
        qTp = ctx.sbuf("qTp", [128, 2, 512], BF16)
        Ab = ctx.sbuf("p2A", [128, 3, 2, 512], BF16)
        osb = ctx.sbuf("osb", [128, NP, 8, 512], BF16)
        lrow = ctx.sbuf("lrow", [1, 2, 512], F32)
        rlrow = ctx.sbuf("rlrow", [1, 2, 512], F32)
        rlbsave = ctx.sbuf("rlbsave", [1, 2, 8, 512], BF16)

        with ExitStack() as psB:
            simE = [ctx.psum("p2simE0", [128, 512], F32, stack=psB),
                    ctx.psum("p2simE1", [128, 512], F32, stack=psB)]
            simO = [ctx.psum("p2simO0", [128, 512], F32, stack=psB),
                    ctx.psum("p2simO1", [128, 512], F32, stack=psB)]
            op_ = ctx.psum("p2op", [128, 512], F32, stack=psB)
            lps = ctx.psum("p2lps", [128, 2, 512], F32, stack=psB)
            pend = []
            for pr in range(NP):
                pb = pr % 2
                au.op("gpsimd", lambda g, pr=pr, pb=pb: g.dma_start(
                    out=vhp[:, pb, :, :],
                    in_=vtm_dram.rearrange("(tk p) d -> p tk d", p=128)[:, :, pr * 128:(pr + 1) * 128]),
                    reads=[("vtmd", cc) for cc in range(NT)],
                    writes=[("vhp", pb)], dma=True, stream=f"p2vl{pb}")
                au.op("sync", lambda sy, pr=pr, pb=pb: sy.dma_start(
                    out=kTp[:, pb, :], in_=ktm_dram[:, pr * 128:(pr + 1) * 128], transpose=True),
                    reads=[("kttd", cc) for cc in range(NT)],
                    writes=[("kTp", pb)], dma=True, stream="p2ktr")
                for qc in range(8):
                    qb = qc % 2
                    au.op("sync", lambda sy, pr=pr, qc=qc, qb=qb: sy.dma_start(
                        out=qTp[:, qb, :],
                        in_=qtm_dram[qc * 512:(qc + 1) * 512, pr * 128:(pr + 1) * 128],
                        transpose=True),
                        reads=[("qttd", cc) for cc in range(qc * 4, qc * 4 + 4)],
                        writes=[("qTp", qb)], dma=True, stream="p2qtr")
                    ntk = qc * 4 + 4

                    def mk_sim(tk, pb=pb, qb=qb):
                        bk = tk % 2

                        def simfn(t, tk=tk, bk=bk, pb=pb, qb=qb):
                            t.matmul(simE[bk][:, :], kTp[0:64, pb, tk * 128:(tk + 1) * 128],
                                     qTp[0:64, qb, :], start=True, stop=True,
                                     tile_position=(0, 0), skip_group_check=True)
                            return t.matmul(simO[bk][:, :], kTp[64:128, pb, tk * 128:(tk + 1) * 128],
                                            qTp[64:128, qb, :], start=True, stop=True,
                                            tile_position=(64, 0), skip_group_check=True)
                        au.op("tensor", simfn, reads=[("kTp", pb), ("qTp", qb)],
                              writes=[("p2sE", tk % 2), ("p2sO", tk % 2)])

                    def mk_av(tk, pb=pb, ntk=ntk):
                        bk = tk % 3

                        def avfn(t, tk=tk, bk=bk, pb=pb, ntk=ntk):
                            for h in range(2):
                                t.matmul(op_[h * 64:(h + 1) * 64, :], vhp[:, pb, tk, h * 64:(h + 1) * 64],
                                         Ab[:, bk, h, :], start=(tk == 0), stop=False,
                                         tile_position=(0, h * 64), skip_group_check=True)
                                i = t.matmul(lps[0:1, h, :], onescB2[:, :], Ab[:, bk, h, :],
                                             start=(tk == 0), stop=(tk == ntk - 1),
                                             skip_group_check=True)
                            return i
                        au.op("tensor", avfn,
                              reads=[("p2A", bk, 0), ("p2A", bk, 1), ("vhp", pb), ("onescB2",)],
                              writes=[("p2op",), ("p2lps",)] if (tk == 0 or tk == ntk - 1)
                              else [("p2opacc",)])

                    mk_sim(0)
                    while pend:
                        pend.pop(0)()
                    for tk in range(ntk):
                        bk = tk % 3
                        sk = tk % 2
                        for h in range(2):
                            sp = simE if h == 0 else simO
                            spk = ("p2sE", sk) if h == 0 else ("p2sO", sk)
                            au.op("scalar", lambda s, sp=sp, sk=sk, bk=bk, h=h: s.activation(
                                Ab[:, bk, h, :], sp[sk][:, :], AF.Exp,
                                bias=nmhat[:, :], scale=SCALE),
                                reads=[("nmhat",)], xreads=[spk], writes=[("p2A", bk, h)])
                            if tk >= qc * 4:
                                au.op("gpsimd", lambda g, bk=bk, qc=qc, tk=tk, h=h: g.affine_select(
                                    Ab[:, bk, h, :], Ab[:, bk, h, :], pattern=[[1, 512]],
                                    compare_op=OP.is_ge, fill=0.0,
                                    base=qc * 512 - tk * 128, channel_multiplier=-1),
                                    reads=[("p2A", bk, h)], writes=[("p2A", bk, h)])
                        if tk + 1 < ntk:
                            mk_sim(tk + 1)
                        if tk >= 2:
                            mk_av(tk - 2)
                    if ntk >= 2:
                        mk_av(ntk - 2)
                    mk_av(ntk - 1)
                    au.op("scalar", lambda s, pr=pr, qc=qc: s.activation(
                        osb[:, pr, qc, :], op_[:, :], AF.Identity),
                        xreads=[("p2op",)], writes=[("osb", pr, qc)])
                    au.op("vector", lambda v: v.tensor_copy(
                        lrow.ap().rearrange("o a x -> o (a x)"),
                        lps.ap().rearrange("p a x -> p (a x)")[0:1, :]),
                        xreads=[("p2lps",)], writes=[("lrow",)])
                    au.op("vector", lambda v: v.reciprocal(
                        rlrow.ap().rearrange("o a x -> o (a x)"),
                        lrow.ap().rearrange("o a x -> o (a x)")),
                        reads=[("lrow",)], writes=[("rlrow",)])
                    def rlcp(v, qc=qc):
                        v.tensor_copy(rlbsave[:, 0, qc, :], rlrow[:, 0, :])
                        return v.tensor_copy(rlbsave[:, 1, qc, :], rlrow[:, 1, :])
                    au.op("vector", rlcp, reads=[("rlrow",)], writes=[("rlb", qc)])

                    def finalize(pr=pr, qc=qc):
                        def rlbcfn(t):
                            t.matmul(op_[0:64, :], onesrB[0:1, 0:64], rlbsave[0:1, 0, qc, :],
                                     start=True, stop=True, tile_position=(0, 0),
                                     skip_group_check=True)
                            return t.matmul(op_[64:128, :], onesrB[0:1, 0:64], rlbsave[0:1, 1, qc, :],
                                            start=True, stop=True, tile_position=(0, 64),
                                            skip_group_check=True)
                        au.op("tensor", rlbcfn, reads=[("rlb", qc), ("onesrB",)], writes=[("p2op",)])
                        au.op("vector", lambda v: v.tensor_mul(
                            osb[:, pr, qc, :], osb[:, pr, qc, :], op_[:, :]),
                            reads=[("osb", pr, qc)], xreads=[("p2op",)], writes=[("osb", pr, qc)])
                    pend.append(finalize)
            while pend:
                pend.pop(0)()
            for qc in range(8):
                OS = [("osb", pr, qc) for pr in range(NP)]
                for ts in range(4):
                    tb = ts % 2
                    c = qc * 4 + ts

                    def opfn(t, ts=ts, qc=qc):
                        for pr in range(NP):
                            for dc in range(2):
                                i = t.matmul(lps[:, dc, :], osb[:, pr, qc, ts * 128:(ts + 1) * 128],
                                             wosb[:, pr, dc * 512:(dc + 1) * 512],
                                             start=(pr == 0), stop=(pr == NP - 1),
                                             skip_group_check=True)
                        return i
                    au.op("tensor", opfn, reads=OS + [("wosb",), ("p2lps",)], writes=[("opp",)])
                    au.op("gpsimd", lambda g, c=c, tb=tb: g.dma_start(
                        out=r2t[:, tb, :], in_=r2_dram[c * 128:(c + 1) * 128, :]),
                        reads=[("r2d", c)],
                        writes=[("r2t", tb), ("p2rk", tb, 0), ("p2rk", tb, 1)], dma=True,
                        stream=f"p2rb{tb}")
                    au.op("scalar", lambda s, tb=tb: s.activation(
                        tob[:, tb, :], lps.ap().rearrange("p a x -> p (a x)"), AF.Identity),
                        xreads=[("opp",)], writes=[("tob", tb)])
                    au.op("vector", lambda v, tb=tb: v.tensor_add(tob[:, tb, :], tob[:, tb, :],
                                                                  bobc[:, :]),
                          reads=[("tob", tb), ("bobc", 0), ("bobc", 1)], writes=[("tob", tb)])
                    au.op("vector", lambda v, tb=tb: v.tensor_add(
                        tob[:, tb, :], tob[:, tb, :], r2t[:, tb, :]),
                        reads=[("tob", tb), ("r2t", tb), ("p2rk", tb, 0)],
                        writes=[("tob", tb), ("p2rk", tb, 0), ("p2rk", tb, 1)])
                    rkeys = [("tob", tb), ("p2rk", tb, 0), ("p2rk", tb, 1)]
                    emit_ln_inplace(ctx, au, tob[:, tb, :], rkeys, lnb2[tb], tag=20 + tb)
                    au.op("sync", lambda sy, c=c, tb=tb: sy.dma_start(
                        out=r3_dram[c * 128:(c + 1) * 128, :], in_=tob[:, tb, :]),
                        reads=rkeys, writes=[("r3d", c)], dma=True, stream=f"p2o{tb}")
        return [("r3d", c) for c in range(NT)]

    if tail_hook_reg is not None:
        return _cont
    return _cont()


def emit_p3(ctx, au, r3_dram, w1_dram, b1_dram, w2_dram, b2_dram, out_dram):
    nc = ctx.nc
    identB, identF = ctx.identB, ctx.identF
    INTER = 4096
    NF = INTER // 128

    w1sb = ctx.sbuf("w1sb", [128, ND, INTER], BF16)
    w2sb = ctx.sbuf("w2sb", [128, NF, D], BF16)
    b1c = ctx.sbuf("b1c", [128, NF], F32)
    b2bc = ctx.sbuf("b2bc", [128, D], F32)
    r3t = ctx.sbuf("r3t", [128, 2, D], F32)
    r3t2 = ctx.sbuf("r3t2", [128, 2, D], F32)
    r3Tb = ctx.sbuf("r3Tb", [128, 2, ND, 256], BF16)
    hck = ctx.sbuf("hck", [128, NF, 256], BF16)
    yt3 = ctx.sbuf("yt3", [128, 2, D], F32)
    onesrF3 = ctx.sbuf("onesrF3", [1, 128], F32)
    sqs3 = ctx.sbuf("p3sqs", [128, 1, D], BF16)
    mu3 = ctx.sbuf("p3mu", [128, 2], F32)
    ssq3 = ctx.sbuf("p3ssq", [128, 2], F32)
    rstd3 = ctx.sbuf("p3rstd", [128, 2], F32)
    mb3 = ctx.sbuf("p3mb", [128, 2], F32)
    lnb3 = [(sqs3[:, 0, :], mu3[:, b:b + 1], ssq3[:, b:b + 1], rstd3[:, b:b + 1],
             mb3[:, b:b + 1]) for b in range(2)]

    au.op("gpsimd", lambda g: g.iota(onesrF3[:, :], pattern=[[0, 128]], base=1,
                                     channel_multiplier=0, allow_small_or_imprecise_dtypes=True),
          writes=[("onesrF3",)])
    au.op("gpsimd", lambda g: g.dma_start(
        out=w1sb[:, :, :], in_=w1_dram.rearrange("(dt p) f -> p dt f", p=128)),
        writes=[("w1sb",)], dma=True, stream="p3w1")
    au.op("gpsimd", lambda g: g.dma_start(
        out=w2sb[:, :, :], in_=w2_dram.rearrange("(ft p) d -> p ft d", p=128)),
        writes=[("w2sb",)], dma=True, stream="p3w2")
    _p3b = ExitStack()
    b1row = ctx.sbuf("b1row", [1, INTER], F32, stack=_p3b)
    b2row = ctx.sbuf("b2row", [1, D], F32, stack=_p3b)
    au.op("gpsimd", lambda g: g.dma_start(out=b1row[:, :], in_=b1_dram[None, :]),
          writes=[("b1row",)], dma=True, stream="p3b1")
    au.op("gpsimd", lambda g: g.dma_start(out=b2row[:, :], in_=b2_dram[None, :]),
          writes=[("b2row",)], dma=True, stream="p3b2")

    with ExitStack() as ps3:
        tp3 = ctx.psum("p3tp", [128, 512], F32, stack=ps3)
        hps3 = [ctx.psum("p3hA", [128, 256], F32, stack=ps3),
                ctx.psum("p3hB", [128, 256], F32, stack=ps3)]
        yps3 = [ctx.psum("p3yA", [128, 2, 512], F32, stack=ps3),
                ctx.psum("p3yB", [128, 2, 512], F32, stack=ps3)]

        def b1cfn(t):
            for ft in range(NF):
                i = t.transpose(tp3[:, ft:ft + 1], b1row[0:1, ft * 128:(ft + 1) * 128],
                                identF[0:1, 0:1])
            return i
        au.op("tensor", b1cfn, reads=[("b1row",), ("identF",)], writes=[("p3tp",)])
        au.op("vector", lambda v: v.tensor_copy(b1c[:, :], tp3[:, 0:NF]),
              reads=[("p3tp",)], writes=[("b1c",)])
        au.op("tensor", lambda t: t.matmul(tp3[:, 0:512], onesrF3[:, :], b2row[0:1, 0:512],
                                           start=True, stop=True),
              reads=[("onesrF3",), ("b2row",)], writes=[("p3tp",)])
        au.op("vector", lambda v: v.tensor_copy(b2bc[:, 0:512], tp3[:, 0:512]),
              reads=[("p3tp",)], writes=[("b2bc", 0)])
        au.op("tensor", lambda t: t.matmul(tp3[:, 0:512], onesrF3[:, :], b2row[0:1, 512:1024],
                                           start=True, stop=True),
              reads=[("onesrF3",), ("b2row",)], writes=[("p3tp",)])
        au.op("vector", lambda v: v.tensor_copy(b2bc[:, 512:1024], tp3[:, 0:512]),
              reads=[("p3tp",)], writes=[("b2bc", 1)])
        au.fence()
        _p3b.close()

        def emit_chunk_T(tc):
            for tt in range(2):
                c = tc * 2 + tt
                b = c % 2
                au.op("gpsimd", lambda g, c=c, b=b: g.dma_start(
                    out=r3t[:, b, :], in_=r3_dram[c * 128:(c + 1) * 128, :]),
                    writes=[("r3t", b)], dma=True, stream=f"p3r{b}")
                for dp in range(ND // 2):
                    def t3fn(t, b=b, dp=dp):
                        t.transpose(tp3[:, 0:128], r3t[:, b, (2 * dp) * 128:(2 * dp + 1) * 128],
                                    identF[:, :])
                        return t.transpose(tp3[:, 128:256],
                                           r3t[:, b, (2 * dp + 1) * 128:(2 * dp + 2) * 128],
                                           identF[:, :])
                    au.op("tensor", t3fn, reads=[("r3t", b), ("identF",)], writes=[("p3tp",)])

                    def t3ev(v, tt=tt, dp=dp, tc=tc):
                        v.tensor_copy(r3Tb[:, tc % 2, 2 * dp, tt * 128:(tt + 1) * 128],
                                      tp3[:, 0:128])
                        return v.tensor_copy(r3Tb[:, tc % 2, 2 * dp + 1, tt * 128:(tt + 1) * 128],
                                             tp3[:, 128:256])
                    au.op("vector", t3ev, reads=[("p3tp",)],
                          writes=[("r3Tb", tc % 2, 2 * dp, tt), ("r3Tb", tc % 2, 2 * dp + 1, tt)])

        emit_chunk_T(0)
        for tc in range(16):
            R3T = [("r3Tb", tc % 2, d, tt) for d in range(ND) for tt in range(2)]
            for ft in range(NF):
                fb = ft % 2

                def hfn(t, ft=ft, fb=fb, tc=tc):
                    for d in range(ND):
                        i = t.matmul(hps3[fb][:, :], w1sb[:, d, ft * 128:(ft + 1) * 128],
                                     r3Tb[:, tc % 2, d, :], start=(d == 0), stop=(d == ND - 1))
                    return i
                au.op("tensor", hfn, reads=R3T + [("w1sb",)], writes=[("p3h", fb)])
                au.op("scalar", lambda s, ft=ft, fb=fb: s.activation(
                    hck[:, ft, :], hps3[fb][:, :], AF.Relu, bias=b1c[:, ft:ft + 1]),
                    reads=[("b1c",)], xreads=[("p3h", fb)], writes=[("hck", ft)])
            HCK = [("hck", ft) for ft in range(NF)]
            if tc + 1 < 16:
                emit_chunk_T(tc + 1)
            for tt in range(2):
                c = tc * 2 + tt
                yb = tt % 2

                def yfn(t, tt=tt, yb=yb):
                    for ft in range(NF):
                        for qc in range(2):
                            i = t.matmul(yps3[yb][:, qc, :], hck[:, ft, tt * 128:(tt + 1) * 128],
                                         w2sb[:, ft, qc * 512:(qc + 1) * 512],
                                         start=(ft == 0), stop=(ft == NF - 1))
                    return i
                au.op("tensor", yfn, reads=HCK + [("w2sb",)], writes=[("p3y", yb)])
                b = (c % 2) + 2
                au.op("gpsimd", lambda g, c=c, b=b: g.dma_start(
                    out=r3t2[:, b - 2, :], in_=r3_dram[c * 128:(c + 1) * 128, :]),
                    writes=[("r3t2", b - 2), ("p3rk", b, 0), ("p3rk", b, 1)], dma=True,
                    stream=f"p3rr{b - 2}")
                au.op("scalar", lambda s, yb=yb, b=b: s.activation(
                    yt3[:, b - 2, :], yps3[yb].ap().rearrange("p a x -> p (a x)"), AF.Identity),
                    xreads=[("p3y", yb)], writes=[("yt3", b - 2)])
                au.op("vector", lambda v, b=b: v.tensor_add(yt3[:, b - 2, :], yt3[:, b - 2, :],
                                                            b2bc[:, :]),
                      reads=[("yt3", b - 2), ("b2bc", 0), ("b2bc", 1)], writes=[("yt3", b - 2)])
                au.op("vector", lambda v, b=b: v.tensor_add(yt3[:, b - 2, :], yt3[:, b - 2, :],
                                                            r3t2[:, b - 2, :]),
                      reads=[("yt3", b - 2), ("r3t2", b - 2), ("p3rk", b, 0)],
                      writes=[("yt3", b - 2), ("p3rk", b, 0), ("p3rk", b, 1)])
                rkeys = [("yt3", b - 2), ("p3rk", b, 0), ("p3rk", b, 1)]
                emit_ln_inplace(ctx, au, yt3[:, b - 2, :], rkeys, lnb3[b - 2], tag=30 + b)
                au.op("sync", lambda sy, c=c, b=b: sy.dma_start(
                    out=out_dram[c * 128:(c + 1) * 128, :], in_=yt3[:, b - 2, :]),
                    reads=rkeys, writes=[("outd", c)], dma=True, stream=f"p3o{b - 2}")
    return [("outd", c) for c in range(NT)]


from concourse.bass_utils import run_bass_kernel_spmd

INTER = 4096


def _rope_tables():
    d = 64
    inv = 1.0 / 10000.0 ** (np.arange(0, d, 2, dtype=np.float64) / d)
    pos = np.arange(T, dtype=np.float64)
    ang = pos[:, None] * inv[None, :]
    ct = np.tile(np.cos(ang), (1, 16)).astype(np.float32)
    st = np.tile(np.sin(ang), (1, 16)).astype(np.float32)
    return ct, st


def _build(phase):
    nc = bass.Bass("TRN2", target_bir_lowering=False)
    with ExitStack() as stack:
        block = stack.enter_context(nc.Block())
        ctx = Ctx(nc, block, stack)
        au = AS(nc, block, stack)
        emit_ident(ctx, au)
        if phase == 0:
            x = nc.dram_tensor("x", [T, D], F32, kind="ExternalInput").ap()
            wqk = nc.dram_tensor("wqk", [2048, D], F32, kind="ExternalInput").ap()
            bqk = nc.dram_tensor("bqk", [2048], F32, kind="ExternalInput").ap()
            r1 = nc.dram_tensor("r1", [T, D], F32, kind="ExternalOutput").ap()
            xb = nc.dram_tensor("xb", [T, D], BF16, kind="Internal").ap()
            attnT = nc.dram_tensor("attnT", [NP, 128, T], BF16, kind="Internal").ap()
            outk = emit_p0(ctx, au, x, wqk, bqk, xb, attnT, r1)
        elif phase == 1:
            r1 = nc.dram_tensor("r1", [T, D], F32, kind="ExternalInput").ap()
            gw = nc.dram_tensor("gw", [8, D], F32, kind="ExternalInput").ap()
            w1 = nc.dram_tensor("w1", [8, D, INTER], F32, kind="ExternalInput").ap()
            w2 = nc.dram_tensor("w2", [8, INTER, D], F32, kind="ExternalInput").ap()
            r2 = nc.dram_tensor("r2", [T, D], F32, kind="ExternalOutput").ap()
            cx = nc.dram_tensor("cx", [8, KCAP, D], BF16, kind="Internal").ap()
            yd = nc.dram_tensor("yd", [8, KCAP, D], F32, kind="Internal").ap()
            probs, mprob, idxu = emit_p1(ctx, au, r1, gw, w1, w2, cx, yd, r2)
            emit_p1_experts(ctx, au, w1, w2, cx, yd, r1, idxu)
            outk = emit_p1_tail(ctx, au, r1, yd, r2, mprob, idxu)
        elif phase == 2:
            r2 = nc.dram_tensor("r2", [T, D], F32, kind="ExternalInput").ap()
            wq = nc.dram_tensor("wq", [D, D], F32, kind="ExternalInput").ap()
            wk = nc.dram_tensor("wk", [D, D], F32, kind="ExternalInput").ap()
            wv = nc.dram_tensor("wv", [D, D], F32, kind="ExternalInput").ap()
            wo = nc.dram_tensor("wo", [D, D], F32, kind="ExternalInput").ap()
            bo = nc.dram_tensor("bo", [D], F32, kind="ExternalInput").ap()
            ct = nc.dram_tensor("ct", [T, 512], F32, kind="ExternalInput").ap()
            st = nc.dram_tensor("st", [T, 512], F32, kind="ExternalInput").ap()
            r3 = nc.dram_tensor("r3", [T, D], F32, kind="ExternalOutput").ap()
            qtm = nc.dram_tensor("qtm", [T, D], BF16, kind="Internal").ap()
            ktm = nc.dram_tensor("ktm", [T, D], BF16, kind="Internal").ap()
            vtm = nc.dram_tensor("vtm", [T, D], BF16, kind="Internal").ap()
            outk = emit_p2(ctx, au, r2, wq, wk, wv, wo, bo, ct, st, qtm, ktm, vtm, r3)
        else:
            r3 = nc.dram_tensor("r3", [T, D], F32, kind="ExternalInput").ap()
            w1 = nc.dram_tensor("w1", [D, INTER], F32, kind="ExternalInput").ap()
            b1 = nc.dram_tensor("b1", [INTER], F32, kind="ExternalInput").ap()
            w2 = nc.dram_tensor("w2", [INTER, D], F32, kind="ExternalInput").ap()
            b2 = nc.dram_tensor("b2", [D], F32, kind="ExternalInput").ap()
            out = nc.dram_tensor("out", [T, D], F32, kind="ExternalOutput").ap()
            outk = emit_p3(ctx, au, r3, w1, b1, w2, b2, out)
        au.op("sync", lambda sy: sy.nop(), reads=outk)
        au.flush()
    return nc


def _build_fused():
    nc = bass.Bass("TRN2", target_bir_lowering=False)
    x = nc.dram_tensor("x", [T, D], F32, kind="ExternalInput").ap()
    wqk = nc.dram_tensor("wqk", [2048, D], F32, kind="ExternalInput").ap()
    bqk = nc.dram_tensor("bqk", [2048], F32, kind="ExternalInput").ap()
    gw = nc.dram_tensor("gw", [8, D], F32, kind="ExternalInput").ap()
    w1 = nc.dram_tensor("w1", [8, D, INTER], F32, kind="ExternalInput").ap()
    w2 = nc.dram_tensor("w2", [8, INTER, D], F32, kind="ExternalInput").ap()
    wq = nc.dram_tensor("wq", [D, D], F32, kind="ExternalInput").ap()
    wk = nc.dram_tensor("wk", [D, D], F32, kind="ExternalInput").ap()
    wv = nc.dram_tensor("wv", [D, D], F32, kind="ExternalInput").ap()
    wo = nc.dram_tensor("wo", [D, D], F32, kind="ExternalInput").ap()
    bo = nc.dram_tensor("bo", [D], F32, kind="ExternalInput").ap()
    fw1 = nc.dram_tensor("fw1", [D, INTER], F32, kind="ExternalInput").ap()
    fb1 = nc.dram_tensor("fb1", [INTER], F32, kind="ExternalInput").ap()
    fw2 = nc.dram_tensor("fw2", [INTER, D], F32, kind="ExternalInput").ap()
    fb2 = nc.dram_tensor("fb2", [D], F32, kind="ExternalInput").ap()
    ct = nc.dram_tensor("ct", [T, 512], F32, kind="ExternalInput").ap()
    st = nc.dram_tensor("st", [T, 512], F32, kind="ExternalInput").ap()
    out = nc.dram_tensor("out", [T, D], F32, kind="ExternalOutput").ap()
    r1 = nc.dram_tensor("r1", [T, D], F32, kind="Internal").ap()
    r2 = nc.dram_tensor("r2", [T, D], F32, kind="Internal").ap()
    r3 = nc.dram_tensor("r3", [T, D], F32, kind="Internal").ap()
    xb = nc.dram_tensor("xb", [T, D], BF16, kind="Internal").ap()
    attnT = nc.dram_tensor("attnT", [NP, 128, T], BF16, kind="Internal").ap()
    cx = nc.dram_tensor("cx", [8, KCAP, D], BF16, kind="Internal").ap()
    yd = nc.dram_tensor("yd", [8, KCAP, D], F32, kind="Internal").ap()
    qtm = nc.dram_tensor("qtm", [T, D], BF16, kind="Internal").ap()
    ktm = nc.dram_tensor("ktm", [T, D], BF16, kind="Internal").ap()
    vtm = nc.dram_tensor("vtm", [T, D], BF16, kind="Internal").ap()
    with ExitStack() as stack:
        block = stack.enter_context(nc.Block())
        ctx = Ctx(nc, block, stack)
        au = AS(nc, block, stack)
        main_stack = ctx.stack
        emit_ident(ctx, au)
        ctx.stack = ExitStack()
        emit_p0(ctx, au, x, wqk, bqk, xb, attnT, r1)
        ctx.stack.close()
        au.fence()
        p1stack = ExitStack()
        ctx.stack = p1stack
        probs, mprob, idxu = emit_p1(ctx, au, r1, gw, w1, w2, cx, yd, r2)
        estack = ExitStack()
        ctx.stack = estack
        emit_p1_experts(ctx, au, w1, w2, cx, yd, r1, idxu)
        estack.close()
        au.fence()
        tailstack = ExitStack()
        ctx.stack = tailstack
        hookbox = {}
        def tail_hook_reg(tileA):
            hookbox["tileA"] = tileA
        ctx.stack = ExitStack()
        p2stack = ctx.stack
        emit_p2_pending = emit_p2(ctx, au, r2, wq, wk, wv, wo, bo, ct, st, qtm, ktm, vtm, r3,
                                  tail_hook_reg=tail_hook_reg)
        ctx.stack = tailstack
        emit_p1_tail(ctx, au, r1, yd, r2, mprob, idxu,
                     after_tile=lambda c, rt, rk: hookbox["tileA"](c, rt, rk))
        tailstack.close()
        ctx.stack = p2stack
        emit_p2_pending()
        p2stack.close()
        p1stack.close()
        au.fence()
        ctx.stack = ExitStack()
        outk = emit_p3(ctx, au, r3, fw1, fb1, fw2, fb2, out)
        ctx.stack.close()
        ctx.stack = main_stack
        au.op("sync", lambda sy: sy.nop(), reads=outk)
        au.flush()
    return nc


def kernel(L0_attn_x, L0_attn_W_qk, L0_attn_b_qk, L0_ffn_gate_w, L0_ffn_w1, L0_ffn_w2,
           L1_attn_Wq, L1_attn_Wk, L1_attn_Wv, L1_attn_Wo, L1_attn_bo,
           L1_ffn_W1, L1_ffn_b1, L1_ffn_W2, L1_ffn_b2):
    n = 8
    cores = list(range(n))
    ct, st = _rope_tables()
    A = np.ascontiguousarray
    base = {
        "wqk": A(np.asarray(L0_attn_W_qk)), "bqk": A(np.asarray(L0_attn_b_qk)),
        "gw": A(np.asarray(L0_ffn_gate_w)), "w1": A(np.asarray(L0_ffn_w1)),
        "w2": A(np.asarray(L0_ffn_w2)),
        "wq": A(np.asarray(L1_attn_Wq)), "wk": A(np.asarray(L1_attn_Wk)),
        "wv": A(np.asarray(L1_attn_Wv)), "wo": A(np.asarray(L1_attn_Wo)),
        "bo": A(np.asarray(L1_attn_bo)),
        "fw1": A(np.asarray(L1_ffn_W1)), "fb1": A(np.asarray(L1_ffn_b1)),
        "fw2": A(np.asarray(L1_ffn_W2)), "fb2": A(np.asarray(L1_ffn_b2)),
        "ct": ct, "st": st,
    }
    in_maps = []
    for i in range(n):
        m = dict(base)
        m["x"] = A(np.asarray(L0_attn_x)[i])
        in_maps.append(m)
    nc = _build_fused()
    res = run_bass_kernel_spmd(nc, in_maps, core_ids=cores)
    return np.stack([res.results[i]["out"] for i in range(n)], axis=0).astype(np.float32)


# version 25
# speedup vs baseline: 1.0331x; 1.0331x over previous
import math
from contextlib import ExitStack, contextmanager
import numpy as np
import concourse.bass as bass
import concourse.mybir as mybir

F32 = mybir.dt.float32
BF16 = mybir.dt.bfloat16
U32 = mybir.dt.uint32
AF = mybir.ActivationFunctionType
OP = mybir.AluOpType
AX = mybir.AxisListType

T = 4096
D = 1024
NH = 16
DH = 64
NP = 8
NT = T // 128
ND = D // 128
LAM = 1.0507009873554805
ALPHA = 1.6732632423543772
LA = LAM * ALPHA
LN_LA = math.log(LA)
SCALE = DH ** -0.5
OSCALE = T ** -0.5
LN_EPS = 1e-5

ENGS = ("tensor", "vector", "scalar", "gpsimd", "sync")


class AS:

    def __init__(self, nc, block, stack):
        self.nc, self.block, self.stack = nc, block, stack
        self.sems = {}
        for e in ENGS:
            self.sems[(e, "c")] = stack.enter_context(nc.semaphore(f"as_{e}_c"))
        self.cnt = {k: 0 for k in self.sems}
        self.queue = {e: [] for e in ENGS}
        self.waited = {e: {} for e in ENGS}
        self.last_w = {}
        self.readers = {}
        self.flushed = False

    def dstream(self, e, stream):
        key = (e, "d", stream)
        if key not in self.sems:
            self.sems[key] = self.stack.enter_context(
                self.nc.semaphore(f"as_{e}_d_{stream}"))
            self.cnt[key] = 0
        return key

    def op(self, e, fn, reads=(), writes=(), xreads=(), dma=False, stream=None):
        if dma:
            mykey = self.dstream(e, stream or e)
            inc = 16
        else:
            mykey = (e, "c")
            inc = 1
        deps = {}

        def add_dep(dk, dc):
            if dk == mykey and not dma and e == "tensor":
                return
            deps[dk] = max(deps.get(dk, 0), dc)

        for r in tuple(reads) + tuple(writes) + tuple(xreads):
            lw = self.last_w.get(r)
            if lw is not None:
                add_dep(*lw)
        for r in tuple(writes) + tuple(xreads):
            for rd in self.readers.get(r, ()):
                add_dep(*rd)
        mycnt = self.cnt[mykey] + inc
        self.cnt[mykey] = mycnt
        waits = []
        w = self.waited[e]
        for dk, dc in deps.items():
            if w.get(dk, 0) < dc:
                w[dk] = dc
                waits.append((dk, dc))
        sems = self.sems
        me = sems[mykey]

        def closure(eng, waits=waits, fn=fn, me=me, inc=inc):
            for dk, dc in waits:
                eng.wait_ge(sems[dk], dc)
            try:
                instr = fn(eng)
            except Exception:
                import sys
                print("FAILING OP waits:", waits, "inc:", inc, file=sys.stderr)
                raise
            instr.then_inc(me, inc)

        self.queue[e].append(closure)
        for r in tuple(writes):
            self.last_w[r] = (mykey, mycnt)
            self.readers[r] = []
        for r in tuple(reads) + tuple(xreads):
            self.readers.setdefault(r, []).append((mykey, mycnt))
        return (mykey, mycnt)

    def fence(self):
        snap = [(k, c) for k, c in self.cnt.items() if c > 0]
        self.closed = getattr(self, "closed", set())
        for e in ENGS:
            waits = []
            for k, c in snap:
                if k == (e, "c"):
                    continue
                if self.waited[e].get(k, 0) < c:
                    self.waited[e][k] = c
                    waits.append((k, c))
            if waits:
                def wfn(eng, waits=waits):
                    for k, c in waits:
                        eng.wait_ge(self.sems[k], c)
                self.queue[e].append(wfn)
        for k, c in snap:
            if len(k) == 3:
                self.closed.add(k)

    def flush(self):
        assert not self.flushed
        self.flushed = True
        for key, cnt in list(self.cnt.items()):
            if len(key) == 3 and cnt > 0:
                e = key[0]
                if self.waited[e].get(key, 0) < cnt:
                    self.waited[e][key] = cnt
                    self.queue[e].append(
                        lambda eng, key=key, cnt=cnt: eng.wait_ge(self.sems[key], cnt))
        for e in ENGS:
            q = self.queue[e]
            if not q:
                continue

            def runner(eng, q=q):
                for c in q:
                    c(eng)

            getattr(self.block, e)(runner)


class Ctx:
    def __init__(self, nc, block, stack):
        self.nc, self.block, self.stack = nc, block, stack

    def sbuf(self, name, shape, dtype, stack=None):
        return (stack or self.stack).enter_context(self.nc.sbuf_tensor(name, shape, dtype))

    def psum(self, name, shape, dtype=F32, stack=None):
        return (stack or self.stack).enter_context(self.nc.psum_tensor(name, shape, dtype))


def emit_ident(ctx, au):
    identB = ctx.sbuf("identB", [128, 128], BF16)
    identF = ctx.sbuf("identF", [128, 128], F32)
    au.op("gpsimd", lambda g: g.iota(identB[:, :], pattern=[[1, 128]], base=0,
                                     channel_multiplier=-1, allow_small_or_imprecise_dtypes=True),
          writes=[("identB",)])
    au.op("gpsimd", lambda g: g.iota(identF[:, :], pattern=[[1, 128]], base=0,
                                     channel_multiplier=-1, allow_small_or_imprecise_dtypes=True),
          writes=[("identF",)])
    au.op("vector", lambda v: v.tensor_scalar(identB[:, :], identB[:, :], 0.0, None, OP.is_equal),
          reads=[("identB",)], writes=[("identB",)])
    au.op("vector", lambda v: v.tensor_scalar(identF[:, :], identF[:, :], 0.0, None, OP.is_equal),
          reads=[("identF",)], writes=[("identF",)])
    ccol = ctx.sbuf("ccol", [128, 2], F32)
    au.op("vector", lambda v: v.memset(ccol[:, 0:1], LN_LA), writes=[("ccol",)])
    ctx.identB, ctx.identF, ctx.ccol = identB, identF, ccol


def emit_tm_to_fm(ctx, au, pstack, src_dram, dst_fm, nt, nd, tag, also_bf16_dram=None,
                  src_key=None):
    identB = ctx.identB
    xtile = ctx.sbuf(f"tt_{tag}", [128, 2, nd * 128], BF16)
    tps = [ctx.psum(f"tp_{tag}A", [128, 1024], BF16, stack=pstack),
           ctx.psum(f"tp_{tag}B", [128, 1024], BF16, stack=pstack)]
    for c in range(nt):
        b = c % 2
        au.op("gpsimd", lambda g, c=c, b=b: g.dma_start(
            out=xtile[:, b, :], in_=src_dram[c * 128:(c + 1) * 128, :]),
            reads=[(src_key, c)] if src_key else [], writes=[(f"tt_{tag}", b)], dma=True,
            stream=f"tt{tag}{b}")
        if also_bf16_dram is not None:
            au.op("gpsimd", lambda g, c=c, b=b: g.dma_start(
                out=also_bf16_dram[c * 128:(c + 1) * 128, :], in_=xtile[:, b, :]),
                reads=[(f"tt_{tag}", b)], writes=[(f"{tag}bd", c)], dma=True,
                stream=f"st{tag}{b}")
        for dp in range(nd // 2):
            pb = (c * (nd // 2) + dp) % 2

            def mmfn(t, c=c, dp=dp, pb=pb, b=b):
                t.transpose(tps[pb][:, 0:128], xtile[:, b, (2 * dp) * 128:(2 * dp + 1) * 128],
                            identB[:, :])
                return t.transpose(tps[pb][:, 128:256],
                                   xtile[:, b, (2 * dp + 1) * 128:(2 * dp + 2) * 128], identB[:, :])

            au.op("tensor", mmfn, reads=[(f"tt_{tag}", b), ("identB",)],
                  writes=[(f"tp_{tag}", pb)])

            def evfn(v, c=c, dp=dp, pb=pb):
                v.tensor_copy(dst_fm[:, 2 * dp, c * 128:(c + 1) * 128], tps[pb][:, 0:128])
                return v.tensor_copy(dst_fm[:, 2 * dp + 1, c * 128:(c + 1) * 128],
                                     tps[pb][:, 128:256])

            au.op("vector", evfn, reads=[(f"tp_{tag}", pb)],
                  writes=[(f"fm_{tag}", 2 * dp), (f"fm_{tag}", 2 * dp + 1)])


def emit_ln_inplace(ctx, au, rt, rkeys, lnbufs, outkeys=(), tag=0):
    sqs, mu, ssq, rstd, mb = lnbufs
    au.op("vector", lambda v: v.reduce_sum(mu[:, :], rt, axis=AX.X),
          reads=rkeys, writes=[("ln_mu", tag)])
    au.op("scalar", lambda s: s.activation(sqs[:, :], rt, AF.Square, accum_out=ssq[:, :]),
          reads=rkeys, writes=[("ln_ssq", tag), ("ln_sqs", tag), ("ln_sqs_sh",)])
    au.op("vector", lambda v: v.tensor_scalar(mu[:, :], mu[:, :], 1.0 / D, None, OP.mult),
          reads=[("ln_mu", tag)], writes=[("ln_mu", tag)])
    au.op("vector", lambda v: v.tensor_tensor(rstd[:, :], mu[:, :], mu[:, :], OP.mult),
          reads=[("ln_mu", tag)], writes=[("ln_rstd", tag)])
    au.op("vector", lambda v: v.scalar_tensor_tensor(rstd[:, :], ssq[:, :], 1.0 / D, rstd[:, :],
                                                     OP.mult, OP.subtract),
          reads=[("ln_ssq", tag), ("ln_rstd", tag)], writes=[("ln_rstd", tag)])
    au.op("vector", lambda v: v.tensor_scalar(rstd[:, :], rstd[:, :], 0.0, LN_EPS, OP.max, OP.add),
          reads=[("ln_rstd", tag)], writes=[("ln_rstd", tag)])
    au.op("scalar", lambda s: s.activation(rstd[:, :], rstd[:, :], AF.Sqrt),
          reads=[("ln_rstd", tag)], writes=[("ln_rstd", tag)])
    au.op("vector", lambda v: v.reciprocal(rstd[:, :], rstd[:, :]),
          reads=[("ln_rstd", tag)], writes=[("ln_rstd", tag)])
    au.op("vector", lambda v: v.scalar_tensor_tensor(mb[:, :], mu[:, :], -1.0, rstd[:, :],
                                                     OP.mult, OP.mult),
          reads=[("ln_mu", tag), ("ln_rstd", tag)], writes=[("ln_mb", tag)])
    au.op("scalar", lambda s: s.activation(rt, rt, AF.Identity, bias=mb[:, :], scale=rstd[:, :]),
          reads=list(rkeys) + [("ln_mb", tag), ("ln_rstd", tag)], writes=list(rkeys) + list(outkeys))


def emit_p0(ctx, au, x_dram, wqk_dram, bqk_dram, xb_dram, attnT_dram, r1_dram, stages='ABCDE'):
    nc = ctx.nc
    identB, identF = ctx.identB, ctx.identF

    xT = ctx.sbuf("xT", [128, ND, T], BF16)
    wqkT = ctx.sbuf("wqkT", [128, ND, 2048], BF16)
    b_row = ctx.sbuf("b_row", [1, 2048], F32)
    nkbar_row = ctx.sbuf("nkbar_row", [1, 2048], F32)
    xbar = ctx.sbuf("xbar", [128, ND], F32)
    xbar_b = ctx.sbuf("xbar_b", [128, ND], BF16)
    qb_col = ctx.sbuf("qb_col", [128, NP], F32)
    kb_col = ctx.sbuf("kb_col", [128, NP], F32)
    ov_col = ctx.sbuf("ov_col", [128, NP], F32)

    with ExitStack() as ps_ab:
        xtile = ctx.sbuf("tt_x", [128, 2, D], BF16)
        for c in range(NT):
            b = c % 2
            au.op("gpsimd", lambda g, c=c, b=b: g.dma_start(
                out=xtile[:, b, :], in_=x_dram[c * 128:(c + 1) * 128, :]),
                writes=[("tt_x", b)], dma=True, stream=f"ttx{b}")
            au.op("gpsimd", lambda g, c=c, b=b: g.dma_start(
                out=xb_dram[c * 128:(c + 1) * 128, :], in_=xtile[:, b, :]),
                reads=[("tt_x", b)], writes=[("xbd", c)], dma=True, stream=f"stx{b}")
            au.op("sync", lambda sy, c=c: sy.dma_start(
                out=xT[:, :, c * 128:(c + 1) * 128], in_=xb_dram[c * 128:(c + 1) * 128, :],
                transpose=True),
                reads=[("xbd", c)], writes=[("fm_x", c % 2)] + [("fm_xd", d) for d in range(ND)],
                dma=True, stream="xtr")
        XT = [("fm_xd", d) for d in range(ND)]

        wnat = ctx.sbuf("wnat", [128, 2, 4, D], BF16)
        au.op("gpsimd", lambda g: g.dma_start(out=b_row[:, :], in_=bqk_dram[None, :]),
              writes=[("b_row",)], dma=True, stream="bload")
        wps = [ctx.psum("wpsA", [128, 1024], BF16, stack=ps_ab),
               ctx.psum("wpsB", [128, 1024], BF16, stack=ps_ab)]
        for rnd in range(4):
            rb = rnd % 2
            au.op("gpsimd", lambda g, rnd=rnd, rb=rb: g.dma_start(
                out=wnat[:, rb, :, :],
                in_=wqk_dram.rearrange("(jt p) d -> p jt d", p=128)[:, rnd * 4:(rnd + 1) * 4, :]),
                writes=[("wnat", rb)], dma=True, stream=f"wload{rb}")
            for jl in range(4):
                jt = rnd * 4 + jl
                for dp in range(ND // 2):
                    pb = (jt * (ND // 2) + dp) % 2

                    def mmfn(t, jl=jl, dp=dp, pb=pb, rb=rb):
                        t.transpose(wps[pb][:, 0:128],
                                    wnat[:, rb, jl, (2 * dp) * 128:(2 * dp + 1) * 128],
                                    identB[:, :])
                        return t.transpose(wps[pb][:, 128:256],
                                           wnat[:, rb, jl, (2 * dp + 1) * 128:(2 * dp + 2) * 128],
                                           identB[:, :])

                    au.op("tensor", mmfn, reads=[("wnat", rb), ("identB",)], writes=[("wps", pb)])

                    def evfn(v, jt=jt, dp=dp, pb=pb):
                        v.tensor_copy(wqkT[:, 2 * dp, jt * 128:(jt + 1) * 128], wps[pb][:, 0:128])
                        return v.tensor_copy(wqkT[:, 2 * dp + 1, jt * 128:(jt + 1) * 128],
                                             wps[pb][:, 128:256])

                    au.op("vector", evfn, reads=[("wps", pb)], writes=[("wqkT",)])

        if 'C' not in stages:
            return []
        def xbarfn(v):
            for d in range(ND):
                i = v.reduce_sum(xbar[:, d:d + 1], xT[:, d, :], axis=AX.X)
            return i

        au.op("vector", xbarfn, reads=XT, writes=[("xbar",)])
        au.op("vector", lambda v: v.tensor_scalar(xbar[:, :], xbar[:, :], 1.0 / T, None, OP.mult),
              reads=[("xbar",)], writes=[("xbar",)])
        au.op("vector", lambda v: v.tensor_copy(xbar_b[:, :], xbar[:, :]),
              reads=[("xbar",)], writes=[("xbar_b",)])
        au.op("vector", lambda v: v.tensor_scalar(ov_col[:, :], xbar[:, :], -LA * T * OSCALE,
                                                  None, OP.mult),
              reads=[("xbar",)], writes=[("ov_col",)])
        kps1 = ctx.psum("kps1", [1, 512], F32, stack=ps_ab)
        bps = ctx.psum("bps", [128, 512], F32, stack=ps_ab)
        for c in range(4):
            def kbarfn(t, c=c):
                for d in range(ND):
                    i = t.matmul(kps1[:, :], xbar_b[:, d:d + 1], wqkT[:, d, c * 512:(c + 1) * 512],
                                 start=(d == 0), stop=(d == ND - 1))
                return i

            au.op("tensor", kbarfn, reads=[("xbar_b",), ("wqkT",)], writes=[("kps1",)])
            au.op("scalar", lambda s, c=c: s.activation(nkbar_row[:, c * 512:(c + 1) * 512],
                                                        kps1[:, :], AF.Identity, scale=-1.0),
                  reads=[("kps1",)], writes=[("nkbar_row",)])

        def biasfn(t):
            for ph in range(NP):
                t.transpose(bps[:, ph * 2:ph * 2 + 1],
                            b_row[0:1, ph * 256: ph * 256 + 256: 2], identF[0:1, 0:1])
                i = t.transpose(bps[:, ph * 2 + 1:ph * 2 + 2],
                                nkbar_row[0:1, ph * 256 + 1: ph * 256 + 256: 2], identF[0:1, 0:1])
            return i

        au.op("tensor", biasfn, reads=[("b_row",), ("nkbar_row",), ("identF",)],
              writes=[("bps",)])

        def biasev(v):
            for ph in range(NP):
                v.tensor_copy(qb_col[:, ph:ph + 1], bps[:, ph * 2:ph * 2 + 1])
                i = v.tensor_copy(kb_col[:, ph:ph + 1], bps[:, ph * 2 + 1:ph * 2 + 2])
            return i

        au.op("vector", biasev, reads=[("bps",)], writes=[("qb_col",), ("kb_col",)])

    TQC = 512
    NTQ = T // TQC
    with ExitStack() as ps_d:
        kpair = ctx.sbuf("kpair", [128, 2, T], BF16)
        qsl = ctx.sbuf("qsl", [128, 2, TQC], BF16)
        vpair = ctx.sbuf("vpair", [128, 1, NT, 128], BF16)
        Eb = ctx.sbuf("Eb", [128, 3, 2, TQC], BF16)
        A1b = ctx.sbuf("A1b", [128, 3, 2, TQC], BF16)
        A2b = ctx.sbuf("A2b", [128, 3, 2, TQC], BF16)
        othf = ctx.sbuf("othf", [128, 2, TQC], BF16)

        qps = ctx.psum("qps", [128, 512], F32, stack=ps_d)
        sps = [ctx.psum("spsA", [128, 2, 512], F32, stack=ps_d),
               ctx.psum("spsB", [128, 2, 512], F32, stack=ps_d),
               ctx.psum("spsC", [128, 2, 512], F32, stack=ps_d)]
        opsum = ctx.psum("opsum", [128, 512], F32, stack=ps_d)
        kpps = [qps, qps]

        XBD = [("xbd", c) for c in range(NT)]
        for ph in range(NP):
            bp = 0
            au.op("gpsimd", lambda g, ph=ph, bp=bp: g.dma_start(
                out=vpair[:, bp, :, :],
                in_=xb_dram.rearrange("(tk p) d -> p tk d", p=128)[:, :, ph * 128:(ph + 1) * 128]),
                reads=XBD, writes=[("vpair", bp)], dma=True, stream="vp0")
            for tc in range(T // 512):
                kb = tc % 2

                def kproj(t, ph=ph, tc=tc, kb=kb):
                    for d in range(ND):
                        st0, sp = (d == 0), (d == ND - 1)
                        t.matmul(kpps[kb][0:64, :], wqkT[:, d, ph * 256 + 1: ph * 256 + 128: 2],
                                 xT[:, d, tc * 512:(tc + 1) * 512], start=st0, stop=sp,
                                 tile_position=(0, 0), skip_group_check=True)
                        i = t.matmul(kpps[kb][64:128, :],
                                     wqkT[:, d, ph * 256 + 129: ph * 256 + 256: 2],
                                     xT[:, d, tc * 512:(tc + 1) * 512], start=st0, stop=sp,
                                     tile_position=(0, 64), skip_group_check=True)
                    return i

                au.op("tensor", kproj, reads=XT + [("wqkT",)], writes=[("qps",)])
                au.op("scalar", lambda s, bp=bp, ph=ph, tc=tc, kb=kb: s.activation(
                    kpair[:, bp, tc * 512:(tc + 1) * 512], kpps[kb][:, :], AF.Identity,
                    bias=kb_col[:, ph:ph + 1], scale=1.0),
                    reads=[("kb_col",)], xreads=[("qps",)], writes=[("kpair", bp, tc)])
            for tq in range(NTQ):
                bq = tq % 2

                def qproj(t, ph=ph, tq=tq):
                    col0 = tq * TQC
                    for d in range(ND):
                        st0, sp = (d == 0), (d == ND - 1)
                        t.matmul(qps[0:64, :], wqkT[:, d, ph * 256: ph * 256 + 128: 2],
                                 xT[:, d, col0:col0 + 512], start=st0, stop=sp,
                                 tile_position=(0, 0), skip_group_check=True)
                        i = t.matmul(qps[64:128, :],
                                     wqkT[:, d, ph * 256 + 128: ph * 256 + 256: 2],
                                     xT[:, d, col0:col0 + 512], start=st0, stop=sp,
                                     tile_position=(0, 64), skip_group_check=True)
                    return i

                au.op("tensor", qproj, reads=XT + [("wqkT",)], writes=[("qps",)])
                au.op("scalar", lambda s, bq=bq, ph=ph: s.activation(
                    qsl[:, bq, :], qps[:, :], AF.Identity,
                    bias=qb_col[:, ph:ph + 1], scale=1.0),
                    reads=[("qb_col",)], xreads=[("qps",)], writes=[("qsl", bq)])
                def mk_sim(tk):
                    bk = tk % 3

                    def simfn(t, bp=bp, bq=bq, tk=tk, bk=bk):
                        for h in range(2):
                            i = t.matmul(sps[bk][:, h, :],
                                         kpair[h * 64:(h + 1) * 64, bp, tk * 128:(tk + 1) * 128],
                                         qsl[h * 64:(h + 1) * 64, bq, :],
                                         start=True, stop=True, tile_position=(h * 64, 0),
                                         skip_group_check=True)
                        return i

                    au.op("tensor", simfn, reads=[("kpair", bp, tk // 4), ("qsl", bq)],
                          writes=[("sps", bk)])

                def mk_av(tk):
                    bk = tk % 3

                    def avfn(t, bp=bp, tk=tk, bk=bk):
                        for h in range(2):
                            st0 = (tk == 0)
                            t.matmul(opsum[h * 64:(h + 1) * 64, :],
                                     vpair[:, bp, tk, h * 64:(h + 1) * 64],
                                     A1b[:, bk, h, :],
                                     start=st0, stop=False, tile_position=(0, h * 64),
                                     skip_group_check=True)
                            i = t.matmul(opsum[h * 64:(h + 1) * 64, :],
                                         vpair[:, bp, tk, h * 64:(h + 1) * 64],
                                         A2b[:, bk, h, :],
                                         start=False, stop=(tk == NT - 1), tile_position=(0, h * 64),
                                         skip_group_check=True)
                        return i

                    au.op("tensor", avfn,
                          reads=[("A1b", bk), ("A2b", bk), ("vpair", bp)],
                          writes=[("opsum",)] if (tk == 0 or tk == NT - 1) else [("opsum_acc",)])

                mk_sim(0)
                for tk in range(NT):
                    bk = tk % 3
                    au.op("scalar", lambda s, bk=bk: s.activation(
                        Eb[:, bk, :, :].rearrange("p a b -> p (a b)"),
                        sps[bk].ap().rearrange("p a b -> p (a b)"), AF.Exp,
                        bias=ctx.ccol[:, 0:1], scale=SCALE),
                        reads=[("ccol",)], xreads=[("sps", bk)], writes=[("Eb", bk)])
                    if tk % 6 == 5:
                        au.op("scalar", lambda s, bk=bk: s.activation(
                            A1b[:, bk, :, :].rearrange("p a b -> p (a b)"),
                            sps[bk].ap().rearrange("p a b -> p (a b)"), AF.Relu,
                            scale=LAM * SCALE),
                            reads=[], xreads=[("sps", bk)], writes=[("A1b", bk)])
                    else:
                        au.op("vector", lambda v, bk=bk: v.tensor_scalar(
                            A1b[:, bk, :, :].rearrange("p a b -> p (a b)"),
                            sps[bk].ap().rearrange("p a b -> p (a b)"),
                            LAM * SCALE, 0.0, OP.mult, OP.max),
                            reads=[], xreads=[("sps", bk)], writes=[("A1b", bk)])
                    au.op("vector", lambda v, bk=bk: v.tensor_scalar(
                        A2b[:, bk, :, :].rearrange("p a b -> p (a b)"),
                        Eb[:, bk, :, :].rearrange("p a b -> p (a b)"), LA, None, OP.min),
                        reads=[("Eb", bk)], writes=[("A2b", bk)])
                    if tk + 1 < NT:
                        mk_sim(tk + 1)
                    if tk >= 2:
                        mk_av(tk - 2)
                mk_av(NT - 2)
                mk_av(NT - 1)
                bq2 = tq % 2
                au.op("scalar", lambda s, ph=ph, bq2=bq2: s.activation(
                    othf[:, bq2, :], opsum[:, :], AF.Identity,
                    bias=ov_col[:, ph:ph + 1], scale=OSCALE),
                    reads=[("ov_col",)], xreads=[("opsum",)], writes=[("othf", bq2)])
                au.op("sync", lambda sy, ph=ph, tq=tq, bq2=bq2: sy.dma_start(
                    out=attnT_dram[ph, :, tq * TQC:(tq + 1) * TQC], in_=othf[:, bq2, :]),
                    reads=[("othf", bq2)], writes=[("attnTd", ph, tq)], dma=True, stream=f"ost{bq2}")

    if 'E' not in stages:
        return []
    with ExitStack() as ps_e:
        atile = ctx.sbuf("atile", [128, 2, NP, 128], BF16)
        rtile = ctx.sbuf("rtile", [128, 2, D], F32)
        xf32 = ctx.sbuf("xf32", [128, 2, D], F32)
        sqs = ctx.sbuf("sqs", [128, 2, D], BF16)
        mu = ctx.sbuf("mu", [128, 2], F32)
        ssq = ctx.sbuf("ssq", [128, 2], F32)
        rstd = ctx.sbuf("rstd", [128, 2], F32)
        mb = ctx.sbuf("mb", [128, 2], F32)
        ttp = [ctx.psum("ttpA", [128, 1024], BF16, stack=ps_e),
               ctx.psum("ttpB", [128, 1024], BF16, stack=ps_e)]
        lnbufs = [(sqs[:, b, :], mu[:, b:b + 1], ssq[:, b:b + 1], rstd[:, b:b + 1],
                   mb[:, b:b + 1]) for b in range(2)]

        for c in range(NT):
            b = c % 2
            tqc = c // 4
            au.op("gpsimd", lambda g, c=c, b=b: g.dma_start(
                out=xf32[:, b, :], in_=x_dram[c * 128:(c + 1) * 128, :]),
                writes=[("xf32", b)], dma=True, stream=f"xf{b}")
            au.op("gpsimd", lambda g, c=c, b=b: g.dma_start(
                out=atile[:, b, :, :],
                in_=attnT_dram.rearrange("h p t -> p h t")[:, :, c * 128:(c + 1) * 128]),
                reads=[("attnTd", ph, tqc) for ph in range(NP)],
                writes=[("atile", b)], dma=True, stream=f"at{b}")
            for dp in range(ND // 2):
                pb = dp % 2

                def ttfn(t, c=c, dp=dp, pb=pb, b=b):
                    t.transpose(ttp[pb][:, 0:128], atile[:, b, 2 * dp, :], identB[:, :])
                    return t.transpose(ttp[pb][:, 128:256], atile[:, b, 2 * dp + 1, :], identB[:, :])

                au.op("tensor", ttfn, reads=[("atile", b), ("identB",)], writes=[("ttp", pb)])

                def addfn(v, dp=dp, pb=pb, b=b):
                    v.tensor_add(rtile[:, b, (2 * dp) * 128:(2 * dp + 1) * 128],
                                 xf32[:, b, (2 * dp) * 128:(2 * dp + 1) * 128], ttp[pb][:, 0:128])
                    return v.tensor_add(rtile[:, b, (2 * dp + 1) * 128:(2 * dp + 2) * 128],
                                        xf32[:, b, (2 * dp + 1) * 128:(2 * dp + 2) * 128],
                                        ttp[pb][:, 128:256])

                au.op("vector", addfn, reads=[("ttp", pb), ("xf32", b)],
                      writes=[("rtile", b, dp)])
            rkeys = [("rtile", b, dp) for dp in range(ND // 2)]
            emit_ln_inplace(ctx, au, rtile[:, b, :], rkeys, lnbufs[b], tag=b)
            au.op("sync", lambda sy, c=c, b=b: sy.dma_start(
                out=r1_dram[c * 128:(c + 1) * 128, :], in_=rtile[:, b, :]),
                reads=rkeys, writes=[("r1d", c)], dma=True, stream=f"r1st{b}")
    return [("r1d", c) for c in range(NT)]


BIGIDX = 4000000.0
KCAP = 1024
NE = 8


def _breg(ctx, g):
    if not hasattr(ctx, "breg"):
        ctx.breg = g.to_reg(KCAP - 1)
    return ctx.breg


def emit_p1(ctx, au, r1_dram, gw_dram, w1_dram, w2_dram, cx_dram, y_dram, r2_dram):
    nc = ctx.nc
    identB, identF = ctx.identB, ctx.identF
    from concourse.bass import IndirectOffsetOnAxis

    mprob = ctx.sbuf("mprob", [128, NT, NE], F32)
    idxu = ctx.sbuf("idxu", [128, NT, NE], U32)
    _p1s = ExitStack()
    probs = ctx.sbuf("probs", [128, NT, NE], F32, stack=_p1s)
    probsE = ctx.sbuf("probsE", [8, NT * 128], F32, stack=_p1s)
    masksA = ctx.sbuf("masksA", [128, NT, NE], BF16, stack=_p1s)
    gwT = ctx.sbuf("gwT", [128, ND, NE], F32, stack=_p1s)
    gwsb = ctx.sbuf("gwsb", [8, D], F32, stack=_p1s)
    Umat = ctx.sbuf("Umat", [128, 128], BF16, stack=_p1s)
    onescolB = ctx.sbuf("onescolB", [128, 1], BF16, stack=_p1s)
    onesrowF = ctx.sbuf("onesrowF", [1, 128], F32, stack=_p1s)
    cnts = ctx.sbuf("cnts", [1, NT * NE], F32, stack=_p1s)
    offs = ctx.sbuf("offs", [1, 2, NT * NE], F32, stack=_p1s)
    offs_ex = ctx.sbuf("offs_ex", [1, NT * NE], F32, stack=_p1s)
    tau = ctx.sbuf("tau", [8, 8], F32, stack=_p1s)
    taurow = ctx.sbuf("taurow", [1, NE], F32, stack=_p1s)
    taubc = ctx.sbuf("taubc", [128, NE], F32, stack=_p1s)
    r1t = ctx.sbuf("r1t", [128, 2, D], F32, stack=_p1s)
    rtT = ctx.sbuf("rtT", [128, 2, ND, 128], F32, stack=_p1s)
    scs = ctx.sbuf("scs", [128, 2, NE], F32, stack=_p1s)
    smx = ctx.sbuf("smx", [128, 2, 4], F32, stack=_p1s)
    maskf = ctx.sbuf("maskf", [128, 2, NE], F32, stack=_p1s)
    post = ctx.sbuf("post", [128, 2, NE], F32, stack=_p1s)
    idxf = ctx.sbuf("idxf", [128, 2, NE], F32, stack=_p1s)
    rep = ctx.sbuf("rep", [128, NT * 128], F32, stack=_p1s)
    scr = ctx.sbuf("scr", [128, NT * 128], BF16, stack=_p1s)
    sel2 = ctx.sbuf("sel2", [128, 8], F32, stack=_p1s)
    selrepT = ctx.sbuf("selrepT", [8, 128], F32, stack=_p1s)
    iot8 = ctx.sbuf("iot8", [128, 8], F32, stack=_p1s)
    srt8 = ctx.sbuf("srt8", [8, 128], F32, stack=_p1s)
    bcols = ctx.sbuf("bcols", [128, 10], F32, stack=_p1s)
    bsb = ctx.sbuf("bsb", [8, 1], F32, stack=_p1s)
    shtb = ctx.sbuf("shtb", [128, 8, 16], BF16, stack=_p1s)
    predb = ctx.sbuf("predb", [128, 1], BF16, stack=_p1s)

    au.op("gpsimd", lambda g: g.iota(Umat[:, :], pattern=[[1, 128]], base=0,
                                     channel_multiplier=-1, allow_small_or_imprecise_dtypes=True),
          writes=[("Umat",)])
    au.op("vector", lambda v: v.tensor_scalar(Umat[:, :], Umat[:, :], 0.0, None, OP.is_ge),
          reads=[("Umat",)], writes=[("Umat",)])
    au.op("gpsimd", lambda g: g.iota(onescolB[:, :], pattern=[[0, 1]], base=1,
                                     channel_multiplier=0, allow_small_or_imprecise_dtypes=True),
          writes=[("onescolB",)])
    au.op("gpsimd", lambda g: g.iota(onesrowF[:, :], pattern=[[0, 128]], base=1,
                                     channel_multiplier=0, allow_small_or_imprecise_dtypes=True),
          writes=[("onesrowF",)])
    au.op("gpsimd", lambda g: g.dma_start(out=gwsb[:, :], in_=gw_dram[:, :]),
          writes=[("gwsb",)], dma=True, stream="p1gw")


    with ExitStack() as psA:
        tpg = ctx.psum("tpg", [128, 512], F32, stack=psA)
        scp = ctx.psum("scp", [128, 512], F32, stack=psA)
        tpp = [ctx.psum("tppA", [128, 512], F32, stack=psA),
               ctx.psum("tppB", [128, 512], F32, stack=psA)]
        pep = ctx.psum("pep", [128, 512], F32, stack=psA)
        aux1 = ctx.psum("aux1", [128, 512], F32, stack=psA)
        aux2 = ctx.psum("aux2", [128, 512], F32, stack=psA)
        aux3 = ctx.psum("aux3", [128, 512], F32, stack=psA)

        def gwtfn(t):
            for d in range(ND):
                i = t.transpose(tpg[:, d * 8:(d + 1) * 8], gwsb[:, d * 128:(d + 1) * 128],
                                identF[0:8, 0:8])
            return i
        au.op("tensor", gwtfn, reads=[("gwsb",), ("identF",)], writes=[("tpg",)])

        def gwevf(v):
            for d in range(ND):
                i = v.tensor_copy(gwT[:, d, :], tpg[:, d * 8:(d + 1) * 8])
            return i
        au.op("vector", gwevf, reads=[("tpg",)], writes=[("gwT",)])

        for c in range(NT):
            b = c % 2
            au.op("gpsimd", lambda g, c=c, b=b: g.dma_start(
                out=r1t[:, b, :], in_=r1_dram[c * 128:(c + 1) * 128, :]),
                writes=[("r1t", b)], dma=True, stream=f"p1r1{b}")
            for dp in range(ND // 2):
                def tpfn(t, b=b, dp=dp):
                    t.transpose(tpp[b][:, 0:128], r1t[:, b, (2 * dp) * 128:(2 * dp + 1) * 128],
                                identF[:, :])
                    return t.transpose(tpp[b][:, 128:256],
                                       r1t[:, b, (2 * dp + 1) * 128:(2 * dp + 2) * 128],
                                       identF[:, :])
                au.op("tensor", tpfn, reads=[("r1t", b), ("identF",)], writes=[("tpp", b)])

                def tpev(v, b=b, dp=dp):
                    v.tensor_copy(rtT[:, b, 2 * dp, :], tpp[b][:, 0:128])
                    return v.tensor_copy(rtT[:, b, 2 * dp + 1, :], tpp[b][:, 128:256])
                au.op("vector", tpev, reads=[("tpp", b)], writes=[("rtT", b, dp)])
            rT = [("rtT", b, dp) for dp in range(ND // 2)]

            def scfn(t, b=b):
                for d in range(ND):
                    i = t.matmul(scp[:, 0:NE], rtT[:, b, d, :], gwT[:, d, :],
                                 start=(d == 0), stop=(d == ND - 1))
                return i
            au.op("tensor", scfn, reads=rT + [("gwT",)], writes=[("scp",)])
            au.op("vector", lambda v, b=b: v.reduce_max(smx[:, b, 0:1], scp[:, 0:NE], axis=AX.X),
                  xreads=[("scp",)], writes=[("smx", b, 0)])
            au.op("vector", lambda v, b=b: v.tensor_scalar(smx[:, b, 1:2], smx[:, b, 0:1],
                                                           -1.0, None, OP.mult),
                  reads=[("smx", b, 0)], writes=[("smx", b, 1)])
            au.op("scalar", lambda s, b=b: s.activation(scs[:, b, :], scp[:, 0:NE], AF.Exp,
                                                        bias=smx[:, b, 1:2], scale=1.0),
                  reads=[("smx", b, 1)], xreads=[("scp",)], writes=[("scs", b)])
            au.op("vector", lambda v, b=b: v.reduce_sum(smx[:, b, 2:3], scs[:, b, :], axis=AX.X),
                  reads=[("scs", b)], writes=[("smx", b, 2)])
            au.op("vector", lambda v, b=b: v.reciprocal(smx[:, b, 3:4], smx[:, b, 2:3]),
                  reads=[("smx", b, 2)], writes=[("smx", b, 3)])
            au.op("vector", lambda v, c=c, b=b: v.tensor_scalar(probs[:, c, :], scs[:, b, :],
                                                                smx[:, b, 3:4], None, OP.mult),
                  reads=[("scs", b), ("smx", b, 3)], writes=[("probs", c)])
            au.op("tensor", lambda t, c=c: t.transpose(pep[0:8, 0:128], probs[:, c, :],
                                                       identF[:, :]),
                  reads=[("probs", c), ("identF",)], writes=[("pep",)])
            au.op("vector", lambda v, c=c: v.tensor_copy(probsE[0:8, c * 128:(c + 1) * 128],
                                                         pep[0:8, 0:128]),
                  reads=[("pep",)], writes=[("probsE", c)])

        PRE = [("probsE", c) for c in range(NT)]
        au.op("gpsimd", lambda g: g.iota(bcols[:, 0:1], pattern=[[0, 1]], base=0,
                                         channel_multiplier=1,
                                         allow_small_or_imprecise_dtypes=True),
              writes=[("bc", 0)])
        au.op("gpsimd", lambda g: g.iota(iot8[:, :], pattern=[[-16, 8]], base=0,
                                         channel_multiplier=1,
                                         allow_small_or_imprecise_dtypes=True),
              writes=[("iot8",)])
        au.op("gpsimd", lambda g: g.iota(srt8[:, :], pattern=[[1, 128]], base=0,
                                         channel_multiplier=-16,
                                         allow_small_or_imprecise_dtypes=True),
              writes=[("srt8",)])
        au.op("vector", lambda v: v.tensor_scalar(sel2[:, :], iot8[:, :], 0.0, None, OP.is_ge),
              reads=[("iot8",)], writes=[("sel2",)])
        au.op("vector", lambda v: v.tensor_scalar(iot8[:, :], iot8[:, :], 16.0, 0.0,
                                                  OP.is_ge, OP.add,
                                                  accum_out=bcols[:, 1:2]),
              reads=[("iot8",)], writes=[("iot8",), ("bc", 1)])
        au.op("vector", lambda v: v.tensor_sub(sel2[:, :], sel2[:, :], iot8[:, :]),
              reads=[("sel2",), ("iot8",)], writes=[("sel2",)])
        au.op("vector", lambda v: v.scalar_tensor_tensor(bcols[:, 2:3], bcols[:, 1:2], -16.0,
                                                         bcols[:, 0:1], OP.mult, OP.add),
              reads=[("bc", 1), ("bc", 0)], writes=[("bc", 2)])
        au.op("vector", lambda v: v.tensor_scalar(selrepT[:, :], srt8[:, :], 0.0, None, OP.is_ge),
              reads=[("srt8",)], writes=[("selrepT",)])
        au.op("vector", lambda v: v.tensor_scalar(srt8[:, :], srt8[:, :], 16.0, None, OP.is_ge),
              reads=[("srt8",)], writes=[("srt8",)])
        au.op("vector", lambda v: v.tensor_sub(selrepT[:, :], selrepT[:, :], srt8[:, :]),
              reads=[("selrepT",)] + [("srt8",)], writes=[("selrepT",)])
        for cc in range(8):
            ax = aux1 if cc % 2 == 0 else aux2
            axk = ("aux1",) if cc % 2 == 0 else ("aux2",)
            au.op("tensor", lambda t, cc=cc, ax=ax: t.matmul(
                ax[:, :], selrepT[0:8, :], probsE[0:8, cc * 512:(cc + 1) * 512],
                start=True, stop=True),
                reads=[("selrepT",)] + PRE, writes=[axk])
            au.op("vector", lambda v, cc=cc, ax=ax: v.tensor_copy(
                rep[:, cc * 512:(cc + 1) * 512], ax[:, :]),
                reads=[axk], writes=[("rep", cc)])
        REPK = [("rep", cc) for cc in range(8)]
        au.op("gpsimd", lambda g: g.iota(shtb[:, :, :], pattern=[[16, 8], [1, 16]], base=0,
                                         channel_multiplier=-1,
                                         allow_small_or_imprecise_dtypes=True),
              writes=[("shtb",)])
        au.op("vector", lambda v: v.tensor_scalar(shtb[:, :, :], shtb[:, :, :], -1.0, None,
                                                  OP.is_equal),
              reads=[("shtb",)], writes=[("shtb",)])
        au.op("vector", lambda v: v.memset(shtb[:, :, 15:16], 0.0),
              reads=[("shtb",)], writes=[("shtb",)])
        au.op("vector", lambda v: v.memset(bcols[:, 3:4], 0.0), writes=[("bc", 3)])
        au.op("vector", lambda v: v.memset(bcols[:, 4:5], 1.0 / 16.0), writes=[("bc", 4)])
        shtf = shtb.ap().rearrange("p a b -> p (a b)")
        for it in range(7):
            au.op("scalar", lambda s: s.activation(bcols[:, 6:7], bcols[:, 2:3], AF.Identity,
                                                   bias=bcols[:, 3:4], scale=bcols[:, 4:5]),
                  reads=[("bc", 2), ("bc", 3), ("bc", 4)], writes=[("bc", 6)])
            au.op("vector", lambda v: v.tensor_scalar(scr[:, :], rep[:, :], bcols[:, 6:7],
                                                      0.0, OP.is_ge, OP.add,
                                                      accum_out=bcols[:, 7:8]),
                  reads=REPK + [("bc", 6)], writes=[("scr",), ("bc", 7)])
            au.op("vector", lambda v: v.tensor_scalar(bcols[:, 8:9], bcols[:, 7:8], float(KCAP),
                                                      None, OP.is_ge),
                  reads=[("bc", 7)], writes=[("bc", 8)])
            au.op("vector", lambda v: v.tensor_scalar(predb[:, :], bcols[:, 7:8], float(KCAP),
                                                      None, OP.is_ge),
                  reads=[("bc", 7)], writes=[("predb",)])
            au.op("tensor", lambda t: t.matmul(aux3[:, 0:1], shtf, predb[:, :],
                                               start=True, stop=True),
                  reads=[("shtb",), ("predb",)], writes=[("aux3",)])
            au.op("vector", lambda v: v.tensor_copy(bcols[:, 5:6], aux3[:, 0:1]),
                  reads=[("aux3",)], writes=[("bc", 5)])
            au.op("vector", lambda v: v.tensor_sub(bcols[:, 1:2], bcols[:, 8:9], bcols[:, 5:6]),
                  reads=[("bc", 8), ("bc", 5)], writes=[("bc", 1)])
            au.op("vector", lambda v: v.tensor_mul(bcols[:, 9:10], bcols[:, 1:2], bcols[:, 6:7]),
                  reads=[("bc", 1), ("bc", 6)], writes=[("bc", 9)])
            au.op("tensor", lambda t: t.matmul(aux1[0:8, 0:1], sel2[:, :], bcols[:, 9:10],
                                               start=True, stop=True),
                  reads=[("sel2",), ("bc", 9)], writes=[("aux1",)])
            au.op("vector", lambda v: v.tensor_copy(bsb[:, :], aux1[0:8, 0:1]),
                  reads=[("aux1",)], writes=[("bsb",)])
            au.op("tensor", lambda t: t.matmul(aux2[:, 0:1], selrepT[0:8, :], bsb[0:8, :],
                                               start=True, stop=True),
                  reads=[("selrepT",), ("bsb",)], writes=[("aux2",)])
            au.op("vector", lambda v: v.tensor_copy(bcols[:, 3:4], aux2[:, 0:1]),
                  reads=[("aux2",)], writes=[("bc", 3)])
            au.op("vector", lambda v: v.tensor_scalar(bcols[:, 4:5], bcols[:, 4:5], 1.0 / 16.0,
                                                      None, OP.mult),
                  reads=[("bc", 4)], writes=[("bc", 4)])
        au.op("tensor", lambda t: t.matmul(aux3[0:8, 0:1], sel2[:, :], bcols[:, 3:4],
                                           start=True, stop=True),
              reads=[("sel2",), ("bc", 3)], writes=[("aux3",)])
        au.op("vector", lambda v: v.tensor_scalar(tau[:, 0:1], aux3[0:8, 0:1], 1.0 / 16.0,
                                                  None, OP.mult),
              reads=[("aux3",)], writes=[("tau", 0)])
        au.op("tensor", lambda t: t.transpose(pep[0:1, 0:8], tau[0:8, 0:1], identF[0:8, 0:8]),
              reads=[("tau", 0), ("identF",)], writes=[("pep",)])
        au.op("vector", lambda v: v.tensor_copy(taurow[:, :], pep[0:1, 0:8]),
              reads=[("pep",)], writes=[("taurow",)])
        au.op("tensor", lambda t: t.matmul(aux1[:, 0:8], onesrowF[:, :], taurow[:, :],
                                           start=True, stop=True),
              reads=[("onesrowF",), ("taurow",)], writes=[("aux1",)])
        au.op("vector", lambda v: v.tensor_copy(taubc[:, :], aux1[:, 0:8]),
              reads=[("aux1",)], writes=[("taubc",)])

        for c in range(NT):
            b = c % 2
            au.op("vector", lambda v, c=c, b=b: v.tensor_tensor(maskf[:, b, :], probs[:, c, :],
                                                                taubc[:, :], OP.is_ge),
                  reads=[("probs", c), ("taubc",)], writes=[("maskf", b)])
            au.op("vector", lambda v, c=c, b=b: v.tensor_mul(mprob[:, c, :], probs[:, c, :],
                                                             maskf[:, b, :]),
                  reads=[("probs", c), ("maskf", b)], writes=[("mprob", c)])
            au.op("vector", lambda v, c=c, b=b: v.tensor_copy(masksA[:, c, :], maskf[:, b, :]),
                  reads=[("maskf", b)], writes=[("masksA", c)])
            au.op("tensor", lambda t, c=c: t.matmul(aux2[0:1, 0:8], onescolB[:, :], masksA[:, c, :],
                                                    start=True, stop=True),
                  reads=[("masksA", c), ("onescolB",)], writes=[("aux2",)])
            au.op("vector", lambda v, c=c: v.tensor_copy(cnts[0:1, c * NE:(c + 1) * NE],
                                                         aux2[0:1, 0:8]),
                  reads=[("aux2",)], writes=[("cnts", c)])
        CN = [("cnts", c) for c in range(NT)]
        au.op("vector", lambda v: v.tensor_copy(offs[:, 0, :], cnts[:, :]),
              reads=CN, writes=[("offs", 0)])
        par = 0
        for sh in (8, 16, 32, 64, 128):
            def shfn(v, sh=sh, par=par):
                v.tensor_copy(offs[:, 1 - par, 0:sh], offs[:, par, 0:sh])
                return v.tensor_add(offs[:, 1 - par, sh:NT * NE], offs[:, par, sh:NT * NE],
                                    offs[:, par, 0:NT * NE - sh])
            au.op("vector", shfn, reads=[("offs", par)], writes=[("offs", 1 - par)])
            par = 1 - par
        au.op("vector", lambda v, par=par: v.tensor_sub(offs_ex[:, :], offs[:, par, :], cnts[:, :]),
              reads=[("offs", par)] + CN, writes=[("offs_ex",)])

        for c in range(NT):
            b = c % 2

            def posfn(t, c=c):
                t.matmul(aux3[:, 0:8], Umat[:, :], masksA[:, c, :], start=True, stop=False)
                return t.matmul(aux3[:, 0:8], onesrowF[:, :], offs_ex[0:1, c * NE:(c + 1) * NE],
                                start=False, stop=True)
            au.op("tensor", posfn, reads=[("masksA", c), ("Umat",), ("offs_ex",), ("onesrowF",)],
                  writes=[("aux3",)])
            au.op("vector", lambda v, b=b: v.tensor_copy(post[:, b, :], aux3[:, 0:8]),
                  reads=[("aux3",)], writes=[("post", b)])
            au.op("vector", lambda v, c=c, b=b: v.scalar_tensor_tensor(
                idxf[:, b, :], masksA[:, c, :], -BIGIDX, post[:, b, :], OP.mult, OP.add),
                reads=[("masksA", c), ("post", b)], writes=[("idxf", b)])
            au.op("vector", lambda v, b=b: v.tensor_scalar(idxf[:, b, :], idxf[:, b, :],
                                                           BIGIDX - 1.0, None, OP.add),
                  reads=[("idxf", b)], writes=[("idxf", b)])
            au.op("vector", lambda v, c=c, b=b: v.tensor_copy(idxu[:, c, :], idxf[:, b, :]),
                  reads=[("idxf", b)], writes=[("idxu", c)])
            for e in range(NE):
                au.op("gpsimd", lambda g, c=c, b=b, e=e: g.indirect_dma_start(
                    out=cx_dram.rearrange("e k d -> (e k) d"),
                    out_offset=IndirectOffsetOnAxis(ap=idxu[:, c, e:e + 1], axis=0),
                    in_=r1t[:, b, :], in_offset=None, element_offset=e * KCAP * D,
                    bounds_check=_breg(ctx, g), oob_is_err=False),
                    reads=[("idxu", c), ("r1t", b)], writes=[("cxd", e, c)],
                    dma=True, stream=f"p1sc{b}{e}")
    au.fence()
    _p1s.close()
    return probs, mprob, idxu


def emit_p1_experts(ctx, au, w1_dram, w2_dram, cx_dram, y_dram, r1_dram, idxu):
    from concourse.bass import IndirectOffsetOnAxis
    identB = ctx.identB
    r1b = ctx.sbuf("r1b", [128, NT, D], BF16)
    cxsb = ctx.sbuf("cxsb", [128, 1, 8, KCAP], BF16)
    cxT = ctx.sbuf("cxT", [128, ND, KCAP], BF16)
    hsb = ctx.sbuf("hsb", [128, 32, KCAP], BF16)
    w1c = ctx.sbuf("w1c", [128, 2, ND, 512], BF16)
    w2c = ctx.sbuf("w2c", [128, 2, 2, D], BF16)
    ysb = ctx.sbuf("ysb", [128, 4, D], F32)
    au.op("gpsimd", lambda g: g.dma_start(
        out=r1b[:, :, :], in_=r1_dram.rearrange("(c p) d -> p c d", p=128)),
        writes=[("r1b",)], dma=True, stream="p1r1b")

    def emit_scatters(e):
        for c in range(NT):
            au.op("gpsimd", lambda g, c=c, e=e: g.indirect_dma_start(
                out=cx_dram.rearrange("e k d -> (e k) d"),
                out_offset=IndirectOffsetOnAxis(ap=idxu[:, c, e:e + 1], axis=0),
                in_=r1b[:, c, :], in_offset=None, element_offset=e * KCAP * D,
                bounds_check=_breg(ctx, g), oob_is_err=False),
                reads=[("idxu", c), ("r1b",)], writes=[("cxd", e, c % 4)],
                dma=True, stream=f"p1sc{e % 2}")

    CXD = lambda e: [("cxd", e, j) for j in range(4)]
    emit_scatters(0)
    for e in range(NE):
        with ExitStack() as psE:
            ctp = [ctx.psum(f"ctpA{e}", [128, 1024], BF16, stack=psE),
                   ctx.psum(f"ctpB{e}", [128, 1024], BF16, stack=psE)]
            hps = [ctx.psum(f"hpsA{e}", [128, 2, 512], F32, stack=psE),
                   ctx.psum(f"hpsB{e}", [128, 2, 512], F32, stack=psE)]
            if e + 1 < NE:
                emit_scatters(e + 1)
            for d in range(ND):
                au.op("sync", lambda sy, e=e, d=d: sy.dma_start(
                    out=cxT[:, d, :], in_=cx_dram[e][:, d * 128:(d + 1) * 128], transpose=True),
                    reads=CXD(e) + ([("yev", e - 1)] if e > 0 else []),
                    writes=[("cxT", d)], dma=True, stream="p1ctr")
            CXT = [("cxT", d) for d in range(ND)]
            for fc in range(8):
                wb = fc % 2
                au.op("gpsimd", lambda g, e=e, fc=fc, wb=wb: g.dma_start(
                    out=w1c[:, wb, :, :],
                    in_=w1_dram[e].rearrange("(dt p) f -> p dt f", p=128)[:, :,
                                                                         fc * 512:(fc + 1) * 512]),
                    writes=[("w1c", wb)], dma=True, stream=f"p1w1{wb}")
                for fcc in range(4):
                    hb = fcc % 2

                    def hmm(t, fc=fc, fcc=fcc, wb=wb, hb=hb):
                        for d in range(ND):
                            for tc in range(2):
                                i = t.matmul(hps[hb][:, tc, :],
                                             w1c[:, wb, d, fcc * 128:(fcc + 1) * 128],
                                             cxT[:, d, tc * 512:(tc + 1) * 512],
                                             start=(d == 0), stop=(d == ND - 1))
                        return i
                    au.op("tensor", hmm, reads=CXT + [("w1c", wb)], writes=[("hps", hb)])
                    ft = fc * 4 + fcc
                    au.op("scalar", lambda s, ft=ft, hb=hb: s.activation(
                        hsb[:, ft, :], hps[hb].ap().rearrange("p a b -> p (a b)"), AF.Gelu),
                        xreads=[("hps", hb)], writes=[("hsb", ft)])
            HS = [("hsb", ft) for ft in range(32)]
        with ExitStack() as psY:
            yps = [ctx.psum(f"yps{e}_{j}", [128, 2, 512], F32, stack=psY) for j in range(4)]
            for half in range(2):
                for fp_ in range(16):
                    wb = fp_ % 2
                    au.op("gpsimd", lambda g, e=e, fp_=fp_, wb=wb: g.dma_start(
                        out=w2c[:, wb, :, :],
                        in_=w2_dram[e, fp_ * 256:(fp_ + 1) * 256, :].rearrange(
                            "(a p) d -> p a d", p=128)),
                        writes=[("w2c", wb)], dma=True, stream=f"p1w2{wb}")
                    for fq in range(2):
                        ft = fp_ * 2 + fq

                        def ymm(t, half=half, ft=ft, wb=wb, fq=fq):
                            for j in range(4):
                                tj = half * 4 + j
                                for dc in range(2):
                                    i = t.matmul(yps[j][:, dc, :],
                                                 hsb[:, ft, tj * 128:(tj + 1) * 128],
                                                 w2c[:, wb, fq, dc * 512:(dc + 1) * 512],
                                                 start=(ft == 0), stop=(ft == 31))
                            return i
                        au.op("tensor", ymm,
                              reads=[("hsb", ft), ("w2c", wb)] + (HS if ft == 0 else []),
                              writes=[("yps",)] if (ft == 0 or ft == 31) else [("yacc",)])
                for j in range(4):
                    tj = half * 4 + j
                    au.op("vector", lambda v, j=j: v.tensor_copy(
                        ysb[:, j, :], yps[j].ap().rearrange("p a b -> p (a b)")),
                        xreads=[("yps",)], writes=[("ysbuf", j), ("yev", e)])
                for j in range(4):
                    tj = half * 4 + j
                    au.op("sync", lambda sy, e=e, tj=tj, j=j: sy.dma_start(
                        out=y_dram[e, tj * 128:(tj + 1) * 128, :], in_=ysb[:, j, :]),
                        reads=[("ysbuf", j)], writes=[("yd", e, tj)], dma=True,
                        stream=f"p1y{j}")


def emit_p1_tail(ctx, au, r1_dram, y_dram, r2_dram, mprob, idxu, after_tile=None):
    from concourse.bass import IndirectOffsetOnAxis
    rtile = ctx.sbuf("p1rt", [128, 2, D], F32)
    acc2 = ctx.sbuf("p1acc2", [128, 2, D], F32)
    tmpg = ctx.sbuf("tmpg", [128, 4, D], F32)
    sqs = ctx.sbuf("p1sqs", [128, 2, D], BF16)
    mu = ctx.sbuf("p1mu", [128, 2], F32)
    ssq = ctx.sbuf("p1ssq", [128, 2], F32)
    rstd = ctx.sbuf("p1rstd", [128, 2], F32)
    mb = ctx.sbuf("p1mb", [128, 2], F32)
    lnbufs = [(sqs[:, b, :], mu[:, b:b + 1], ssq[:, b:b + 1], rstd[:, b:b + 1],
               mb[:, b:b + 1]) for b in range(2)]
    au.op("vector", lambda v: v.memset(tmpg[:, :, :], 0.0),
          writes=[("tmpg", 0), ("tmpg", 1), ("tmpg", 2), ("tmpg", 3)])
    YD = [("yd", e, tj) for e in range(NE) for tj in range(8)]
    for c in range(NT):
        b = c % 2
        au.op("gpsimd", lambda g, c=c, b=b: g.dma_start(
            out=rtile[:, b, :], in_=r1_dram[c * 128:(c + 1) * 128, :]),
            writes=[("p1rt", b), ("rk", b, 0), ("rk", b, 1), ("rk", b, 2), ("rk", b, 3)],
            dma=True, stream=f"p1f{b}")
        for e in range(NE):
            ge = e % 4
            au.op("gpsimd", lambda g, c=c, e=e, ge=ge: g.indirect_dma_start(
                out=tmpg[:, ge, :], out_offset=None,
                in_=y_dram.rearrange("e k d -> (e k) d"),
                in_offset=IndirectOffsetOnAxis(ap=idxu[:, c, e:e + 1], axis=0),
                element_offset=e * KCAP * D,
                bounds_check=_breg(ctx, g), oob_is_err=False),
                reads=[("idxu", c)] + (YD if c == 0 and e == 0 else [("yd", e, tj) for tj in range(8)]),
                writes=[("tmpg", ge)], dma=True, stream=f"p1g{e}")
            au.op("vector", lambda v, c=c, e=e, ge=ge, b=b: v.scalar_tensor_tensor(
                rtile[:, b, :], tmpg[:, ge, :], mprob[:, c, e:e + 1], rtile[:, b, :],
                OP.mult, OP.add),
                reads=[("tmpg", ge), ("mprob", c), ("p1rt", b), ("rk", b, 0), ("rk", b, 1),
                       ("rk", b, 2), ("rk", b, 3)],
                writes=[("rk", b, 0), ("rk", b, 1), ("rk", b, 2), ("rk", b, 3)])
        rkeys = [("rk", b, j) for j in range(4)]
        emit_ln_inplace(ctx, au, rtile[:, b, :], rkeys, lnbufs[b], tag=10 + b)
        au.op("sync", lambda sy, c=c, b=b: sy.dma_start(
            out=r2_dram[c * 128:(c + 1) * 128, :], in_=rtile[:, b, :]),
            reads=rkeys, writes=[("r2d", c)], dma=True, stream=f"p1r2{b}")
        if after_tile is not None:
            after_tile(c, rtile[:, b, :], rkeys)
    return [("r2d", c) for c in range(NT)]


def emit_p2(ctx, au, r2_dram, wq_dram, wk_dram, wv_dram, wo_dram, bo_dram,
            ct_dram, st_dram, qtm_dram, ktm_dram, vtm_dram, r3_dram, tail_hook_reg=None):
    nc = ctx.nc
    identB, identF = ctx.identB, ctx.identF

    wosb = ctx.sbuf("wosb", [128, ND, D], BF16)
    r2t = ctx.sbuf("r2t", [128, 2, D], F32)
    nmhat = ctx.sbuf("nmhat", [128, 1], F32)
    bobc = ctx.sbuf("bobc", [128, D], F32)
    tob = ctx.sbuf("tob", [128, 2, D], F32)
    sqs2 = ctx.sbuf("p2sqs", [128, 2, D], BF16)
    mu2 = ctx.sbuf("p2mu", [128, 2], F32)
    ssq2 = ctx.sbuf("p2ssq", [128, 2], F32)
    rstd2 = ctx.sbuf("p2rstd", [128, 2], F32)
    mb2 = ctx.sbuf("p2mb", [128, 2], F32)
    lnb2 = [(sqs2[:, b, :], mu2[:, b:b + 1], ssq2[:, b:b + 1], rstd2[:, b:b + 1],
             mb2[:, b:b + 1]) for b in range(2)]
    onesrF = ctx.sbuf("onesrF", [1, 128], F32)
    onescB2 = ctx.sbuf("onescB2", [128, 1], BF16)
    onesrB = ctx.sbuf("onesrB", [1, 128], BF16)
    sbA = ExitStack()
    wqsb = ctx.sbuf("wqsb", [128, ND, D], BF16, stack=sbA)
    wksb = ctx.sbuf("wksb", [128, ND, D], BF16, stack=sbA)
    wvsb = ctx.sbuf("wvsb", [128, ND, D], BF16, stack=sbA)
    rtTb = ctx.sbuf("rtTb", [128, 2, ND, 128], BF16, stack=sbA)
    ctab = ctx.sbuf("ctab", [128, 2, 512], BF16, stack=sbA)
    stab = ctx.sbuf("stab", [128, 2, 512], BF16, stack=sbA)
    rsc1 = ctx.sbuf("rsc1", [128, 512], BF16, stack=sbA)
    rsc2 = ctx.sbuf("rsc2", [128, 512], BF16, stack=sbA)
    qkraw = ctx.sbuf("qkraw", [128, D], BF16, stack=sbA)
    qtt = ctx.sbuf("qtt", [128, 2, D], BF16, stack=sbA)
    ktt = ctx.sbuf("ktt", [128, 2, D], BF16, stack=sbA)
    vtt = ctx.sbuf("vtt", [128, 2, D], BF16, stack=sbA)
    sqsc = ctx.sbuf("sqsc", [128, D], BF16, stack=sbA)
    nrm = ctx.sbuf("nrm", [128, 4], F32, stack=sbA)
    sc11 = ctx.sbuf("sc11", [1, 8], F32, stack=sbA)
    borow = ctx.sbuf("borow", [1, D], F32, stack=sbA)

    au.op("gpsimd", lambda g: g.iota(onesrF[:, :], pattern=[[0, 128]], base=1,
                                     channel_multiplier=0, allow_small_or_imprecise_dtypes=True),
          writes=[("onesrF",)])
    au.op("gpsimd", lambda g: g.iota(onescB2[:, :], pattern=[[0, 1]], base=1,
                                     channel_multiplier=0, allow_small_or_imprecise_dtypes=True),
          writes=[("onescB2",)])
    au.op("gpsimd", lambda g: g.iota(onesrB[:, :], pattern=[[0, 128]], base=1,
                                     channel_multiplier=0, allow_small_or_imprecise_dtypes=True),
          writes=[("onesrB",)])
    for nm, dst, key in (("wq", wqsb, "wqsb"), ("wk", wksb, "wksb"),
                         ("wv", wvsb, "wvsb"), ("wo", wosb, "wosb")):
        srcd = {"wq": wq_dram, "wk": wk_dram, "wv": wv_dram, "wo": wo_dram}[nm]
        au.op("gpsimd", lambda g, srcd=srcd, dst=dst: g.dma_start(
            out=dst[:, :, :], in_=srcd.rearrange("(dt p) c -> p dt c", p=128)),
            writes=[(key,)], dma=True, stream=f"p2w{nm}")
    au.op("gpsimd", lambda g: g.dma_start(out=borow[:, :], in_=bo_dram[None, :]),
          writes=[("borow",)], dma=True, stream="p2bo")
    au.op("vector", lambda v: v.memset(nrm[:, 1:3], 0.0), writes=[("nrm", 1), ("nrm", 2)])

    psA = ExitStack()
    tpa = ctx.psum("p2tpa", [128, 512], F32, stack=psA)
    qpp = ctx.psum("qpp", [128, 2, 512], F32, stack=psA)
    kpp = ctx.psum("kpp", [128, 2, 512], F32, stack=psA)
    vpp = ctx.psum("vpp", [128, 2, 512], F32, stack=psA)
    aux = ctx.psum("p2aux", [128, 512], F32, stack=psA)

    au.op("tensor", lambda t: t.matmul(aux[:, 0:512], onesrF[:, :], borow[0:1, 0:512],
                                       start=True, stop=True),
          reads=[("onesrF",), ("borow",)], writes=[("p2aux",)])
    au.op("vector", lambda v: v.tensor_copy(bobc[:, 0:512], aux[:, 0:512]),
          reads=[("p2aux",)], writes=[("bobc", 0)])
    au.op("tensor", lambda t: t.matmul(aux[:, 0:512], onesrF[:, :], borow[0:1, 512:1024],
                                       start=True, stop=True),
          reads=[("onesrF",), ("borow",)], writes=[("p2aux",)])
    au.op("vector", lambda v: v.tensor_copy(bobc[:, 512:1024], aux[:, 0:512]),
          reads=[("p2aux",)], writes=[("bobc", 1)])

    def tileA(c, src_ap, src_keys):
        b = c % 2
        au.op("gpsimd", lambda g, c=c, b=b: g.dma_start(
            out=ctab[:, b, :], in_=ct_dram[c * 128:(c + 1) * 128, :]),
            writes=[("ctab", b)], dma=True, stream=f"p2ct{b}")
        au.op("gpsimd", lambda g, c=c, b=b: g.dma_start(
            out=stab[:, b, :], in_=st_dram[c * 128:(c + 1) * 128, :]),
            writes=[("stab", b)], dma=True, stream=f"p2st{b}")
        for dp in range(ND // 2):
            def tpfn(t, b=b, dp=dp, src_ap=src_ap):
                t.transpose(tpa[:, 0:128], src_ap[:, (2 * dp) * 128:(2 * dp + 1) * 128],
                            identF[:, :])
                return t.transpose(tpa[:, 128:256],
                                   src_ap[:, (2 * dp + 1) * 128:(2 * dp + 2) * 128],
                                   identF[:, :])
            au.op("tensor", tpfn, reads=list(src_keys) + [("identF",)], writes=[("p2tpa",)])

            def tpev(v, b=b, dp=dp):
                v.tensor_copy(rtTb[:, b, 2 * dp, :], tpa[:, 0:128])
                return v.tensor_copy(rtTb[:, b, 2 * dp + 1, :], tpa[:, 128:256])
            au.op("vector", tpev, reads=[("p2tpa",)], writes=[("rtTb", b, dp)])
        rT = [("rtTb", b, dp) for dp in range(ND // 2)]
        for nmW, wsb, wkey, pp, ppk in (("q", wqsb, "wqsb", qpp, "qpp"),
                                        ("k", wksb, "wksb", kpp, "kpp"),
                                        ("v", wvsb, "wvsb", vpp, "vpp")):
            def pfn(t, b=b, wsb=wsb, pp=pp):
                for d in range(ND):
                    for qc in range(2):
                        i = t.matmul(pp[:, qc, :], rtTb[:, b, d, :],
                                     wsb[:, d, qc * 512:(qc + 1) * 512],
                                     start=(d == 0), stop=(d == ND - 1))
                return i
            au.op("tensor", pfn, reads=rT + [(wkey,)], writes=[(ppk,)])
        au.op("scalar", lambda s, b=b: s.activation(
            vtt[:, b, :], vpp.ap().rearrange("p a x -> p (a x)"), AF.Identity),
            xreads=[("vpp",)], writes=[("vtt", b)])
        au.op("sync", lambda sy, c=c, b=b: sy.dma_start(
            out=vtm_dram[c * 128:(c + 1) * 128, :], in_=vtt[:, b, :]),
            reads=[("vtt", b)], writes=[("vtmd", c)], dma=True, stream=f"p2vs{b}")
        for nmW, pp, ppk, ttt, tk, sqk, macc in (
                ("q", qpp, "qpp", qtt, "qtt", 0, 1), ("k", kpp, "kpp", ktt, "ktt", 0, 2)):
            au.op("vector", lambda v, pp=pp: v.tensor_copy(
                qkraw[:, :], pp.ap().rearrange("p a x -> p (a x)")),
                xreads=[(ppk,)], writes=[("qkraw",)])
            qkf = qkraw.ap()
            x1 = qkf[:, 0:1024:2]
            x2 = qkf[:, 1:1024:2]
            au.op("vector", lambda v, x1=x1, b=b: v.tensor_mul(rsc1[:, :], x1, ctab[:, b, :]),
                  reads=[("ctab", b), ("qkraw",)], writes=[("rsc1",)])
            au.op("vector", lambda v, x2=x2, b=b: v.tensor_mul(rsc2[:, :], x2, stab[:, b, :]),
                  reads=[("stab", b), ("qkraw",)], writes=[("rsc2",)])
            au.op("vector", lambda v, ttt=ttt, b=b: v.tensor_sub(
                ttt[:, b, 0:1024:2], rsc1[:, :], rsc2[:, :]),
                reads=[("rsc1",), ("rsc2",)], writes=[(tk, b, 0)])
            au.op("vector", lambda v, x1=x1, b=b: v.tensor_mul(rsc1[:, :], x1, stab[:, b, :]),
                  reads=[("stab", b), ("qkraw",)], writes=[("rsc1",)])
            au.op("vector", lambda v, x2=x2, b=b: v.tensor_mul(rsc2[:, :], x2, ctab[:, b, :]),
                  reads=[("ctab", b), ("qkraw",)], writes=[("rsc2",)])
            au.op("vector", lambda v, ttt=ttt, b=b: v.tensor_add(
                ttt[:, b, 1:1024:2], rsc1[:, :], rsc2[:, :]),
                reads=[("rsc1",), ("rsc2",)], writes=[(tk, b, 1)])
            au.op("scalar", lambda s, ttt=ttt, b=b: s.activation(
                sqsc[:, :], ttt[:, b, :], AF.Square, accum_out=nrm[:, 0:1]),
                reads=[(tk, b, 0), (tk, b, 1)], writes=[("nrm", 0), ("sqsc",)])
            au.op("vector", lambda v, macc=macc: v.tensor_max(
                nrm[:, macc:macc + 1], nrm[:, macc:macc + 1], nrm[:, 0:1]),
                reads=[("nrm", 0), ("nrm", macc)], writes=[("nrm", macc)])
            dstd = qtm_dram if tk == "qtt" else ktm_dram
            au.op("sync", lambda sy, c=c, b=b, ttt=ttt, dstd=dstd: sy.dma_start(
                out=dstd[c * 128:(c + 1) * 128, :], in_=ttt[:, b, :]),
                reads=[(tk, b, 0), (tk, b, 1)], writes=[(tk + "d", c)], dma=True,
                stream=f"p2{tk}s{b}")

    if tail_hook_reg is not None:
        tail_hook_reg(tileA)
    else:
        for c in range(NT):
            b = c % 2
            au.op("gpsimd", lambda g, c=c, b=b: g.dma_start(
                out=r2t[:, b, :], in_=r2_dram[c * 128:(c + 1) * 128, :]),
                reads=[("r2d", c)], writes=[("r2t", b)], dma=True, stream=f"p2r{b}")
            tileA(c, r2t[:, b, :], [("r2t", b)])

    def _cont():
        def finishA():
            au.op("tensor", lambda t: t.transpose(aux[0:1, 0:128], nrm[:, 1:2], identF[:, :]),
                  reads=[("nrm", 1), ("identF",)], writes=[("p2aux",)])
            au.op("vector", lambda v: v.reduce_max(sc11[0:1, 0:1], aux[0:1, 0:128], axis=AX.X),
                  xreads=[("p2aux",)], writes=[("sc11", 0)])
            au.op("tensor", lambda t: t.transpose(aux[0:1, 0:128], nrm[:, 2:3], identF[:, :]),
                  reads=[("nrm", 2), ("identF",)], writes=[("p2aux",)])
            au.op("vector", lambda v: v.reduce_max(sc11[0:1, 1:2], aux[0:1, 0:128], axis=AX.X),
                  xreads=[("p2aux",)], writes=[("sc11", 1)])
            au.op("vector", lambda v: v.tensor_mul(sc11[0:1, 2:3], sc11[0:1, 0:1], sc11[0:1, 1:2]),
                  reads=[("sc11", 0), ("sc11", 1)], writes=[("sc11", 2)])
            au.op("scalar", lambda s: s.activation(sc11[0:1, 3:4], sc11[0:1, 2:3], AF.Sqrt),
                  reads=[("sc11", 2)], writes=[("sc11", 3)])
            au.op("vector", lambda v: v.tensor_scalar(sc11[0:1, 4:5], sc11[0:1, 3:4], -SCALE, None,
                                                      OP.mult),
                  reads=[("sc11", 3)], writes=[("sc11", 4)])
            au.op("tensor", lambda t: t.matmul(aux[:, 0:1], onesrF[:, :], sc11[0:1, 4:5],
                                               start=True, stop=True),
                  reads=[("sc11", 4), ("onesrF",)], writes=[("p2aux",)])
            au.op("vector", lambda v: v.tensor_copy(nmhat[:, :], aux[:, 0:1]),
                  reads=[("p2aux",)], writes=[("nmhat",)])
            psA.close()

        finishA()
        sbA.close()
        au.fence()
        vhp = ctx.sbuf("vhp", [128, 2, NT, 128], BF16)
        kTp = ctx.sbuf("kTp", [128, 2, T], BF16)
        qTp = ctx.sbuf("qTp", [128, 2, 512], BF16)
        Ab = ctx.sbuf("p2A", [128, 3, 2, 512], BF16)
        osb = ctx.sbuf("osb", [128, NP, 8, 512], BF16)
        lrow = ctx.sbuf("lrow", [1, 2, 512], F32)
        rlrow = ctx.sbuf("rlrow", [1, 2, 512], F32)
        rlbsave = ctx.sbuf("rlbsave", [1, 2, 8, 512], BF16)

        with ExitStack() as psB:
            simE = [ctx.psum("p2simE0", [128, 512], F32, stack=psB),
                    ctx.psum("p2simE1", [128, 512], F32, stack=psB)]
            simO = [ctx.psum("p2simO0", [128, 512], F32, stack=psB),
                    ctx.psum("p2simO1", [128, 512], F32, stack=psB)]
            op_ = ctx.psum("p2op", [128, 512], F32, stack=psB)
            lps = ctx.psum("p2lps", [128, 2, 512], F32, stack=psB)
            pend = []
            for pr in range(NP):
                pb = pr % 2
                au.op("gpsimd", lambda g, pr=pr, pb=pb: g.dma_start(
                    out=vhp[:, pb, :, :],
                    in_=vtm_dram.rearrange("(tk p) d -> p tk d", p=128)[:, :, pr * 128:(pr + 1) * 128]),
                    reads=[("vtmd", cc) for cc in range(NT)],
                    writes=[("vhp", pb)], dma=True, stream=f"p2vl{pb}")
                au.op("sync", lambda sy, pr=pr, pb=pb: sy.dma_start(
                    out=kTp[:, pb, :], in_=ktm_dram[:, pr * 128:(pr + 1) * 128], transpose=True),
                    reads=[("kttd", cc) for cc in range(NT)],
                    writes=[("kTp", pb)], dma=True, stream="p2ktr")
                for qc in range(8):
                    qb = qc % 2
                    au.op("sync", lambda sy, pr=pr, qc=qc, qb=qb: sy.dma_start(
                        out=qTp[:, qb, :],
                        in_=qtm_dram[qc * 512:(qc + 1) * 512, pr * 128:(pr + 1) * 128],
                        transpose=True),
                        reads=[("qttd", cc) for cc in range(qc * 4, qc * 4 + 4)],
                        writes=[("qTp", qb)], dma=True, stream="p2qtr")
                    ntk = qc * 4 + 4

                    def mk_sim(tk, pb=pb, qb=qb):
                        bk = tk % 2

                        def simfn(t, tk=tk, bk=bk, pb=pb, qb=qb):
                            t.matmul(simE[bk][:, :], kTp[0:64, pb, tk * 128:(tk + 1) * 128],
                                     qTp[0:64, qb, :], start=True, stop=True,
                                     tile_position=(0, 0), skip_group_check=True)
                            return t.matmul(simO[bk][:, :], kTp[64:128, pb, tk * 128:(tk + 1) * 128],
                                            qTp[64:128, qb, :], start=True, stop=True,
                                            tile_position=(64, 0), skip_group_check=True)
                        au.op("tensor", simfn, reads=[("kTp", pb), ("qTp", qb)],
                              writes=[("p2sE", tk % 2), ("p2sO", tk % 2)])

                    def mk_av(tk, pb=pb, ntk=ntk):
                        bk = tk % 3

                        def avfn(t, tk=tk, bk=bk, pb=pb, ntk=ntk):
                            for h in range(2):
                                t.matmul(op_[h * 64:(h + 1) * 64, :], vhp[:, pb, tk, h * 64:(h + 1) * 64],
                                         Ab[:, bk, h, :], start=(tk == 0), stop=False,
                                         tile_position=(0, h * 64), skip_group_check=True)
                                i = t.matmul(lps[0:1, h, :], onescB2[:, :], Ab[:, bk, h, :],
                                             start=(tk == 0), stop=(tk == ntk - 1),
                                             skip_group_check=True)
                            return i
                        au.op("tensor", avfn,
                              reads=[("p2A", bk, 0), ("p2A", bk, 1), ("vhp", pb), ("onescB2",)],
                              writes=[("p2op",), ("p2lps",)] if (tk == 0 or tk == ntk - 1)
                              else [("p2opacc",)])

                    mk_sim(0)
                    while pend:
                        pend.pop(0)()
                    for tk in range(ntk):
                        bk = tk % 3
                        sk = tk % 2
                        for h in range(2):
                            sp = simE if h == 0 else simO
                            spk = ("p2sE", sk) if h == 0 else ("p2sO", sk)
                            au.op("scalar", lambda s, sp=sp, sk=sk, bk=bk, h=h: s.activation(
                                Ab[:, bk, h, :], sp[sk][:, :], AF.Exp,
                                bias=nmhat[:, :], scale=SCALE),
                                reads=[("nmhat",)], xreads=[spk], writes=[("p2A", bk, h)])
                            if tk >= qc * 4:
                                au.op("gpsimd", lambda g, bk=bk, qc=qc, tk=tk, h=h: g.affine_select(
                                    Ab[:, bk, h, :], Ab[:, bk, h, :], pattern=[[1, 512]],
                                    compare_op=OP.is_ge, fill=0.0,
                                    base=qc * 512 - tk * 128, channel_multiplier=-1),
                                    reads=[("p2A", bk, h)], writes=[("p2A", bk, h)])
                        if tk + 1 < ntk:
                            mk_sim(tk + 1)
                        if tk >= 2:
                            mk_av(tk - 2)
                    if ntk >= 2:
                        mk_av(ntk - 2)
                    mk_av(ntk - 1)
                    au.op("scalar", lambda s, pr=pr, qc=qc: s.activation(
                        osb[:, pr, qc, :], op_[:, :], AF.Identity),
                        xreads=[("p2op",)], writes=[("osb", pr, qc)])
                    au.op("vector", lambda v: v.tensor_copy(
                        lrow.ap().rearrange("o a x -> o (a x)"),
                        lps.ap().rearrange("p a x -> p (a x)")[0:1, :]),
                        xreads=[("p2lps",)], writes=[("lrow",)])
                    au.op("vector", lambda v: v.reciprocal(
                        rlrow.ap().rearrange("o a x -> o (a x)"),
                        lrow.ap().rearrange("o a x -> o (a x)")),
                        reads=[("lrow",)], writes=[("rlrow",)])
                    def rlcp(v, qc=qc):
                        v.tensor_copy(rlbsave[:, 0, qc, :], rlrow[:, 0, :])
                        return v.tensor_copy(rlbsave[:, 1, qc, :], rlrow[:, 1, :])
                    au.op("vector", rlcp, reads=[("rlrow",)], writes=[("rlb", qc)])

                    def finalize(pr=pr, qc=qc):
                        def rlbcfn(t):
                            t.matmul(op_[0:64, :], onesrB[0:1, 0:64], rlbsave[0:1, 0, qc, :],
                                     start=True, stop=True, tile_position=(0, 0),
                                     skip_group_check=True)
                            return t.matmul(op_[64:128, :], onesrB[0:1, 0:64], rlbsave[0:1, 1, qc, :],
                                            start=True, stop=True, tile_position=(0, 64),
                                            skip_group_check=True)
                        au.op("tensor", rlbcfn, reads=[("rlb", qc), ("onesrB",)], writes=[("p2op",)])
                        au.op("vector", lambda v: v.tensor_mul(
                            osb[:, pr, qc, :], osb[:, pr, qc, :], op_[:, :]),
                            reads=[("osb", pr, qc)], xreads=[("p2op",)], writes=[("osb", pr, qc)])
                    pend.append(finalize)
            while pend:
                pend.pop(0)()
            for qc in range(8):
                OS = [("osb", pr, qc) for pr in range(NP)]
                for ts in range(4):
                    tb = ts % 2
                    c = qc * 4 + ts

                    def opfn(t, ts=ts, qc=qc):
                        for pr in range(NP):
                            for dc in range(2):
                                i = t.matmul(lps[:, dc, :], osb[:, pr, qc, ts * 128:(ts + 1) * 128],
                                             wosb[:, pr, dc * 512:(dc + 1) * 512],
                                             start=(pr == 0), stop=(pr == NP - 1),
                                             skip_group_check=True)
                        return i
                    au.op("tensor", opfn, reads=OS + [("wosb",), ("p2lps",)], writes=[("opp",)])
                    au.op("gpsimd", lambda g, c=c, tb=tb: g.dma_start(
                        out=r2t[:, tb, :], in_=r2_dram[c * 128:(c + 1) * 128, :]),
                        reads=[("r2d", c)],
                        writes=[("r2t", tb), ("p2rk", tb, 0), ("p2rk", tb, 1)], dma=True,
                        stream=f"p2rb{tb}")
                    au.op("scalar", lambda s, tb=tb: s.activation(
                        tob[:, tb, :], lps.ap().rearrange("p a x -> p (a x)"), AF.Identity),
                        xreads=[("opp",)], writes=[("tob", tb)])
                    au.op("vector", lambda v, tb=tb: v.tensor_add(tob[:, tb, :], tob[:, tb, :],
                                                                  bobc[:, :]),
                          reads=[("tob", tb), ("bobc", 0), ("bobc", 1)], writes=[("tob", tb)])
                    au.op("vector", lambda v, tb=tb: v.tensor_add(
                        tob[:, tb, :], tob[:, tb, :], r2t[:, tb, :]),
                        reads=[("tob", tb), ("r2t", tb), ("p2rk", tb, 0)],
                        writes=[("tob", tb), ("p2rk", tb, 0), ("p2rk", tb, 1)])
                    rkeys = [("tob", tb), ("p2rk", tb, 0), ("p2rk", tb, 1)]
                    emit_ln_inplace(ctx, au, tob[:, tb, :], rkeys, lnb2[tb], tag=20 + tb)
                    au.op("sync", lambda sy, c=c, tb=tb: sy.dma_start(
                        out=r3_dram[c * 128:(c + 1) * 128, :], in_=tob[:, tb, :]),
                        reads=rkeys, writes=[("r3d", c)], dma=True, stream=f"p2o{tb}")
        return [("r3d", c) for c in range(NT)]

    if tail_hook_reg is not None:
        return _cont
    return _cont()


def emit_p3(ctx, au, r3_dram, w1_dram, b1_dram, w2_dram, b2_dram, out_dram):
    nc = ctx.nc
    identB, identF = ctx.identB, ctx.identF
    INTER = 4096
    NF = INTER // 128

    w1sb = ctx.sbuf("w1sb", [128, ND, INTER], BF16)
    w2sb = ctx.sbuf("w2sb", [128, NF, D], BF16)
    b1c = ctx.sbuf("b1c", [128, NF], F32)
    b2bc = ctx.sbuf("b2bc", [128, D], F32)
    r3t = ctx.sbuf("r3t", [128, 2, D], F32)
    r3t2 = ctx.sbuf("r3t2", [128, 2, D], F32)
    r3Tb = ctx.sbuf("r3Tb", [128, 2, ND, 256], BF16)
    hck = ctx.sbuf("hck", [128, NF, 256], BF16)
    yt3 = ctx.sbuf("yt3", [128, 2, D], F32)
    onesrF3 = ctx.sbuf("onesrF3", [1, 128], F32)
    sqs3 = ctx.sbuf("p3sqs", [128, 1, D], BF16)
    mu3 = ctx.sbuf("p3mu", [128, 2], F32)
    ssq3 = ctx.sbuf("p3ssq", [128, 2], F32)
    rstd3 = ctx.sbuf("p3rstd", [128, 2], F32)
    mb3 = ctx.sbuf("p3mb", [128, 2], F32)
    lnb3 = [(sqs3[:, 0, :], mu3[:, b:b + 1], ssq3[:, b:b + 1], rstd3[:, b:b + 1],
             mb3[:, b:b + 1]) for b in range(2)]

    au.op("gpsimd", lambda g: g.iota(onesrF3[:, :], pattern=[[0, 128]], base=1,
                                     channel_multiplier=0, allow_small_or_imprecise_dtypes=True),
          writes=[("onesrF3",)])
    au.op("gpsimd", lambda g: g.dma_start(
        out=w1sb[:, :, :], in_=w1_dram.rearrange("(dt p) f -> p dt f", p=128)),
        writes=[("w1sb",)], dma=True, stream="p3w1")
    au.op("gpsimd", lambda g: g.dma_start(
        out=w2sb[:, :, :], in_=w2_dram.rearrange("(ft p) d -> p ft d", p=128)),
        writes=[("w2sb",)], dma=True, stream="p3w2")
    _p3b = ExitStack()
    b1row = ctx.sbuf("b1row", [1, INTER], F32, stack=_p3b)
    b2row = ctx.sbuf("b2row", [1, D], F32, stack=_p3b)
    au.op("gpsimd", lambda g: g.dma_start(out=b1row[:, :], in_=b1_dram[None, :]),
          writes=[("b1row",)], dma=True, stream="p3b1")
    au.op("gpsimd", lambda g: g.dma_start(out=b2row[:, :], in_=b2_dram[None, :]),
          writes=[("b2row",)], dma=True, stream="p3b2")

    with ExitStack() as ps3:
        tp3 = ctx.psum("p3tp", [128, 512], F32, stack=ps3)
        hps3 = [ctx.psum("p3hA", [128, 256], F32, stack=ps3),
                ctx.psum("p3hB", [128, 256], F32, stack=ps3)]
        yps3 = [ctx.psum("p3yA", [128, 2, 512], F32, stack=ps3),
                ctx.psum("p3yB", [128, 2, 512], F32, stack=ps3)]

        def b1cfn(t):
            for ft in range(NF):
                i = t.transpose(tp3[:, ft:ft + 1], b1row[0:1, ft * 128:(ft + 1) * 128],
                                identF[0:1, 0:1])
            return i
        au.op("tensor", b1cfn, reads=[("b1row",), ("identF",)], writes=[("p3tp",)])
        au.op("vector", lambda v: v.tensor_copy(b1c[:, :], tp3[:, 0:NF]),
              reads=[("p3tp",)], writes=[("b1c",)])
        au.op("tensor", lambda t: t.matmul(tp3[:, 0:512], onesrF3[:, :], b2row[0:1, 0:512],
                                           start=True, stop=True),
              reads=[("onesrF3",), ("b2row",)], writes=[("p3tp",)])
        au.op("vector", lambda v: v.tensor_copy(b2bc[:, 0:512], tp3[:, 0:512]),
              reads=[("p3tp",)], writes=[("b2bc", 0)])
        au.op("tensor", lambda t: t.matmul(tp3[:, 0:512], onesrF3[:, :], b2row[0:1, 512:1024],
                                           start=True, stop=True),
              reads=[("onesrF3",), ("b2row",)], writes=[("p3tp",)])
        au.op("vector", lambda v: v.tensor_copy(b2bc[:, 512:1024], tp3[:, 0:512]),
              reads=[("p3tp",)], writes=[("b2bc", 1)])
        au.fence()
        _p3b.close()

        def emit_chunk_T(tc):
            for tt in range(2):
                c = tc * 2 + tt
                b = c % 2
                au.op("gpsimd", lambda g, c=c, b=b: g.dma_start(
                    out=r3t[:, b, :], in_=r3_dram[c * 128:(c + 1) * 128, :]),
                    writes=[("r3t", b)], dma=True, stream=f"p3r{b}")
                for dp in range(ND // 2):
                    def t3fn(t, b=b, dp=dp):
                        t.transpose(tp3[:, 0:128], r3t[:, b, (2 * dp) * 128:(2 * dp + 1) * 128],
                                    identF[:, :])
                        return t.transpose(tp3[:, 128:256],
                                           r3t[:, b, (2 * dp + 1) * 128:(2 * dp + 2) * 128],
                                           identF[:, :])
                    au.op("tensor", t3fn, reads=[("r3t", b), ("identF",)], writes=[("p3tp",)])

                    def t3ev(v, tt=tt, dp=dp, tc=tc):
                        v.tensor_copy(r3Tb[:, tc % 2, 2 * dp, tt * 128:(tt + 1) * 128],
                                      tp3[:, 0:128])
                        return v.tensor_copy(r3Tb[:, tc % 2, 2 * dp + 1, tt * 128:(tt + 1) * 128],
                                             tp3[:, 128:256])
                    au.op("vector", t3ev, reads=[("p3tp",)],
                          writes=[("r3Tb", tc % 2, 2 * dp, tt), ("r3Tb", tc % 2, 2 * dp + 1, tt)])

        emit_chunk_T(0)
        for tc in range(16):
            R3T = [("r3Tb", tc % 2, d, tt) for d in range(ND) for tt in range(2)]
            for ft in range(NF):
                fb = ft % 2

                def hfn(t, ft=ft, fb=fb, tc=tc):
                    for d in range(ND):
                        i = t.matmul(hps3[fb][:, :], w1sb[:, d, ft * 128:(ft + 1) * 128],
                                     r3Tb[:, tc % 2, d, :], start=(d == 0), stop=(d == ND - 1))
                    return i
                au.op("tensor", hfn, reads=R3T + [("w1sb",)], writes=[("p3h", fb)])
                au.op("scalar", lambda s, ft=ft, fb=fb: s.activation(
                    hck[:, ft, :], hps3[fb][:, :], AF.Relu, bias=b1c[:, ft:ft + 1]),
                    reads=[("b1c",)], xreads=[("p3h", fb)], writes=[("hck", ft)])
            HCK = [("hck", ft) for ft in range(NF)]
            if tc + 1 < 16:
                emit_chunk_T(tc + 1)
            for tt in range(2):
                c = tc * 2 + tt
                yb = tt % 2

                def yfn(t, tt=tt, yb=yb):
                    for ft in range(NF):
                        for qc in range(2):
                            i = t.matmul(yps3[yb][:, qc, :], hck[:, ft, tt * 128:(tt + 1) * 128],
                                         w2sb[:, ft, qc * 512:(qc + 1) * 512],
                                         start=(ft == 0), stop=(ft == NF - 1))
                    return i
                au.op("tensor", yfn, reads=HCK + [("w2sb",)], writes=[("p3y", yb)])
                b = (c % 2) + 2
                au.op("gpsimd", lambda g, c=c, b=b: g.dma_start(
                    out=r3t2[:, b - 2, :], in_=r3_dram[c * 128:(c + 1) * 128, :]),
                    writes=[("r3t2", b - 2), ("p3rk", b, 0), ("p3rk", b, 1)], dma=True,
                    stream=f"p3rr{b - 2}")
                au.op("scalar", lambda s, yb=yb, b=b: s.activation(
                    yt3[:, b - 2, :], yps3[yb].ap().rearrange("p a x -> p (a x)"), AF.Identity),
                    xreads=[("p3y", yb)], writes=[("yt3", b - 2)])
                au.op("vector", lambda v, b=b: v.tensor_add(yt3[:, b - 2, :], yt3[:, b - 2, :],
                                                            b2bc[:, :]),
                      reads=[("yt3", b - 2), ("b2bc", 0), ("b2bc", 1)], writes=[("yt3", b - 2)])
                au.op("vector", lambda v, b=b: v.tensor_add(yt3[:, b - 2, :], yt3[:, b - 2, :],
                                                            r3t2[:, b - 2, :]),
                      reads=[("yt3", b - 2), ("r3t2", b - 2), ("p3rk", b, 0)],
                      writes=[("yt3", b - 2), ("p3rk", b, 0), ("p3rk", b, 1)])
                rkeys = [("yt3", b - 2), ("p3rk", b, 0), ("p3rk", b, 1)]
                emit_ln_inplace(ctx, au, yt3[:, b - 2, :], rkeys, lnb3[b - 2], tag=30 + b)
                au.op("sync", lambda sy, c=c, b=b: sy.dma_start(
                    out=out_dram[c * 128:(c + 1) * 128, :], in_=yt3[:, b - 2, :]),
                    reads=rkeys, writes=[("outd", c)], dma=True, stream=f"p3o{b - 2}")
    return [("outd", c) for c in range(NT)]


from concourse.bass_utils import run_bass_kernel_spmd

INTER = 4096


def _rope_tables():
    d = 64
    inv = 1.0 / 10000.0 ** (np.arange(0, d, 2, dtype=np.float64) / d)
    pos = np.arange(T, dtype=np.float64)
    ang = pos[:, None] * inv[None, :]
    ct = np.tile(np.cos(ang), (1, 16)).astype(np.float32)
    st = np.tile(np.sin(ang), (1, 16)).astype(np.float32)
    return ct, st


def _build(phase):
    nc = bass.Bass("TRN2", target_bir_lowering=False)
    with ExitStack() as stack:
        block = stack.enter_context(nc.Block())
        ctx = Ctx(nc, block, stack)
        au = AS(nc, block, stack)
        emit_ident(ctx, au)
        if phase == 0:
            x = nc.dram_tensor("x", [T, D], F32, kind="ExternalInput").ap()
            wqk = nc.dram_tensor("wqk", [2048, D], F32, kind="ExternalInput").ap()
            bqk = nc.dram_tensor("bqk", [2048], F32, kind="ExternalInput").ap()
            r1 = nc.dram_tensor("r1", [T, D], F32, kind="ExternalOutput").ap()
            xb = nc.dram_tensor("xb", [T, D], BF16, kind="Internal").ap()
            attnT = nc.dram_tensor("attnT", [NP, 128, T], BF16, kind="Internal").ap()
            outk = emit_p0(ctx, au, x, wqk, bqk, xb, attnT, r1)
        elif phase == 1:
            r1 = nc.dram_tensor("r1", [T, D], F32, kind="ExternalInput").ap()
            gw = nc.dram_tensor("gw", [8, D], F32, kind="ExternalInput").ap()
            w1 = nc.dram_tensor("w1", [8, D, INTER], F32, kind="ExternalInput").ap()
            w2 = nc.dram_tensor("w2", [8, INTER, D], F32, kind="ExternalInput").ap()
            r2 = nc.dram_tensor("r2", [T, D], F32, kind="ExternalOutput").ap()
            cx = nc.dram_tensor("cx", [8, KCAP, D], BF16, kind="Internal").ap()
            yd = nc.dram_tensor("yd", [8, KCAP, D], F32, kind="Internal").ap()
            probs, mprob, idxu = emit_p1(ctx, au, r1, gw, w1, w2, cx, yd, r2)
            emit_p1_experts(ctx, au, w1, w2, cx, yd, r1, idxu)
            outk = emit_p1_tail(ctx, au, r1, yd, r2, mprob, idxu)
        elif phase == 2:
            r2 = nc.dram_tensor("r2", [T, D], F32, kind="ExternalInput").ap()
            wq = nc.dram_tensor("wq", [D, D], F32, kind="ExternalInput").ap()
            wk = nc.dram_tensor("wk", [D, D], F32, kind="ExternalInput").ap()
            wv = nc.dram_tensor("wv", [D, D], F32, kind="ExternalInput").ap()
            wo = nc.dram_tensor("wo", [D, D], F32, kind="ExternalInput").ap()
            bo = nc.dram_tensor("bo", [D], F32, kind="ExternalInput").ap()
            ct = nc.dram_tensor("ct", [T, 512], F32, kind="ExternalInput").ap()
            st = nc.dram_tensor("st", [T, 512], F32, kind="ExternalInput").ap()
            r3 = nc.dram_tensor("r3", [T, D], F32, kind="ExternalOutput").ap()
            qtm = nc.dram_tensor("qtm", [T, D], BF16, kind="Internal").ap()
            ktm = nc.dram_tensor("ktm", [T, D], BF16, kind="Internal").ap()
            vtm = nc.dram_tensor("vtm", [T, D], BF16, kind="Internal").ap()
            outk = emit_p2(ctx, au, r2, wq, wk, wv, wo, bo, ct, st, qtm, ktm, vtm, r3)
        else:
            r3 = nc.dram_tensor("r3", [T, D], F32, kind="ExternalInput").ap()
            w1 = nc.dram_tensor("w1", [D, INTER], F32, kind="ExternalInput").ap()
            b1 = nc.dram_tensor("b1", [INTER], F32, kind="ExternalInput").ap()
            w2 = nc.dram_tensor("w2", [INTER, D], F32, kind="ExternalInput").ap()
            b2 = nc.dram_tensor("b2", [D], F32, kind="ExternalInput").ap()
            out = nc.dram_tensor("out", [T, D], F32, kind="ExternalOutput").ap()
            outk = emit_p3(ctx, au, r3, w1, b1, w2, b2, out)
        au.op("sync", lambda sy: sy.nop(), reads=outk)
        au.flush()
    return nc


def _build_fused():
    nc = bass.Bass("TRN2", target_bir_lowering=False)
    x = nc.dram_tensor("x", [T, D], F32, kind="ExternalInput").ap()
    wqk = nc.dram_tensor("wqk", [2048, D], F32, kind="ExternalInput").ap()
    bqk = nc.dram_tensor("bqk", [2048], F32, kind="ExternalInput").ap()
    gw = nc.dram_tensor("gw", [8, D], F32, kind="ExternalInput").ap()
    w1 = nc.dram_tensor("w1", [8, D, INTER], F32, kind="ExternalInput").ap()
    w2 = nc.dram_tensor("w2", [8, INTER, D], F32, kind="ExternalInput").ap()
    wq = nc.dram_tensor("wq", [D, D], F32, kind="ExternalInput").ap()
    wk = nc.dram_tensor("wk", [D, D], F32, kind="ExternalInput").ap()
    wv = nc.dram_tensor("wv", [D, D], F32, kind="ExternalInput").ap()
    wo = nc.dram_tensor("wo", [D, D], F32, kind="ExternalInput").ap()
    bo = nc.dram_tensor("bo", [D], F32, kind="ExternalInput").ap()
    fw1 = nc.dram_tensor("fw1", [D, INTER], F32, kind="ExternalInput").ap()
    fb1 = nc.dram_tensor("fb1", [INTER], F32, kind="ExternalInput").ap()
    fw2 = nc.dram_tensor("fw2", [INTER, D], F32, kind="ExternalInput").ap()
    fb2 = nc.dram_tensor("fb2", [D], F32, kind="ExternalInput").ap()
    ct = nc.dram_tensor("ct", [T, 512], F32, kind="ExternalInput").ap()
    st = nc.dram_tensor("st", [T, 512], F32, kind="ExternalInput").ap()
    out = nc.dram_tensor("out", [T, D], F32, kind="ExternalOutput").ap()
    r1 = nc.dram_tensor("r1", [T, D], F32, kind="Internal").ap()
    r2 = nc.dram_tensor("r2", [T, D], F32, kind="Internal").ap()
    r3 = nc.dram_tensor("r3", [T, D], F32, kind="Internal").ap()
    xb = nc.dram_tensor("xb", [T, D], BF16, kind="Internal").ap()
    attnT = nc.dram_tensor("attnT", [NP, 128, T], BF16, kind="Internal").ap()
    cx = nc.dram_tensor("cx", [8, KCAP, D], BF16, kind="Internal").ap()
    yd = nc.dram_tensor("yd", [8, KCAP, D], F32, kind="Internal").ap()
    qtm = nc.dram_tensor("qtm", [T, D], BF16, kind="Internal").ap()
    ktm = nc.dram_tensor("ktm", [T, D], BF16, kind="Internal").ap()
    vtm = nc.dram_tensor("vtm", [T, D], BF16, kind="Internal").ap()
    with ExitStack() as stack:
        block = stack.enter_context(nc.Block())
        ctx = Ctx(nc, block, stack)
        au = AS(nc, block, stack)
        main_stack = ctx.stack
        emit_ident(ctx, au)
        ctx.stack = ExitStack()
        emit_p0(ctx, au, x, wqk, bqk, xb, attnT, r1)
        ctx.stack.close()
        au.fence()
        p1stack = ExitStack()
        ctx.stack = p1stack
        probs, mprob, idxu = emit_p1(ctx, au, r1, gw, w1, w2, cx, yd, r2)
        estack = ExitStack()
        ctx.stack = estack
        emit_p1_experts(ctx, au, w1, w2, cx, yd, r1, idxu)
        estack.close()
        au.fence()
        tailstack = ExitStack()
        ctx.stack = tailstack
        hookbox = {}
        def tail_hook_reg(tileA):
            hookbox["tileA"] = tileA
        ctx.stack = ExitStack()
        p2stack = ctx.stack
        emit_p2_pending = emit_p2(ctx, au, r2, wq, wk, wv, wo, bo, ct, st, qtm, ktm, vtm, r3,
                                  tail_hook_reg=tail_hook_reg)
        ctx.stack = tailstack
        emit_p1_tail(ctx, au, r1, yd, r2, mprob, idxu,
                     after_tile=lambda c, rt, rk: hookbox["tileA"](c, rt, rk))
        tailstack.close()
        ctx.stack = p2stack
        emit_p2_pending()
        p2stack.close()
        p1stack.close()
        au.fence()
        ctx.stack = ExitStack()
        outk = emit_p3(ctx, au, r3, fw1, fb1, fw2, fb2, out)
        ctx.stack.close()
        ctx.stack = main_stack
        au.op("sync", lambda sy: sy.nop(), reads=outk)
        au.flush()
    return nc


def kernel(L0_attn_x, L0_attn_W_qk, L0_attn_b_qk, L0_ffn_gate_w, L0_ffn_w1, L0_ffn_w2,
           L1_attn_Wq, L1_attn_Wk, L1_attn_Wv, L1_attn_Wo, L1_attn_bo,
           L1_ffn_W1, L1_ffn_b1, L1_ffn_W2, L1_ffn_b2):
    n = 8
    cores = list(range(n))
    ct, st = _rope_tables()
    A = np.ascontiguousarray
    base = {
        "wqk": A(np.asarray(L0_attn_W_qk)), "bqk": A(np.asarray(L0_attn_b_qk)),
        "gw": A(np.asarray(L0_ffn_gate_w)), "w1": A(np.asarray(L0_ffn_w1)),
        "w2": A(np.asarray(L0_ffn_w2)),
        "wq": A(np.asarray(L1_attn_Wq)), "wk": A(np.asarray(L1_attn_Wk)),
        "wv": A(np.asarray(L1_attn_Wv)), "wo": A(np.asarray(L1_attn_Wo)),
        "bo": A(np.asarray(L1_attn_bo)),
        "fw1": A(np.asarray(L1_ffn_W1)), "fb1": A(np.asarray(L1_ffn_b1)),
        "fw2": A(np.asarray(L1_ffn_W2)), "fb2": A(np.asarray(L1_ffn_b2)),
        "ct": ct, "st": st,
    }
    in_maps = []
    for i in range(n):
        m = dict(base)
        m["x"] = A(np.asarray(L0_attn_x)[i])
        in_maps.append(m)
    nc = _build_fused()
    res = run_bass_kernel_spmd(nc, in_maps, core_ids=cores)
    return np.stack([res.results[i]["out"] for i in range(n)], axis=0).astype(np.float32)


# version 26 (final)
# speedup vs baseline: 1.0443x; 1.0109x over previous
import math
from contextlib import ExitStack, contextmanager
import numpy as np
import concourse.bass as bass
import concourse.mybir as mybir

F32 = mybir.dt.float32
BF16 = mybir.dt.bfloat16
U32 = mybir.dt.uint32
AF = mybir.ActivationFunctionType
OP = mybir.AluOpType
AX = mybir.AxisListType

T = 4096
D = 1024
NH = 16
DH = 64
NP = 8
NT = T // 128
ND = D // 128
LAM = 1.0507009873554805
ALPHA = 1.6732632423543772
LA = LAM * ALPHA
LN_LA = math.log(LA)
SCALE = DH ** -0.5
OSCALE = T ** -0.5
LN_EPS = 1e-5

ENGS = ("tensor", "vector", "scalar", "gpsimd", "sync")


class AS:

    def __init__(self, nc, block, stack):
        self.nc, self.block, self.stack = nc, block, stack
        self.sems = {}
        for e in ENGS:
            self.sems[(e, "c")] = stack.enter_context(nc.semaphore(f"as_{e}_c"))
        self.cnt = {k: 0 for k in self.sems}
        self.queue = {e: [] for e in ENGS}
        self.waited = {e: {} for e in ENGS}
        self.last_w = {}
        self.readers = {}
        self.flushed = False

    def dstream(self, e, stream):
        key = (e, "d", stream)
        if key not in self.sems:
            self.sems[key] = self.stack.enter_context(
                self.nc.semaphore(f"as_{e}_d_{stream}"))
            self.cnt[key] = 0
        return key

    def op(self, e, fn, reads=(), writes=(), xreads=(), dma=False, stream=None):
        if dma:
            mykey = self.dstream(e, stream or e)
            inc = 16
        else:
            mykey = (e, "c")
            inc = 1
        deps = {}

        def add_dep(dk, dc):
            if dk == mykey and not dma and e == "tensor":
                return
            deps[dk] = max(deps.get(dk, 0), dc)

        for r in tuple(reads) + tuple(writes) + tuple(xreads):
            lw = self.last_w.get(r)
            if lw is not None:
                add_dep(*lw)
        for r in tuple(writes) + tuple(xreads):
            for rd in self.readers.get(r, ()):
                add_dep(*rd)
        mycnt = self.cnt[mykey] + inc
        self.cnt[mykey] = mycnt
        waits = []
        w = self.waited[e]
        for dk, dc in deps.items():
            if w.get(dk, 0) < dc:
                w[dk] = dc
                waits.append((dk, dc))
        sems = self.sems
        me = sems[mykey]

        def closure(eng, waits=waits, fn=fn, me=me, inc=inc):
            for dk, dc in waits:
                eng.wait_ge(sems[dk], dc)
            try:
                instr = fn(eng)
            except Exception:
                import sys
                print("FAILING OP waits:", waits, "inc:", inc, file=sys.stderr)
                raise
            instr.then_inc(me, inc)

        self.queue[e].append(closure)
        for r in tuple(writes):
            self.last_w[r] = (mykey, mycnt)
            self.readers[r] = []
        for r in tuple(reads) + tuple(xreads):
            self.readers.setdefault(r, []).append((mykey, mycnt))
        return (mykey, mycnt)

    def fence(self):
        snap = [(k, c) for k, c in self.cnt.items() if c > 0]
        self.closed = getattr(self, "closed", set())
        for e in ENGS:
            waits = []
            for k, c in snap:
                if k == (e, "c"):
                    continue
                if self.waited[e].get(k, 0) < c:
                    self.waited[e][k] = c
                    waits.append((k, c))
            if waits:
                def wfn(eng, waits=waits):
                    for k, c in waits:
                        eng.wait_ge(self.sems[k], c)
                self.queue[e].append(wfn)
        for k, c in snap:
            if len(k) == 3:
                self.closed.add(k)

    def flush(self):
        assert not self.flushed
        self.flushed = True
        for key, cnt in list(self.cnt.items()):
            if len(key) == 3 and cnt > 0:
                e = key[0]
                if self.waited[e].get(key, 0) < cnt:
                    self.waited[e][key] = cnt
                    self.queue[e].append(
                        lambda eng, key=key, cnt=cnt: eng.wait_ge(self.sems[key], cnt))
        for e in ENGS:
            q = self.queue[e]
            if not q:
                continue

            def runner(eng, q=q):
                for c in q:
                    c(eng)

            getattr(self.block, e)(runner)


class Ctx:
    def __init__(self, nc, block, stack):
        self.nc, self.block, self.stack = nc, block, stack

    def sbuf(self, name, shape, dtype, stack=None):
        return (stack or self.stack).enter_context(self.nc.sbuf_tensor(name, shape, dtype))

    def psum(self, name, shape, dtype=F32, stack=None):
        return (stack or self.stack).enter_context(self.nc.psum_tensor(name, shape, dtype))


def emit_ident(ctx, au):
    identB = ctx.sbuf("identB", [128, 128], BF16)
    identF = ctx.sbuf("identF", [128, 128], F32)
    au.op("gpsimd", lambda g: g.iota(identB[:, :], pattern=[[1, 128]], base=0,
                                     channel_multiplier=-1, allow_small_or_imprecise_dtypes=True),
          writes=[("identB",)])
    au.op("gpsimd", lambda g: g.iota(identF[:, :], pattern=[[1, 128]], base=0,
                                     channel_multiplier=-1, allow_small_or_imprecise_dtypes=True),
          writes=[("identF",)])
    au.op("vector", lambda v: v.tensor_scalar(identB[:, :], identB[:, :], 0.0, None, OP.is_equal),
          reads=[("identB",)], writes=[("identB",)])
    au.op("vector", lambda v: v.tensor_scalar(identF[:, :], identF[:, :], 0.0, None, OP.is_equal),
          reads=[("identF",)], writes=[("identF",)])
    ccol = ctx.sbuf("ccol", [128, 2], F32)
    au.op("vector", lambda v: v.memset(ccol[:, 0:1], LN_LA), writes=[("ccol",)])
    ctx.identB, ctx.identF, ctx.ccol = identB, identF, ccol


def emit_tm_to_fm(ctx, au, pstack, src_dram, dst_fm, nt, nd, tag, also_bf16_dram=None,
                  src_key=None):
    identB = ctx.identB
    xtile = ctx.sbuf(f"tt_{tag}", [128, 2, nd * 128], BF16)
    tps = [ctx.psum(f"tp_{tag}A", [128, 1024], BF16, stack=pstack),
           ctx.psum(f"tp_{tag}B", [128, 1024], BF16, stack=pstack)]
    for c in range(nt):
        b = c % 2
        au.op("gpsimd", lambda g, c=c, b=b: g.dma_start(
            out=xtile[:, b, :], in_=src_dram[c * 128:(c + 1) * 128, :]),
            reads=[(src_key, c)] if src_key else [], writes=[(f"tt_{tag}", b)], dma=True,
            stream=f"tt{tag}{b}")
        if also_bf16_dram is not None:
            au.op("gpsimd", lambda g, c=c, b=b: g.dma_start(
                out=also_bf16_dram[c * 128:(c + 1) * 128, :], in_=xtile[:, b, :]),
                reads=[(f"tt_{tag}", b)], writes=[(f"{tag}bd", c)], dma=True,
                stream=f"st{tag}{b}")
        for dp in range(nd // 2):
            pb = (c * (nd // 2) + dp) % 2

            def mmfn(t, c=c, dp=dp, pb=pb, b=b):
                t.transpose(tps[pb][:, 0:128], xtile[:, b, (2 * dp) * 128:(2 * dp + 1) * 128],
                            identB[:, :])
                return t.transpose(tps[pb][:, 128:256],
                                   xtile[:, b, (2 * dp + 1) * 128:(2 * dp + 2) * 128], identB[:, :])

            au.op("tensor", mmfn, reads=[(f"tt_{tag}", b), ("identB",)],
                  writes=[(f"tp_{tag}", pb)])

            def evfn(v, c=c, dp=dp, pb=pb):
                v.tensor_copy(dst_fm[:, 2 * dp, c * 128:(c + 1) * 128], tps[pb][:, 0:128])
                return v.tensor_copy(dst_fm[:, 2 * dp + 1, c * 128:(c + 1) * 128],
                                     tps[pb][:, 128:256])

            au.op("vector", evfn, reads=[(f"tp_{tag}", pb)],
                  writes=[(f"fm_{tag}", 2 * dp), (f"fm_{tag}", 2 * dp + 1)])


def emit_ln_inplace(ctx, au, rt, rkeys, lnbufs, outkeys=(), tag=0):
    sqs, mu, ssq, rstd, mb = lnbufs
    au.op("vector", lambda v: v.reduce_sum(mu[:, :], rt, axis=AX.X),
          reads=rkeys, writes=[("ln_mu", tag)])
    au.op("scalar", lambda s: s.activation(sqs[:, :], rt, AF.Square, accum_out=ssq[:, :]),
          reads=rkeys, writes=[("ln_ssq", tag), ("ln_sqs", tag), ("ln_sqs_sh",)])
    au.op("vector", lambda v: v.tensor_scalar(mu[:, :], mu[:, :], 1.0 / D, None, OP.mult),
          reads=[("ln_mu", tag)], writes=[("ln_mu", tag)])
    au.op("vector", lambda v: v.tensor_tensor(rstd[:, :], mu[:, :], mu[:, :], OP.mult),
          reads=[("ln_mu", tag)], writes=[("ln_rstd", tag)])
    au.op("vector", lambda v: v.scalar_tensor_tensor(rstd[:, :], ssq[:, :], 1.0 / D, rstd[:, :],
                                                     OP.mult, OP.subtract),
          reads=[("ln_ssq", tag), ("ln_rstd", tag)], writes=[("ln_rstd", tag)])
    au.op("vector", lambda v: v.tensor_scalar(rstd[:, :], rstd[:, :], 0.0, LN_EPS, OP.max, OP.add),
          reads=[("ln_rstd", tag)], writes=[("ln_rstd", tag)])
    au.op("scalar", lambda s: s.activation(rstd[:, :], rstd[:, :], AF.Sqrt),
          reads=[("ln_rstd", tag)], writes=[("ln_rstd", tag)])
    au.op("vector", lambda v: v.reciprocal(rstd[:, :], rstd[:, :]),
          reads=[("ln_rstd", tag)], writes=[("ln_rstd", tag)])
    au.op("vector", lambda v: v.scalar_tensor_tensor(mb[:, :], mu[:, :], -1.0, rstd[:, :],
                                                     OP.mult, OP.mult),
          reads=[("ln_mu", tag), ("ln_rstd", tag)], writes=[("ln_mb", tag)])
    au.op("scalar", lambda s: s.activation(rt, rt, AF.Identity, bias=mb[:, :], scale=rstd[:, :]),
          reads=list(rkeys) + [("ln_mb", tag), ("ln_rstd", tag)], writes=list(rkeys) + list(outkeys))


def emit_p0(ctx, au, x_dram, wqk_dram, bqk_dram, xb_dram, attnT_dram, r1_dram, stages='ABCDE'):
    nc = ctx.nc
    identB, identF = ctx.identB, ctx.identF

    xT = ctx.sbuf("xT", [128, ND, T], BF16)
    wqkT = ctx.sbuf("wqkT", [128, ND, 2048], BF16)
    b_row = ctx.sbuf("b_row", [1, 2048], F32)
    nkbar_row = ctx.sbuf("nkbar_row", [1, 2048], F32)
    xbar = ctx.sbuf("xbar", [128, ND], F32)
    xbar_b = ctx.sbuf("xbar_b", [128, ND], BF16)
    qb_col = ctx.sbuf("qb_col", [128, NP], F32)
    kb_col = ctx.sbuf("kb_col", [128, NP], F32)
    ov_col = ctx.sbuf("ov_col", [128, NP], F32)

    with ExitStack() as ps_ab:
        xtile = ctx.sbuf("tt_x", [128, 2, D], BF16)
        for c in range(NT):
            b = c % 2
            au.op("gpsimd", lambda g, c=c, b=b: g.dma_start(
                out=xtile[:, b, :], in_=x_dram[c * 128:(c + 1) * 128, :]),
                writes=[("tt_x", b)], dma=True, stream=f"ttx{b}")
            au.op("gpsimd", lambda g, c=c, b=b: g.dma_start(
                out=xb_dram[c * 128:(c + 1) * 128, :], in_=xtile[:, b, :]),
                reads=[("tt_x", b)], writes=[("xbd", c)], dma=True, stream=f"stx{b}")
            au.op("sync", lambda sy, c=c: sy.dma_start(
                out=xT[:, :, c * 128:(c + 1) * 128], in_=xb_dram[c * 128:(c + 1) * 128, :],
                transpose=True),
                reads=[("xbd", c)], writes=[("fm_x", c % 2)] + [("fm_xd", d) for d in range(ND)],
                dma=True, stream="xtr")
        XT = [("fm_xd", d) for d in range(ND)]

        wnat = ctx.sbuf("wnat", [128, 2, 4, D], BF16)
        au.op("gpsimd", lambda g: g.dma_start(out=b_row[:, :], in_=bqk_dram[None, :]),
              writes=[("b_row",)], dma=True, stream="bload")
        wps = [ctx.psum("wpsA", [128, 1024], BF16, stack=ps_ab),
               ctx.psum("wpsB", [128, 1024], BF16, stack=ps_ab)]
        for rnd in range(4):
            rb = rnd % 2
            au.op("gpsimd", lambda g, rnd=rnd, rb=rb: g.dma_start(
                out=wnat[:, rb, :, :],
                in_=wqk_dram.rearrange("(jt p) d -> p jt d", p=128)[:, rnd * 4:(rnd + 1) * 4, :]),
                writes=[("wnat", rb)], dma=True, stream=f"wload{rb}")
            for jl in range(4):
                jt = rnd * 4 + jl
                for dp in range(ND // 2):
                    pb = (jt * (ND // 2) + dp) % 2

                    def mmfn(t, jl=jl, dp=dp, pb=pb, rb=rb):
                        t.transpose(wps[pb][:, 0:128],
                                    wnat[:, rb, jl, (2 * dp) * 128:(2 * dp + 1) * 128],
                                    identB[:, :])
                        return t.transpose(wps[pb][:, 128:256],
                                           wnat[:, rb, jl, (2 * dp + 1) * 128:(2 * dp + 2) * 128],
                                           identB[:, :])

                    au.op("tensor", mmfn, reads=[("wnat", rb), ("identB",)], writes=[("wps", pb)])

                    def evfn(v, jt=jt, dp=dp, pb=pb):
                        v.tensor_copy(wqkT[:, 2 * dp, jt * 128:(jt + 1) * 128], wps[pb][:, 0:128])
                        return v.tensor_copy(wqkT[:, 2 * dp + 1, jt * 128:(jt + 1) * 128],
                                             wps[pb][:, 128:256])

                    au.op("vector", evfn, reads=[("wps", pb)], writes=[("wqkT",)])

        if 'C' not in stages:
            return []
        def xbarfn(v):
            for d in range(ND):
                i = v.reduce_sum(xbar[:, d:d + 1], xT[:, d, :], axis=AX.X)
            return i

        au.op("vector", xbarfn, reads=XT, writes=[("xbar",)])
        au.op("vector", lambda v: v.tensor_scalar(xbar[:, :], xbar[:, :], 1.0 / T, None, OP.mult),
              reads=[("xbar",)], writes=[("xbar",)])
        au.op("vector", lambda v: v.tensor_copy(xbar_b[:, :], xbar[:, :]),
              reads=[("xbar",)], writes=[("xbar_b",)])
        au.op("vector", lambda v: v.tensor_scalar(ov_col[:, :], xbar[:, :], -LA * T * OSCALE,
                                                  None, OP.mult),
              reads=[("xbar",)], writes=[("ov_col",)])
        kps1 = ctx.psum("kps1", [1, 512], F32, stack=ps_ab)
        bps = ctx.psum("bps", [128, 512], F32, stack=ps_ab)
        for c in range(4):
            def kbarfn(t, c=c):
                for d in range(ND):
                    i = t.matmul(kps1[:, :], xbar_b[:, d:d + 1], wqkT[:, d, c * 512:(c + 1) * 512],
                                 start=(d == 0), stop=(d == ND - 1))
                return i

            au.op("tensor", kbarfn, reads=[("xbar_b",), ("wqkT",)], writes=[("kps1",)])
            au.op("scalar", lambda s, c=c: s.activation(nkbar_row[:, c * 512:(c + 1) * 512],
                                                        kps1[:, :], AF.Identity, scale=-1.0),
                  reads=[("kps1",)], writes=[("nkbar_row",)])

        def biasfn(t):
            for ph in range(NP):
                t.transpose(bps[:, ph * 2:ph * 2 + 1],
                            b_row[0:1, ph * 256: ph * 256 + 256: 2], identF[0:1, 0:1])
                i = t.transpose(bps[:, ph * 2 + 1:ph * 2 + 2],
                                nkbar_row[0:1, ph * 256 + 1: ph * 256 + 256: 2], identF[0:1, 0:1])
            return i

        au.op("tensor", biasfn, reads=[("b_row",), ("nkbar_row",), ("identF",)],
              writes=[("bps",)])

        def biasev(v):
            for ph in range(NP):
                v.tensor_copy(qb_col[:, ph:ph + 1], bps[:, ph * 2:ph * 2 + 1])
                i = v.tensor_copy(kb_col[:, ph:ph + 1], bps[:, ph * 2 + 1:ph * 2 + 2])
            return i

        au.op("vector", biasev, reads=[("bps",)], writes=[("qb_col",), ("kb_col",)])

    TQC = 512
    NTQ = T // TQC
    with ExitStack() as ps_d:
        kpair = ctx.sbuf("kpair", [128, 2, T], BF16)
        qsl = ctx.sbuf("qsl", [128, 2, TQC], BF16)
        vpair = ctx.sbuf("vpair", [128, 1, NT, 128], BF16)
        Eb = ctx.sbuf("Eb", [128, 3, 2, TQC], BF16)
        A1b = ctx.sbuf("A1b", [128, 3, 2, TQC], BF16)
        A2b = ctx.sbuf("A2b", [128, 3, 2, TQC], BF16)
        othf = ctx.sbuf("othf", [128, 2, TQC], BF16)

        qps = ctx.psum("qps", [128, 512], F32, stack=ps_d)
        sps = [ctx.psum("spsA", [128, 2, 512], F32, stack=ps_d),
               ctx.psum("spsB", [128, 2, 512], F32, stack=ps_d),
               ctx.psum("spsC", [128, 2, 512], F32, stack=ps_d)]
        opsum = ctx.psum("opsum", [128, 512], F32, stack=ps_d)
        kpps = [qps, qps]

        XBD = [("xbd", c) for c in range(NT)]
        for ph in range(NP):
            bp = 0
            au.op("gpsimd", lambda g, ph=ph, bp=bp: g.dma_start(
                out=vpair[:, bp, :, :],
                in_=xb_dram.rearrange("(tk p) d -> p tk d", p=128)[:, :, ph * 128:(ph + 1) * 128]),
                reads=XBD, writes=[("vpair", bp)], dma=True, stream="vp0")
            for tc in range(T // 512):
                kb = tc % 2

                def kproj(t, ph=ph, tc=tc, kb=kb):
                    for d in range(ND):
                        st0, sp = (d == 0), (d == ND - 1)
                        t.matmul(kpps[kb][0:64, :], wqkT[:, d, ph * 256 + 1: ph * 256 + 128: 2],
                                 xT[:, d, tc * 512:(tc + 1) * 512], start=st0, stop=sp,
                                 tile_position=(0, 0), skip_group_check=True)
                        i = t.matmul(kpps[kb][64:128, :],
                                     wqkT[:, d, ph * 256 + 129: ph * 256 + 256: 2],
                                     xT[:, d, tc * 512:(tc + 1) * 512], start=st0, stop=sp,
                                     tile_position=(0, 64), skip_group_check=True)
                    return i

                au.op("tensor", kproj, reads=XT + [("wqkT",)], writes=[("qps",)])
                au.op("scalar", lambda s, bp=bp, ph=ph, tc=tc, kb=kb: s.activation(
                    kpair[:, bp, tc * 512:(tc + 1) * 512], kpps[kb][:, :], AF.Identity,
                    bias=kb_col[:, ph:ph + 1], scale=1.0),
                    reads=[("kb_col",)], xreads=[("qps",)], writes=[("kpair", bp, tc)])
            for tq in range(NTQ):
                bq = tq % 2

                def qproj(t, ph=ph, tq=tq):
                    col0 = tq * TQC
                    for d in range(ND):
                        st0, sp = (d == 0), (d == ND - 1)
                        t.matmul(qps[0:64, :], wqkT[:, d, ph * 256: ph * 256 + 128: 2],
                                 xT[:, d, col0:col0 + 512], start=st0, stop=sp,
                                 tile_position=(0, 0), skip_group_check=True)
                        i = t.matmul(qps[64:128, :],
                                     wqkT[:, d, ph * 256 + 128: ph * 256 + 256: 2],
                                     xT[:, d, col0:col0 + 512], start=st0, stop=sp,
                                     tile_position=(0, 64), skip_group_check=True)
                    return i

                au.op("tensor", qproj, reads=XT + [("wqkT",)], writes=[("qps",)])
                au.op("scalar", lambda s, bq=bq, ph=ph: s.activation(
                    qsl[:, bq, :], qps[:, :], AF.Identity,
                    bias=qb_col[:, ph:ph + 1], scale=1.0),
                    reads=[("qb_col",)], xreads=[("qps",)], writes=[("qsl", bq)])
                def mk_sim(tk):
                    bk = tk % 3

                    def simfn(t, bp=bp, bq=bq, tk=tk, bk=bk):
                        for h in range(2):
                            i = t.matmul(sps[bk][:, h, :],
                                         kpair[h * 64:(h + 1) * 64, bp, tk * 128:(tk + 1) * 128],
                                         qsl[h * 64:(h + 1) * 64, bq, :],
                                         start=True, stop=True, tile_position=(h * 64, 0),
                                         skip_group_check=True)
                        return i

                    au.op("tensor", simfn, reads=[("kpair", bp, tk // 4), ("qsl", bq)],
                          writes=[("sps", bk)])

                def mk_av(tk):
                    bk = tk % 3

                    def avfn(t, bp=bp, tk=tk, bk=bk):
                        for h in range(2):
                            st0 = (tk == 0)
                            t.matmul(opsum[h * 64:(h + 1) * 64, :],
                                     vpair[:, bp, tk, h * 64:(h + 1) * 64],
                                     A1b[:, bk, h, :],
                                     start=st0, stop=False, tile_position=(0, h * 64),
                                     skip_group_check=True)
                            i = t.matmul(opsum[h * 64:(h + 1) * 64, :],
                                         vpair[:, bp, tk, h * 64:(h + 1) * 64],
                                         A2b[:, bk, h, :],
                                         start=False, stop=(tk == NT - 1), tile_position=(0, h * 64),
                                         skip_group_check=True)
                        return i

                    au.op("tensor", avfn,
                          reads=[("A1b", bk), ("A2b", bk), ("vpair", bp)],
                          writes=[("opsum",)] if (tk == 0 or tk == NT - 1) else [("opsum_acc",)])

                mk_sim(0)
                for tk in range(NT):
                    bk = tk % 3
                    au.op("scalar", lambda s, bk=bk: s.activation(
                        Eb[:, bk, :, :].rearrange("p a b -> p (a b)"),
                        sps[bk].ap().rearrange("p a b -> p (a b)"), AF.Exp,
                        bias=ctx.ccol[:, 0:1], scale=SCALE),
                        reads=[("ccol",)], xreads=[("sps", bk)], writes=[("Eb", bk)])
                    if tk % 6 == 5:
                        au.op("scalar", lambda s, bk=bk: s.activation(
                            A1b[:, bk, :, :].rearrange("p a b -> p (a b)"),
                            sps[bk].ap().rearrange("p a b -> p (a b)"), AF.Relu,
                            scale=LAM * SCALE),
                            reads=[], xreads=[("sps", bk)], writes=[("A1b", bk)])
                    else:
                        au.op("vector", lambda v, bk=bk: v.tensor_scalar(
                            A1b[:, bk, :, :].rearrange("p a b -> p (a b)"),
                            sps[bk].ap().rearrange("p a b -> p (a b)"),
                            LAM * SCALE, 0.0, OP.mult, OP.max),
                            reads=[], xreads=[("sps", bk)], writes=[("A1b", bk)])
                    au.op("vector", lambda v, bk=bk: v.tensor_scalar(
                        A2b[:, bk, :, :].rearrange("p a b -> p (a b)"),
                        Eb[:, bk, :, :].rearrange("p a b -> p (a b)"), LA, None, OP.min),
                        reads=[("Eb", bk)], writes=[("A2b", bk)])
                    if tk + 1 < NT:
                        mk_sim(tk + 1)
                    if tk >= 2:
                        mk_av(tk - 2)
                mk_av(NT - 2)
                mk_av(NT - 1)
                bq2 = tq % 2
                au.op("scalar", lambda s, ph=ph, bq2=bq2: s.activation(
                    othf[:, bq2, :], opsum[:, :], AF.Identity,
                    bias=ov_col[:, ph:ph + 1], scale=OSCALE),
                    reads=[("ov_col",)], xreads=[("opsum",)], writes=[("othf", bq2)])
                au.op("sync", lambda sy, ph=ph, tq=tq, bq2=bq2: sy.dma_start(
                    out=attnT_dram[ph, :, tq * TQC:(tq + 1) * TQC], in_=othf[:, bq2, :]),
                    reads=[("othf", bq2)], writes=[("attnTd", ph, tq)], dma=True, stream=f"ost{bq2}")

    if 'E' not in stages:
        return []
    with ExitStack() as ps_e:
        atile = ctx.sbuf("atile", [128, 2, NP, 128], BF16)
        rtile = ctx.sbuf("rtile", [128, 2, D], F32)
        xf32 = ctx.sbuf("xf32", [128, 2, D], F32)
        sqs = ctx.sbuf("sqs", [128, 2, D], BF16)
        mu = ctx.sbuf("mu", [128, 2], F32)
        ssq = ctx.sbuf("ssq", [128, 2], F32)
        rstd = ctx.sbuf("rstd", [128, 2], F32)
        mb = ctx.sbuf("mb", [128, 2], F32)
        ttp = [ctx.psum("ttpA", [128, 1024], BF16, stack=ps_e),
               ctx.psum("ttpB", [128, 1024], BF16, stack=ps_e)]
        lnbufs = [(sqs[:, b, :], mu[:, b:b + 1], ssq[:, b:b + 1], rstd[:, b:b + 1],
                   mb[:, b:b + 1]) for b in range(2)]

        for c in range(NT):
            b = c % 2
            tqc = c // 4
            au.op("gpsimd", lambda g, c=c, b=b: g.dma_start(
                out=xf32[:, b, :], in_=x_dram[c * 128:(c + 1) * 128, :]),
                writes=[("xf32", b)], dma=True, stream=f"xf{b}")
            au.op("gpsimd", lambda g, c=c, b=b: g.dma_start(
                out=atile[:, b, :, :],
                in_=attnT_dram.rearrange("h p t -> p h t")[:, :, c * 128:(c + 1) * 128]),
                reads=[("attnTd", ph, tqc) for ph in range(NP)],
                writes=[("atile", b)], dma=True, stream=f"at{b}")
            for dp in range(ND // 2):
                pb = dp % 2

                def ttfn(t, c=c, dp=dp, pb=pb, b=b):
                    t.transpose(ttp[pb][:, 0:128], atile[:, b, 2 * dp, :], identB[:, :])
                    return t.transpose(ttp[pb][:, 128:256], atile[:, b, 2 * dp + 1, :], identB[:, :])

                au.op("tensor", ttfn, reads=[("atile", b), ("identB",)], writes=[("ttp", pb)])

                def addfn(v, dp=dp, pb=pb, b=b):
                    v.tensor_add(rtile[:, b, (2 * dp) * 128:(2 * dp + 1) * 128],
                                 xf32[:, b, (2 * dp) * 128:(2 * dp + 1) * 128], ttp[pb][:, 0:128])
                    return v.tensor_add(rtile[:, b, (2 * dp + 1) * 128:(2 * dp + 2) * 128],
                                        xf32[:, b, (2 * dp + 1) * 128:(2 * dp + 2) * 128],
                                        ttp[pb][:, 128:256])

                au.op("vector", addfn, reads=[("ttp", pb), ("xf32", b)],
                      writes=[("rtile", b, dp)])
            rkeys = [("rtile", b, dp) for dp in range(ND // 2)]
            emit_ln_inplace(ctx, au, rtile[:, b, :], rkeys, lnbufs[b], tag=b)
            au.op("sync", lambda sy, c=c, b=b: sy.dma_start(
                out=r1_dram[c * 128:(c + 1) * 128, :], in_=rtile[:, b, :]),
                reads=rkeys, writes=[("r1d", c)], dma=True, stream=f"r1st{b}")
    return [("r1d", c) for c in range(NT)]


BIGIDX = 4000000.0
KCAP = 1024
NE = 8


def _breg(ctx, g):
    if not hasattr(ctx, "breg"):
        ctx.breg = g.to_reg(KCAP - 1)
    return ctx.breg


def emit_p1(ctx, au, r1_dram, gw_dram, w1_dram, w2_dram, cx_dram, y_dram, r2_dram):
    nc = ctx.nc
    identB, identF = ctx.identB, ctx.identF
    from concourse.bass import IndirectOffsetOnAxis

    mprob = ctx.sbuf("mprob", [128, NT, NE], F32)
    idxu = ctx.sbuf("idxu", [128, NT, NE], U32)
    _p1s = ExitStack()
    probs = ctx.sbuf("probs", [128, NT, NE], F32, stack=_p1s)
    probsE = ctx.sbuf("probsE", [8, NT * 128], F32, stack=_p1s)
    masksA = ctx.sbuf("masksA", [128, NT, NE], BF16, stack=_p1s)
    gwT = ctx.sbuf("gwT", [128, ND, NE], F32, stack=_p1s)
    gwsb = ctx.sbuf("gwsb", [8, D], F32, stack=_p1s)
    Umat = ctx.sbuf("Umat", [128, 128], BF16, stack=_p1s)
    onescolB = ctx.sbuf("onescolB", [128, 1], BF16, stack=_p1s)
    onesrowF = ctx.sbuf("onesrowF", [1, 128], F32, stack=_p1s)
    cnts = ctx.sbuf("cnts", [1, NT * NE], F32, stack=_p1s)
    offs = ctx.sbuf("offs", [1, 2, NT * NE], F32, stack=_p1s)
    offs_ex = ctx.sbuf("offs_ex", [1, NT * NE], F32, stack=_p1s)
    tau = ctx.sbuf("tau", [8, 8], F32, stack=_p1s)
    taurow = ctx.sbuf("taurow", [1, NE], F32, stack=_p1s)
    taubc = ctx.sbuf("taubc", [128, NE], F32, stack=_p1s)
    r1t = ctx.sbuf("r1t", [128, 2, D], F32, stack=_p1s)
    rtT = ctx.sbuf("rtT", [128, 2, ND, 128], F32, stack=_p1s)
    scs = ctx.sbuf("scs", [128, 2, NE], F32, stack=_p1s)
    smx = ctx.sbuf("smx", [128, 2, 4], F32, stack=_p1s)
    maskf = ctx.sbuf("maskf", [128, 2, NE], F32, stack=_p1s)
    post = ctx.sbuf("post", [128, 2, NE], F32, stack=_p1s)
    idxf = ctx.sbuf("idxf", [128, 2, NE], F32, stack=_p1s)
    rep = ctx.sbuf("rep", [128, NT * 128], F32, stack=_p1s)
    scr = ctx.sbuf("scr", [128, NT * 128], BF16, stack=_p1s)
    sel2 = ctx.sbuf("sel2", [128, 8], F32, stack=_p1s)
    selrepT = ctx.sbuf("selrepT", [8, 128], F32, stack=_p1s)
    iot8 = ctx.sbuf("iot8", [128, 8], F32, stack=_p1s)
    srt8 = ctx.sbuf("srt8", [8, 128], F32, stack=_p1s)
    bcols = ctx.sbuf("bcols", [128, 10], F32, stack=_p1s)
    bsb = ctx.sbuf("bsb", [8, 1], F32, stack=_p1s)
    shtb = ctx.sbuf("shtb", [128, 8, 16], BF16, stack=_p1s)
    predb = ctx.sbuf("predb", [128, 1], BF16, stack=_p1s)

    au.op("gpsimd", lambda g: g.iota(Umat[:, :], pattern=[[1, 128]], base=0,
                                     channel_multiplier=-1, allow_small_or_imprecise_dtypes=True),
          writes=[("Umat",)])
    au.op("vector", lambda v: v.tensor_scalar(Umat[:, :], Umat[:, :], 0.0, None, OP.is_ge),
          reads=[("Umat",)], writes=[("Umat",)])
    au.op("gpsimd", lambda g: g.iota(onescolB[:, :], pattern=[[0, 1]], base=1,
                                     channel_multiplier=0, allow_small_or_imprecise_dtypes=True),
          writes=[("onescolB",)])
    au.op("gpsimd", lambda g: g.iota(onesrowF[:, :], pattern=[[0, 128]], base=1,
                                     channel_multiplier=0, allow_small_or_imprecise_dtypes=True),
          writes=[("onesrowF",)])
    au.op("gpsimd", lambda g: g.dma_start(out=gwsb[:, :], in_=gw_dram[:, :]),
          writes=[("gwsb",)], dma=True, stream="p1gw")


    with ExitStack() as psA:
        tpg = ctx.psum("tpg", [128, 512], F32, stack=psA)
        scp = ctx.psum("scp", [128, 512], F32, stack=psA)
        tpp = [ctx.psum("tppA", [128, 512], F32, stack=psA),
               ctx.psum("tppB", [128, 512], F32, stack=psA)]
        pep = ctx.psum("pep", [128, 512], F32, stack=psA)
        aux1 = ctx.psum("aux1", [128, 512], F32, stack=psA)
        aux2 = ctx.psum("aux2", [128, 512], F32, stack=psA)
        aux3 = ctx.psum("aux3", [128, 512], F32, stack=psA)

        def gwtfn(t):
            for d in range(ND):
                i = t.transpose(tpg[:, d * 8:(d + 1) * 8], gwsb[:, d * 128:(d + 1) * 128],
                                identF[0:8, 0:8])
            return i
        au.op("tensor", gwtfn, reads=[("gwsb",), ("identF",)], writes=[("tpg",)])

        def gwevf(v):
            for d in range(ND):
                i = v.tensor_copy(gwT[:, d, :], tpg[:, d * 8:(d + 1) * 8])
            return i
        au.op("vector", gwevf, reads=[("tpg",)], writes=[("gwT",)])

        for c in range(NT):
            b = c % 2
            au.op("gpsimd", lambda g, c=c, b=b: g.dma_start(
                out=r1t[:, b, :], in_=r1_dram[c * 128:(c + 1) * 128, :]),
                writes=[("r1t", b)], dma=True, stream=f"p1r1{b}")
            for dp in range(ND // 2):
                def tpfn(t, b=b, dp=dp):
                    t.transpose(tpp[b][:, 0:128], r1t[:, b, (2 * dp) * 128:(2 * dp + 1) * 128],
                                identF[:, :])
                    return t.transpose(tpp[b][:, 128:256],
                                       r1t[:, b, (2 * dp + 1) * 128:(2 * dp + 2) * 128],
                                       identF[:, :])
                au.op("tensor", tpfn, reads=[("r1t", b), ("identF",)], writes=[("tpp", b)])

                def tpev(v, b=b, dp=dp):
                    v.tensor_copy(rtT[:, b, 2 * dp, :], tpp[b][:, 0:128])
                    return v.tensor_copy(rtT[:, b, 2 * dp + 1, :], tpp[b][:, 128:256])
                au.op("vector", tpev, reads=[("tpp", b)], writes=[("rtT", b, dp)])
            rT = [("rtT", b, dp) for dp in range(ND // 2)]

            def scfn(t, b=b):
                for d in range(ND):
                    i = t.matmul(scp[:, 0:NE], rtT[:, b, d, :], gwT[:, d, :],
                                 start=(d == 0), stop=(d == ND - 1))
                return i
            au.op("tensor", scfn, reads=rT + [("gwT",)], writes=[("scp",)])
            au.op("vector", lambda v, b=b: v.reduce_max(smx[:, b, 0:1], scp[:, 0:NE], axis=AX.X),
                  xreads=[("scp",)], writes=[("smx", b, 0)])
            au.op("vector", lambda v, b=b: v.tensor_scalar(smx[:, b, 1:2], smx[:, b, 0:1],
                                                           -1.0, None, OP.mult),
                  reads=[("smx", b, 0)], writes=[("smx", b, 1)])
            au.op("scalar", lambda s, b=b: s.activation(scs[:, b, :], scp[:, 0:NE], AF.Exp,
                                                        bias=smx[:, b, 1:2], scale=1.0),
                  reads=[("smx", b, 1)], xreads=[("scp",)], writes=[("scs", b)])
            au.op("vector", lambda v, b=b: v.reduce_sum(smx[:, b, 2:3], scs[:, b, :], axis=AX.X),
                  reads=[("scs", b)], writes=[("smx", b, 2)])
            au.op("vector", lambda v, b=b: v.reciprocal(smx[:, b, 3:4], smx[:, b, 2:3]),
                  reads=[("smx", b, 2)], writes=[("smx", b, 3)])
            au.op("vector", lambda v, c=c, b=b: v.tensor_scalar(probs[:, c, :], scs[:, b, :],
                                                                smx[:, b, 3:4], None, OP.mult),
                  reads=[("scs", b), ("smx", b, 3)], writes=[("probs", c)])
            au.op("tensor", lambda t, c=c: t.transpose(pep[0:8, 0:128], probs[:, c, :],
                                                       identF[:, :]),
                  reads=[("probs", c), ("identF",)], writes=[("pep",)])
            au.op("vector", lambda v, c=c: v.tensor_copy(probsE[0:8, c * 128:(c + 1) * 128],
                                                         pep[0:8, 0:128]),
                  reads=[("pep",)], writes=[("probsE", c)])

        PRE = [("probsE", c) for c in range(NT)]
        au.op("gpsimd", lambda g: g.iota(bcols[:, 0:1], pattern=[[0, 1]], base=0,
                                         channel_multiplier=1,
                                         allow_small_or_imprecise_dtypes=True),
              writes=[("bc", 0)])
        au.op("gpsimd", lambda g: g.iota(iot8[:, :], pattern=[[-16, 8]], base=0,
                                         channel_multiplier=1,
                                         allow_small_or_imprecise_dtypes=True),
              writes=[("iot8",)])
        au.op("gpsimd", lambda g: g.iota(srt8[:, :], pattern=[[1, 128]], base=0,
                                         channel_multiplier=-16,
                                         allow_small_or_imprecise_dtypes=True),
              writes=[("srt8",)])
        au.op("vector", lambda v: v.tensor_scalar(sel2[:, :], iot8[:, :], 0.0, None, OP.is_ge),
              reads=[("iot8",)], writes=[("sel2",)])
        au.op("vector", lambda v: v.tensor_scalar(iot8[:, :], iot8[:, :], 16.0, 0.0,
                                                  OP.is_ge, OP.add,
                                                  accum_out=bcols[:, 1:2]),
              reads=[("iot8",)], writes=[("iot8",), ("bc", 1)])
        au.op("vector", lambda v: v.tensor_sub(sel2[:, :], sel2[:, :], iot8[:, :]),
              reads=[("sel2",), ("iot8",)], writes=[("sel2",)])
        au.op("vector", lambda v: v.scalar_tensor_tensor(bcols[:, 2:3], bcols[:, 1:2], -16.0,
                                                         bcols[:, 0:1], OP.mult, OP.add),
              reads=[("bc", 1), ("bc", 0)], writes=[("bc", 2)])
        au.op("vector", lambda v: v.tensor_scalar(selrepT[:, :], srt8[:, :], 0.0, None, OP.is_ge),
              reads=[("srt8",)], writes=[("selrepT",)])
        au.op("vector", lambda v: v.tensor_scalar(srt8[:, :], srt8[:, :], 16.0, None, OP.is_ge),
              reads=[("srt8",)], writes=[("srt8",)])
        au.op("vector", lambda v: v.tensor_sub(selrepT[:, :], selrepT[:, :], srt8[:, :]),
              reads=[("selrepT",)] + [("srt8",)], writes=[("selrepT",)])
        for cc in range(8):
            ax = aux1 if cc % 2 == 0 else aux2
            axk = ("aux1",) if cc % 2 == 0 else ("aux2",)
            au.op("tensor", lambda t, cc=cc, ax=ax: t.matmul(
                ax[:, :], selrepT[0:8, :], probsE[0:8, cc * 512:(cc + 1) * 512],
                start=True, stop=True),
                reads=[("selrepT",)] + PRE, writes=[axk])
            au.op("vector", lambda v, cc=cc, ax=ax: v.tensor_copy(
                rep[:, cc * 512:(cc + 1) * 512], ax[:, :]),
                reads=[axk], writes=[("rep", cc)])
        REPK = [("rep", cc) for cc in range(8)]
        au.op("gpsimd", lambda g: g.iota(shtb[:, :, :], pattern=[[16, 8], [1, 16]], base=0,
                                         channel_multiplier=-1,
                                         allow_small_or_imprecise_dtypes=True),
              writes=[("shtb",)])
        au.op("vector", lambda v: v.tensor_scalar(shtb[:, :, :], shtb[:, :, :], -1.0, None,
                                                  OP.is_equal),
              reads=[("shtb",)], writes=[("shtb",)])
        au.op("vector", lambda v: v.memset(shtb[:, :, 15:16], 0.0),
              reads=[("shtb",)], writes=[("shtb",)])
        au.op("vector", lambda v: v.memset(bcols[:, 3:4], 0.0), writes=[("bc", 3)])
        au.op("vector", lambda v: v.memset(bcols[:, 4:5], 1.0 / 16.0), writes=[("bc", 4)])
        shtf = shtb.ap().rearrange("p a b -> p (a b)")
        for it in range(7):
            au.op("scalar", lambda s: s.activation(bcols[:, 6:7], bcols[:, 2:3], AF.Identity,
                                                   bias=bcols[:, 3:4], scale=bcols[:, 4:5]),
                  reads=[("bc", 2), ("bc", 3), ("bc", 4)], writes=[("bc", 6)])
            au.op("vector", lambda v: v.tensor_scalar(scr[:, :], rep[:, :], bcols[:, 6:7],
                                                      0.0, OP.is_ge, OP.add,
                                                      accum_out=bcols[:, 7:8]),
                  reads=REPK + [("bc", 6)], writes=[("scr",), ("bc", 7)])
            au.op("vector", lambda v: v.tensor_scalar(bcols[:, 8:9], bcols[:, 7:8], float(KCAP),
                                                      None, OP.is_ge),
                  reads=[("bc", 7)], writes=[("bc", 8)])
            au.op("vector", lambda v: v.tensor_scalar(predb[:, :], bcols[:, 7:8], float(KCAP),
                                                      None, OP.is_ge),
                  reads=[("bc", 7)], writes=[("predb",)])
            au.op("tensor", lambda t: t.matmul(aux3[:, 0:1], shtf, predb[:, :],
                                               start=True, stop=True),
                  reads=[("shtb",), ("predb",)], writes=[("aux3",)])
            au.op("vector", lambda v: v.tensor_copy(bcols[:, 5:6], aux3[:, 0:1]),
                  reads=[("aux3",)], writes=[("bc", 5)])
            au.op("vector", lambda v: v.tensor_sub(bcols[:, 1:2], bcols[:, 8:9], bcols[:, 5:6]),
                  reads=[("bc", 8), ("bc", 5)], writes=[("bc", 1)])
            au.op("vector", lambda v: v.tensor_mul(bcols[:, 9:10], bcols[:, 1:2], bcols[:, 6:7]),
                  reads=[("bc", 1), ("bc", 6)], writes=[("bc", 9)])
            au.op("tensor", lambda t: t.matmul(aux1[0:8, 0:1], sel2[:, :], bcols[:, 9:10],
                                               start=True, stop=True),
                  reads=[("sel2",), ("bc", 9)], writes=[("aux1",)])
            au.op("vector", lambda v: v.tensor_copy(bsb[:, :], aux1[0:8, 0:1]),
                  reads=[("aux1",)], writes=[("bsb",)])
            au.op("tensor", lambda t: t.matmul(aux2[:, 0:1], selrepT[0:8, :], bsb[0:8, :],
                                               start=True, stop=True),
                  reads=[("selrepT",), ("bsb",)], writes=[("aux2",)])
            au.op("vector", lambda v: v.tensor_copy(bcols[:, 3:4], aux2[:, 0:1]),
                  reads=[("aux2",)], writes=[("bc", 3)])
            au.op("vector", lambda v: v.tensor_scalar(bcols[:, 4:5], bcols[:, 4:5], 1.0 / 16.0,
                                                      None, OP.mult),
                  reads=[("bc", 4)], writes=[("bc", 4)])
        au.op("tensor", lambda t: t.matmul(aux3[0:8, 0:1], sel2[:, :], bcols[:, 3:4],
                                           start=True, stop=True),
              reads=[("sel2",), ("bc", 3)], writes=[("aux3",)])
        au.op("vector", lambda v: v.tensor_scalar(tau[:, 0:1], aux3[0:8, 0:1], 1.0 / 16.0,
                                                  None, OP.mult),
              reads=[("aux3",)], writes=[("tau", 0)])
        au.op("tensor", lambda t: t.transpose(pep[0:1, 0:8], tau[0:8, 0:1], identF[0:8, 0:8]),
              reads=[("tau", 0), ("identF",)], writes=[("pep",)])
        au.op("vector", lambda v: v.tensor_copy(taurow[:, :], pep[0:1, 0:8]),
              reads=[("pep",)], writes=[("taurow",)])
        au.op("tensor", lambda t: t.matmul(aux1[:, 0:8], onesrowF[:, :], taurow[:, :],
                                           start=True, stop=True),
              reads=[("onesrowF",), ("taurow",)], writes=[("aux1",)])
        au.op("vector", lambda v: v.tensor_copy(taubc[:, :], aux1[:, 0:8]),
              reads=[("aux1",)], writes=[("taubc",)])

        for c in range(NT):
            b = c % 2
            au.op("vector", lambda v, c=c, b=b: v.tensor_tensor(maskf[:, b, :], probs[:, c, :],
                                                                taubc[:, :], OP.is_ge),
                  reads=[("probs", c), ("taubc",)], writes=[("maskf", b)])
            au.op("vector", lambda v, c=c, b=b: v.tensor_mul(mprob[:, c, :], probs[:, c, :],
                                                             maskf[:, b, :]),
                  reads=[("probs", c), ("maskf", b)], writes=[("mprob", c)])
            au.op("vector", lambda v, c=c, b=b: v.tensor_copy(masksA[:, c, :], maskf[:, b, :]),
                  reads=[("maskf", b)], writes=[("masksA", c)])
            au.op("tensor", lambda t, c=c: t.matmul(aux2[0:1, 0:8], onescolB[:, :], masksA[:, c, :],
                                                    start=True, stop=True),
                  reads=[("masksA", c), ("onescolB",)], writes=[("aux2",)])
            au.op("vector", lambda v, c=c: v.tensor_copy(cnts[0:1, c * NE:(c + 1) * NE],
                                                         aux2[0:1, 0:8]),
                  reads=[("aux2",)], writes=[("cnts", c)])
        CN = [("cnts", c) for c in range(NT)]
        au.op("vector", lambda v: v.tensor_copy(offs[:, 0, :], cnts[:, :]),
              reads=CN, writes=[("offs", 0)])
        par = 0
        for sh in (8, 16, 32, 64, 128):
            def shfn(v, sh=sh, par=par):
                v.tensor_copy(offs[:, 1 - par, 0:sh], offs[:, par, 0:sh])
                return v.tensor_add(offs[:, 1 - par, sh:NT * NE], offs[:, par, sh:NT * NE],
                                    offs[:, par, 0:NT * NE - sh])
            au.op("vector", shfn, reads=[("offs", par)], writes=[("offs", 1 - par)])
            par = 1 - par
        au.op("vector", lambda v, par=par: v.tensor_sub(offs_ex[:, :], offs[:, par, :], cnts[:, :]),
              reads=[("offs", par)] + CN, writes=[("offs_ex",)])

        for c in range(NT):
            b = c % 2

            def posfn(t, c=c):
                t.matmul(aux3[:, 0:8], Umat[:, :], masksA[:, c, :], start=True, stop=False)
                return t.matmul(aux3[:, 0:8], onesrowF[:, :], offs_ex[0:1, c * NE:(c + 1) * NE],
                                start=False, stop=True)
            au.op("tensor", posfn, reads=[("masksA", c), ("Umat",), ("offs_ex",), ("onesrowF",)],
                  writes=[("aux3",)])
            au.op("vector", lambda v, b=b: v.tensor_copy(post[:, b, :], aux3[:, 0:8]),
                  reads=[("aux3",)], writes=[("post", b)])
            au.op("vector", lambda v, c=c, b=b: v.scalar_tensor_tensor(
                idxf[:, b, :], masksA[:, c, :], -BIGIDX, post[:, b, :], OP.mult, OP.add),
                reads=[("masksA", c), ("post", b)], writes=[("idxf", b)])
            au.op("vector", lambda v, b=b: v.tensor_scalar(idxf[:, b, :], idxf[:, b, :],
                                                           BIGIDX - 1.0, None, OP.add),
                  reads=[("idxf", b)], writes=[("idxf", b)])
            au.op("vector", lambda v, c=c, b=b: v.tensor_copy(idxu[:, c, :], idxf[:, b, :]),
                  reads=[("idxf", b)], writes=[("idxu", c)])
            for e in range(NE):
                au.op("gpsimd", lambda g, c=c, b=b, e=e: g.indirect_dma_start(
                    out=cx_dram.rearrange("e k d -> (e k) d"),
                    out_offset=IndirectOffsetOnAxis(ap=idxu[:, c, e:e + 1], axis=0),
                    in_=r1t[:, b, :], in_offset=None, element_offset=e * KCAP * D,
                    bounds_check=_breg(ctx, g), oob_is_err=False),
                    reads=[("idxu", c), ("r1t", b)], writes=[("cxd", e, c)],
                    dma=True, stream=f"p1sc{b}{e}")
    au.fence()
    _p1s.close()
    return probs, mprob, idxu


def emit_p1_experts(ctx, au, w1_dram, w2_dram, cx_dram, y_dram, r1_dram, idxu):
    from concourse.bass import IndirectOffsetOnAxis
    identB = ctx.identB
    r1b = ctx.sbuf("r1b", [128, NT, D], BF16)
    cxsb = ctx.sbuf("cxsb", [128, 1, 8, KCAP], BF16)
    cxT = ctx.sbuf("cxT", [128, ND, KCAP], BF16)
    hsb = ctx.sbuf("hsb", [128, 32, KCAP], BF16)
    w1c = ctx.sbuf("w1c", [128, 2, ND, 512], BF16)
    w2c = ctx.sbuf("w2c", [128, 2, 2, D], BF16)
    ysb = ctx.sbuf("ysb", [128, 4, D], F32)
    au.op("gpsimd", lambda g: g.dma_start(
        out=r1b[:, :, :], in_=r1_dram.rearrange("(c p) d -> p c d", p=128)),
        writes=[("r1b",)], dma=True, stream="p1r1b")

    def emit_scatters(e):
        for c in range(NT):
            au.op("gpsimd", lambda g, c=c, e=e: g.indirect_dma_start(
                out=cx_dram.rearrange("e k d -> (e k) d"),
                out_offset=IndirectOffsetOnAxis(ap=idxu[:, c, e:e + 1], axis=0),
                in_=r1b[:, c, :], in_offset=None, element_offset=e * KCAP * D,
                bounds_check=_breg(ctx, g), oob_is_err=False),
                reads=[("idxu", c), ("r1b",)], writes=[("cxd", e, c % 4)],
                dma=True, stream=f"p1sc{e % 2}")

    CXD = lambda e: [("cxd", e, j) for j in range(4)]
    emit_scatters(0)
    for e in range(NE):
        with ExitStack() as psE:
            ctp = [ctx.psum(f"ctpA{e}", [128, 1024], BF16, stack=psE),
                   ctx.psum(f"ctpB{e}", [128, 1024], BF16, stack=psE)]
            hps = [ctx.psum(f"hpsA{e}", [128, 2, 512], F32, stack=psE),
                   ctx.psum(f"hpsB{e}", [128, 2, 512], F32, stack=psE)]
            if e + 1 < NE:
                emit_scatters(e + 1)
            for d in range(ND):
                au.op("sync", lambda sy, e=e, d=d: sy.dma_start(
                    out=cxT[:, d, :], in_=cx_dram[e][:, d * 128:(d + 1) * 128], transpose=True),
                    reads=CXD(e) + ([("yev", e - 1)] if e > 0 else []),
                    writes=[("cxT", d)], dma=True, stream="p1ctr")
            CXT = [("cxT", d) for d in range(ND)]
            for fc in range(8):
                wb = fc % 2
                au.op("gpsimd", lambda g, e=e, fc=fc, wb=wb: g.dma_start(
                    out=w1c[:, wb, :, :],
                    in_=w1_dram[e].rearrange("(dt p) f -> p dt f", p=128)[:, :,
                                                                         fc * 512:(fc + 1) * 512]),
                    writes=[("w1c", wb)], dma=True, stream=f"p1w1{wb}")
                for fcc in range(4):
                    hb = fcc % 2

                    def hmm(t, fc=fc, fcc=fcc, wb=wb, hb=hb):
                        for d in range(ND):
                            for tc in range(2):
                                i = t.matmul(hps[hb][:, tc, :],
                                             w1c[:, wb, d, fcc * 128:(fcc + 1) * 128],
                                             cxT[:, d, tc * 512:(tc + 1) * 512],
                                             start=(d == 0), stop=(d == ND - 1))
                        return i
                    au.op("tensor", hmm, reads=CXT + [("w1c", wb)], writes=[("hps", hb)])
                    ft = fc * 4 + fcc
                    au.op("scalar", lambda s, ft=ft, hb=hb: s.activation(
                        hsb[:, ft, :], hps[hb].ap().rearrange("p a b -> p (a b)"), AF.Gelu),
                        xreads=[("hps", hb)], writes=[("hsb", ft)])
            HS = [("hsb", ft) for ft in range(32)]
        with ExitStack() as psY:
            yps = [ctx.psum(f"yps{e}_{j}", [128, 2, 512], F32, stack=psY) for j in range(4)]
            for half in range(2):
                for fp_ in range(16):
                    wb = fp_ % 2
                    au.op("gpsimd", lambda g, e=e, fp_=fp_, wb=wb: g.dma_start(
                        out=w2c[:, wb, :, :],
                        in_=w2_dram[e, fp_ * 256:(fp_ + 1) * 256, :].rearrange(
                            "(a p) d -> p a d", p=128)),
                        writes=[("w2c", wb)], dma=True, stream=f"p1w2{wb}")
                    for fq in range(2):
                        ft = fp_ * 2 + fq

                        def ymm(t, half=half, ft=ft, wb=wb, fq=fq):
                            for j in range(4):
                                tj = half * 4 + j
                                for dc in range(2):
                                    i = t.matmul(yps[j][:, dc, :],
                                                 hsb[:, ft, tj * 128:(tj + 1) * 128],
                                                 w2c[:, wb, fq, dc * 512:(dc + 1) * 512],
                                                 start=(ft == 0), stop=(ft == 31))
                            return i
                        au.op("tensor", ymm,
                              reads=[("hsb", ft), ("w2c", wb)] + (HS if ft == 0 else []),
                              writes=[("yps",)] if (ft == 0 or ft == 31) else [("yacc",)])
                for j in range(4):
                    tj = half * 4 + j
                    au.op("vector", lambda v, j=j: v.tensor_copy(
                        ysb[:, j, :], yps[j].ap().rearrange("p a b -> p (a b)")),
                        xreads=[("yps",)], writes=[("ysbuf", j), ("yev", e)])
                for j in range(4):
                    tj = half * 4 + j
                    au.op("sync", lambda sy, e=e, tj=tj, j=j: sy.dma_start(
                        out=y_dram[e, tj * 128:(tj + 1) * 128, :], in_=ysb[:, j, :]),
                        reads=[("ysbuf", j)], writes=[("yd", e, tj)], dma=True,
                        stream=f"p1y{j}")


def emit_p1_tail(ctx, au, r1_dram, y_dram, r2_dram, mprob, idxu, after_tile=None):
    from concourse.bass import IndirectOffsetOnAxis
    rtile = ctx.sbuf("p1rt", [128, 3, D], F32)
    tmpg = ctx.sbuf("tmpg", [128, 4, D], F32)
    sqs = ctx.sbuf("p1sqs", [128, 3, D], BF16)
    mu = ctx.sbuf("p1mu", [128, 3], F32)
    ssq = ctx.sbuf("p1ssq", [128, 3], F32)
    rstd = ctx.sbuf("p1rstd", [128, 3], F32)
    mb = ctx.sbuf("p1mb", [128, 3], F32)
    lnbufs = [(sqs[:, b, :], mu[:, b:b + 1], ssq[:, b:b + 1], rstd[:, b:b + 1],
               mb[:, b:b + 1]) for b in range(3)]
    au.op("vector", lambda v: v.memset(tmpg[:, :, :], 0.0),
          writes=[("tmpg", 0), ("tmpg", 1), ("tmpg", 2), ("tmpg", 3)])
    YD = [("yd", e, tj) for e in range(NE) for tj in range(8)]
    for c in range(NT):
        b = c % 3
        au.op("gpsimd", lambda g, c=c, b=b: g.dma_start(
            out=rtile[:, b, :], in_=r1_dram[c * 128:(c + 1) * 128, :]),
            writes=[("p1rt", b), ("rk", b, 0), ("rk", b, 1), ("rk", b, 2), ("rk", b, 3)],
            dma=True, stream=f"p1f{b}")
        for e in range(NE):
            ge = e % 4
            au.op("gpsimd", lambda g, c=c, e=e, ge=ge: g.indirect_dma_start(
                out=tmpg[:, ge, :], out_offset=None,
                in_=y_dram.rearrange("e k d -> (e k) d"),
                in_offset=IndirectOffsetOnAxis(ap=idxu[:, c, e:e + 1], axis=0),
                element_offset=e * KCAP * D,
                bounds_check=_breg(ctx, g), oob_is_err=False),
                reads=[("idxu", c)] + (YD if c == 0 and e == 0 else [("yd", e, tj) for tj in range(8)]),
                writes=[("tmpg", ge)], dma=True, stream=f"p1g{e}")
            au.op("vector", lambda v, c=c, e=e, ge=ge, b=b: v.scalar_tensor_tensor(
                rtile[:, b, :], tmpg[:, ge, :], mprob[:, c, e:e + 1], rtile[:, b, :],
                OP.mult, OP.add),
                reads=[("tmpg", ge), ("mprob", c), ("p1rt", b), ("rk", b, 0), ("rk", b, 1),
                       ("rk", b, 2), ("rk", b, 3)],
                writes=[("rk", b, 0), ("rk", b, 1), ("rk", b, 2), ("rk", b, 3)])
        rkeys = [("rk", b, j) for j in range(4)]
        emit_ln_inplace(ctx, au, rtile[:, b, :], rkeys, lnbufs[b], tag=10 + b)
        au.op("sync", lambda sy, c=c, b=b: sy.dma_start(
            out=r2_dram[c * 128:(c + 1) * 128, :], in_=rtile[:, b, :]),
            reads=rkeys, writes=[("r2d", c)], dma=True, stream=f"p1r2{b}")
        if after_tile is not None:
            after_tile(c, rtile[:, b, :], rkeys)
    return [("r2d", c) for c in range(NT)]


def emit_p2(ctx, au, r2_dram, wq_dram, wk_dram, wv_dram, wo_dram, bo_dram,
            ct_dram, st_dram, qtm_dram, ktm_dram, vtm_dram, r3_dram, tail_hook_reg=None):
    nc = ctx.nc
    identB, identF = ctx.identB, ctx.identF

    wosb = ctx.sbuf("wosb", [128, ND, D], BF16)
    r2t = ctx.sbuf("r2t", [128, 2, D], F32)
    nmhat = ctx.sbuf("nmhat", [128, 1], F32)
    bobc = ctx.sbuf("bobc", [128, D], F32)
    tob = ctx.sbuf("tob", [128, 2, D], F32)
    sqs2 = ctx.sbuf("p2sqs", [128, 2, D], BF16)
    mu2 = ctx.sbuf("p2mu", [128, 2], F32)
    ssq2 = ctx.sbuf("p2ssq", [128, 2], F32)
    rstd2 = ctx.sbuf("p2rstd", [128, 2], F32)
    mb2 = ctx.sbuf("p2mb", [128, 2], F32)
    lnb2 = [(sqs2[:, b, :], mu2[:, b:b + 1], ssq2[:, b:b + 1], rstd2[:, b:b + 1],
             mb2[:, b:b + 1]) for b in range(2)]
    onesrF = ctx.sbuf("onesrF", [1, 128], F32)
    onescB2 = ctx.sbuf("onescB2", [128, 1], BF16)
    onesrB = ctx.sbuf("onesrB", [1, 128], BF16)
    sbA = ExitStack()
    wqsb = ctx.sbuf("wqsb", [128, ND, D], BF16, stack=sbA)
    wksb = ctx.sbuf("wksb", [128, ND, D], BF16, stack=sbA)
    wvsb = ctx.sbuf("wvsb", [128, ND, D], BF16, stack=sbA)
    rtTb = ctx.sbuf("rtTb", [128, 2, ND, 128], BF16, stack=sbA)
    ctab = ctx.sbuf("ctab", [128, 2, 512], BF16, stack=sbA)
    stab = ctx.sbuf("stab", [128, 2, 512], BF16, stack=sbA)
    rsc1 = ctx.sbuf("rsc1", [128, 512], BF16, stack=sbA)
    rsc2 = ctx.sbuf("rsc2", [128, 512], BF16, stack=sbA)
    qkraw = ctx.sbuf("qkraw", [128, D], BF16, stack=sbA)
    qtt = ctx.sbuf("qtt", [128, 2, D], BF16, stack=sbA)
    ktt = ctx.sbuf("ktt", [128, 2, D], BF16, stack=sbA)
    vtt = ctx.sbuf("vtt", [128, 2, D], BF16, stack=sbA)
    sqsc = ctx.sbuf("sqsc", [128, D], BF16, stack=sbA)
    nrm = ctx.sbuf("nrm", [128, 4], F32, stack=sbA)
    sc11 = ctx.sbuf("sc11", [1, 8], F32, stack=sbA)
    borow = ctx.sbuf("borow", [1, D], F32, stack=sbA)

    au.op("gpsimd", lambda g: g.iota(onesrF[:, :], pattern=[[0, 128]], base=1,
                                     channel_multiplier=0, allow_small_or_imprecise_dtypes=True),
          writes=[("onesrF",)])
    au.op("gpsimd", lambda g: g.iota(onescB2[:, :], pattern=[[0, 1]], base=1,
                                     channel_multiplier=0, allow_small_or_imprecise_dtypes=True),
          writes=[("onescB2",)])
    au.op("gpsimd", lambda g: g.iota(onesrB[:, :], pattern=[[0, 128]], base=1,
                                     channel_multiplier=0, allow_small_or_imprecise_dtypes=True),
          writes=[("onesrB",)])
    for nm, dst, key in (("wq", wqsb, "wqsb"), ("wk", wksb, "wksb"),
                         ("wv", wvsb, "wvsb"), ("wo", wosb, "wosb")):
        srcd = {"wq": wq_dram, "wk": wk_dram, "wv": wv_dram, "wo": wo_dram}[nm]
        au.op("gpsimd", lambda g, srcd=srcd, dst=dst: g.dma_start(
            out=dst[:, :, :], in_=srcd.rearrange("(dt p) c -> p dt c", p=128)),
            writes=[(key,)], dma=True, stream=f"p2w{nm}")
    au.op("gpsimd", lambda g: g.dma_start(out=borow[:, :], in_=bo_dram[None, :]),
          writes=[("borow",)], dma=True, stream="p2bo")
    au.op("vector", lambda v: v.memset(nrm[:, 1:3], 0.0), writes=[("nrm", 1), ("nrm", 2)])

    psA = ExitStack()
    tpa = ctx.psum("p2tpa", [128, 512], F32, stack=psA)
    qpp = ctx.psum("qpp", [128, 2, 512], F32, stack=psA)
    kpp = ctx.psum("kpp", [128, 2, 512], F32, stack=psA)
    vpp = ctx.psum("vpp", [128, 2, 512], F32, stack=psA)
    aux = ctx.psum("p2aux", [128, 512], F32, stack=psA)

    au.op("tensor", lambda t: t.matmul(aux[:, 0:512], onesrF[:, :], borow[0:1, 0:512],
                                       start=True, stop=True),
          reads=[("onesrF",), ("borow",)], writes=[("p2aux",)])
    au.op("vector", lambda v: v.tensor_copy(bobc[:, 0:512], aux[:, 0:512]),
          reads=[("p2aux",)], writes=[("bobc", 0)])
    au.op("tensor", lambda t: t.matmul(aux[:, 0:512], onesrF[:, :], borow[0:1, 512:1024],
                                       start=True, stop=True),
          reads=[("onesrF",), ("borow",)], writes=[("p2aux",)])
    au.op("vector", lambda v: v.tensor_copy(bobc[:, 512:1024], aux[:, 0:512]),
          reads=[("p2aux",)], writes=[("bobc", 1)])

    def tileA(c, src_ap, src_keys):
        b = c % 2
        au.op("gpsimd", lambda g, c=c, b=b: g.dma_start(
            out=ctab[:, b, :], in_=ct_dram[c * 128:(c + 1) * 128, :]),
            writes=[("ctab", b)], dma=True, stream=f"p2ct{b}")
        au.op("gpsimd", lambda g, c=c, b=b: g.dma_start(
            out=stab[:, b, :], in_=st_dram[c * 128:(c + 1) * 128, :]),
            writes=[("stab", b)], dma=True, stream=f"p2st{b}")
        for dp in range(ND // 2):
            def tpfn(t, b=b, dp=dp, src_ap=src_ap):
                t.transpose(tpa[:, 0:128], src_ap[:, (2 * dp) * 128:(2 * dp + 1) * 128],
                            identF[:, :])
                return t.transpose(tpa[:, 128:256],
                                   src_ap[:, (2 * dp + 1) * 128:(2 * dp + 2) * 128],
                                   identF[:, :])
            au.op("tensor", tpfn, reads=list(src_keys) + [("identF",)], writes=[("p2tpa",)])

            def tpev(v, b=b, dp=dp):
                v.tensor_copy(rtTb[:, b, 2 * dp, :], tpa[:, 0:128])
                return v.tensor_copy(rtTb[:, b, 2 * dp + 1, :], tpa[:, 128:256])
            au.op("vector", tpev, reads=[("p2tpa",)], writes=[("rtTb", b, dp)])
        rT = [("rtTb", b, dp) for dp in range(ND // 2)]
        for nmW, wsb, wkey, pp, ppk in (("q", wqsb, "wqsb", qpp, "qpp"),
                                        ("k", wksb, "wksb", kpp, "kpp"),
                                        ("v", wvsb, "wvsb", vpp, "vpp")):
            def pfn(t, b=b, wsb=wsb, pp=pp):
                for d in range(ND):
                    for qc in range(2):
                        i = t.matmul(pp[:, qc, :], rtTb[:, b, d, :],
                                     wsb[:, d, qc * 512:(qc + 1) * 512],
                                     start=(d == 0), stop=(d == ND - 1))
                return i
            au.op("tensor", pfn, reads=rT + [(wkey,)], writes=[(ppk,)])
        au.op("scalar", lambda s, b=b: s.activation(
            vtt[:, b, :], vpp.ap().rearrange("p a x -> p (a x)"), AF.Identity),
            xreads=[("vpp",)], writes=[("vtt", b)])
        au.op("sync", lambda sy, c=c, b=b: sy.dma_start(
            out=vtm_dram[c * 128:(c + 1) * 128, :], in_=vtt[:, b, :]),
            reads=[("vtt", b)], writes=[("vtmd", c)], dma=True, stream=f"p2vs{b}")
        for nmW, pp, ppk, ttt, tk, sqk, macc in (
                ("q", qpp, "qpp", qtt, "qtt", 0, 1), ("k", kpp, "kpp", ktt, "ktt", 0, 2)):
            au.op("vector", lambda v, pp=pp: v.tensor_copy(
                qkraw[:, :], pp.ap().rearrange("p a x -> p (a x)")),
                xreads=[(ppk,)], writes=[("qkraw",)])
            qkf = qkraw.ap()
            x1 = qkf[:, 0:1024:2]
            x2 = qkf[:, 1:1024:2]
            au.op("vector", lambda v, x1=x1, b=b: v.tensor_mul(rsc1[:, :], x1, ctab[:, b, :]),
                  reads=[("ctab", b), ("qkraw",)], writes=[("rsc1",)])
            au.op("vector", lambda v, x2=x2, b=b: v.tensor_mul(rsc2[:, :], x2, stab[:, b, :]),
                  reads=[("stab", b), ("qkraw",)], writes=[("rsc2",)])
            au.op("vector", lambda v, ttt=ttt, b=b: v.tensor_sub(
                ttt[:, b, 0:1024:2], rsc1[:, :], rsc2[:, :]),
                reads=[("rsc1",), ("rsc2",)], writes=[(tk, b, 0)])
            au.op("vector", lambda v, x1=x1, b=b: v.tensor_mul(rsc1[:, :], x1, stab[:, b, :]),
                  reads=[("stab", b), ("qkraw",)], writes=[("rsc1",)])
            au.op("vector", lambda v, x2=x2, b=b: v.tensor_mul(rsc2[:, :], x2, ctab[:, b, :]),
                  reads=[("ctab", b), ("qkraw",)], writes=[("rsc2",)])
            au.op("vector", lambda v, ttt=ttt, b=b: v.tensor_add(
                ttt[:, b, 1:1024:2], rsc1[:, :], rsc2[:, :]),
                reads=[("rsc1",), ("rsc2",)], writes=[(tk, b, 1)])
            au.op("scalar", lambda s, ttt=ttt, b=b: s.activation(
                sqsc[:, :], ttt[:, b, :], AF.Square, accum_out=nrm[:, 0:1]),
                reads=[(tk, b, 0), (tk, b, 1)], writes=[("nrm", 0), ("sqsc",)])
            au.op("vector", lambda v, macc=macc: v.tensor_max(
                nrm[:, macc:macc + 1], nrm[:, macc:macc + 1], nrm[:, 0:1]),
                reads=[("nrm", 0), ("nrm", macc)], writes=[("nrm", macc)])
            dstd = qtm_dram if tk == "qtt" else ktm_dram
            au.op("sync", lambda sy, c=c, b=b, ttt=ttt, dstd=dstd: sy.dma_start(
                out=dstd[c * 128:(c + 1) * 128, :], in_=ttt[:, b, :]),
                reads=[(tk, b, 0), (tk, b, 1)], writes=[(tk + "d", c)], dma=True,
                stream=f"p2{tk}s{b}")

    if tail_hook_reg is not None:
        tail_hook_reg(tileA)
    else:
        for c in range(NT):
            b = c % 2
            au.op("gpsimd", lambda g, c=c, b=b: g.dma_start(
                out=r2t[:, b, :], in_=r2_dram[c * 128:(c + 1) * 128, :]),
                reads=[("r2d", c)], writes=[("r2t", b)], dma=True, stream=f"p2r{b}")
            tileA(c, r2t[:, b, :], [("r2t", b)])

    def _cont():
        def finishA():
            au.op("tensor", lambda t: t.transpose(aux[0:1, 0:128], nrm[:, 1:2], identF[:, :]),
                  reads=[("nrm", 1), ("identF",)], writes=[("p2aux",)])
            au.op("vector", lambda v: v.reduce_max(sc11[0:1, 0:1], aux[0:1, 0:128], axis=AX.X),
                  xreads=[("p2aux",)], writes=[("sc11", 0)])
            au.op("tensor", lambda t: t.transpose(aux[0:1, 0:128], nrm[:, 2:3], identF[:, :]),
                  reads=[("nrm", 2), ("identF",)], writes=[("p2aux",)])
            au.op("vector", lambda v: v.reduce_max(sc11[0:1, 1:2], aux[0:1, 0:128], axis=AX.X),
                  xreads=[("p2aux",)], writes=[("sc11", 1)])
            au.op("vector", lambda v: v.tensor_mul(sc11[0:1, 2:3], sc11[0:1, 0:1], sc11[0:1, 1:2]),
                  reads=[("sc11", 0), ("sc11", 1)], writes=[("sc11", 2)])
            au.op("scalar", lambda s: s.activation(sc11[0:1, 3:4], sc11[0:1, 2:3], AF.Sqrt),
                  reads=[("sc11", 2)], writes=[("sc11", 3)])
            au.op("vector", lambda v: v.tensor_scalar(sc11[0:1, 4:5], sc11[0:1, 3:4], -SCALE, None,
                                                      OP.mult),
                  reads=[("sc11", 3)], writes=[("sc11", 4)])
            au.op("tensor", lambda t: t.matmul(aux[:, 0:1], onesrF[:, :], sc11[0:1, 4:5],
                                               start=True, stop=True),
                  reads=[("sc11", 4), ("onesrF",)], writes=[("p2aux",)])
            au.op("vector", lambda v: v.tensor_copy(nmhat[:, :], aux[:, 0:1]),
                  reads=[("p2aux",)], writes=[("nmhat",)])
            psA.close()

        finishA()
        sbA.close()
        au.fence()
        vhp = ctx.sbuf("vhp", [128, 2, NT, 128], BF16)
        kTp = ctx.sbuf("kTp", [128, 2, T], BF16)
        qTp = ctx.sbuf("qTp", [128, 2, 512], BF16)
        Ab = ctx.sbuf("p2A", [128, 3, 2, 512], BF16)
        osb = ctx.sbuf("osb", [128, NP, 8, 512], BF16)
        lrow = ctx.sbuf("lrow", [1, 2, 512], F32)
        rlrow = ctx.sbuf("rlrow", [1, 2, 512], F32)
        rlbsave = ctx.sbuf("rlbsave", [1, 2, 8, 512], BF16)

        with ExitStack() as psB:
            simE = [ctx.psum("p2simE0", [128, 512], F32, stack=psB),
                    ctx.psum("p2simE1", [128, 512], F32, stack=psB)]
            simO = [ctx.psum("p2simO0", [128, 512], F32, stack=psB),
                    ctx.psum("p2simO1", [128, 512], F32, stack=psB)]
            op_ = ctx.psum("p2op", [128, 512], F32, stack=psB)
            lps = ctx.psum("p2lps", [128, 2, 512], F32, stack=psB)
            pend = []
            for pr in range(NP):
                pb = pr % 2
                au.op("gpsimd", lambda g, pr=pr, pb=pb: g.dma_start(
                    out=vhp[:, pb, :, :],
                    in_=vtm_dram.rearrange("(tk p) d -> p tk d", p=128)[:, :, pr * 128:(pr + 1) * 128]),
                    reads=[("vtmd", cc) for cc in range(NT)],
                    writes=[("vhp", pb)], dma=True, stream=f"p2vl{pb}")
                au.op("sync", lambda sy, pr=pr, pb=pb: sy.dma_start(
                    out=kTp[:, pb, :], in_=ktm_dram[:, pr * 128:(pr + 1) * 128], transpose=True),
                    reads=[("kttd", cc) for cc in range(NT)],
                    writes=[("kTp", pb)], dma=True, stream="p2ktr")
                for qc in range(8):
                    qb = qc % 2
                    au.op("sync", lambda sy, pr=pr, qc=qc, qb=qb: sy.dma_start(
                        out=qTp[:, qb, :],
                        in_=qtm_dram[qc * 512:(qc + 1) * 512, pr * 128:(pr + 1) * 128],
                        transpose=True),
                        reads=[("qttd", cc) for cc in range(qc * 4, qc * 4 + 4)],
                        writes=[("qTp", qb)], dma=True, stream="p2qtr")
                    ntk = qc * 4 + 4

                    def mk_sim(tk, pb=pb, qb=qb):
                        bk = tk % 2

                        def simfn(t, tk=tk, bk=bk, pb=pb, qb=qb):
                            t.matmul(simE[bk][:, :], kTp[0:64, pb, tk * 128:(tk + 1) * 128],
                                     qTp[0:64, qb, :], start=True, stop=True,
                                     tile_position=(0, 0), skip_group_check=True)
                            return t.matmul(simO[bk][:, :], kTp[64:128, pb, tk * 128:(tk + 1) * 128],
                                            qTp[64:128, qb, :], start=True, stop=True,
                                            tile_position=(64, 0), skip_group_check=True)
                        au.op("tensor", simfn, reads=[("kTp", pb), ("qTp", qb)],
                              writes=[("p2sE", tk % 2), ("p2sO", tk % 2)])

                    def mk_av(tk, pb=pb, ntk=ntk):
                        bk = tk % 3

                        def avfn(t, tk=tk, bk=bk, pb=pb, ntk=ntk):
                            for h in range(2):
                                t.matmul(op_[h * 64:(h + 1) * 64, :], vhp[:, pb, tk, h * 64:(h + 1) * 64],
                                         Ab[:, bk, h, :], start=(tk == 0), stop=False,
                                         tile_position=(0, h * 64), skip_group_check=True)
                                i = t.matmul(lps[0:1, h, :], onescB2[:, :], Ab[:, bk, h, :],
                                             start=(tk == 0), stop=(tk == ntk - 1),
                                             skip_group_check=True)
                            return i
                        au.op("tensor", avfn,
                              reads=[("p2A", bk, 0), ("p2A", bk, 1), ("vhp", pb), ("onescB2",)],
                              writes=[("p2op",), ("p2lps",)] if (tk == 0 or tk == ntk - 1)
                              else [("p2opacc",)])

                    mk_sim(0)
                    while pend:
                        pend.pop(0)()
                    for tk in range(ntk):
                        bk = tk % 3
                        sk = tk % 2
                        for h in range(2):
                            sp = simE if h == 0 else simO
                            spk = ("p2sE", sk) if h == 0 else ("p2sO", sk)
                            au.op("scalar", lambda s, sp=sp, sk=sk, bk=bk, h=h: s.activation(
                                Ab[:, bk, h, :], sp[sk][:, :], AF.Exp,
                                bias=nmhat[:, :], scale=SCALE),
                                reads=[("nmhat",)], xreads=[spk], writes=[("p2A", bk, h)])
                            if tk >= qc * 4:
                                au.op("gpsimd", lambda g, bk=bk, qc=qc, tk=tk, h=h: g.affine_select(
                                    Ab[:, bk, h, :], Ab[:, bk, h, :], pattern=[[1, 512]],
                                    compare_op=OP.is_ge, fill=0.0,
                                    base=qc * 512 - tk * 128, channel_multiplier=-1),
                                    reads=[("p2A", bk, h)], writes=[("p2A", bk, h)])
                        if tk + 1 < ntk:
                            mk_sim(tk + 1)
                        if tk >= 2:
                            mk_av(tk - 2)
                    if ntk >= 2:
                        mk_av(ntk - 2)
                    mk_av(ntk - 1)
                    au.op("scalar", lambda s, pr=pr, qc=qc: s.activation(
                        osb[:, pr, qc, :], op_[:, :], AF.Identity),
                        xreads=[("p2op",)], writes=[("osb", pr, qc)])
                    au.op("vector", lambda v: v.tensor_copy(
                        lrow.ap().rearrange("o a x -> o (a x)"),
                        lps.ap().rearrange("p a x -> p (a x)")[0:1, :]),
                        xreads=[("p2lps",)], writes=[("lrow",)])
                    au.op("vector", lambda v: v.reciprocal(
                        rlrow.ap().rearrange("o a x -> o (a x)"),
                        lrow.ap().rearrange("o a x -> o (a x)")),
                        reads=[("lrow",)], writes=[("rlrow",)])
                    def rlcp(v, qc=qc):
                        v.tensor_copy(rlbsave[:, 0, qc, :], rlrow[:, 0, :])
                        return v.tensor_copy(rlbsave[:, 1, qc, :], rlrow[:, 1, :])
                    au.op("vector", rlcp, reads=[("rlrow",)], writes=[("rlb", qc)])

                    def finalize(pr=pr, qc=qc):
                        def rlbcfn(t):
                            t.matmul(op_[0:64, :], onesrB[0:1, 0:64], rlbsave[0:1, 0, qc, :],
                                     start=True, stop=True, tile_position=(0, 0),
                                     skip_group_check=True)
                            return t.matmul(op_[64:128, :], onesrB[0:1, 0:64], rlbsave[0:1, 1, qc, :],
                                            start=True, stop=True, tile_position=(0, 64),
                                            skip_group_check=True)
                        au.op("tensor", rlbcfn, reads=[("rlb", qc), ("onesrB",)], writes=[("p2op",)])
                        au.op("vector", lambda v: v.tensor_mul(
                            osb[:, pr, qc, :], osb[:, pr, qc, :], op_[:, :]),
                            reads=[("osb", pr, qc)], xreads=[("p2op",)], writes=[("osb", pr, qc)])
                    pend.append(finalize)
            while pend:
                pend.pop(0)()
            for qc in range(8):
                OS = [("osb", pr, qc) for pr in range(NP)]
                for ts in range(4):
                    tb = ts % 2
                    c = qc * 4 + ts

                    def opfn(t, ts=ts, qc=qc):
                        for pr in range(NP):
                            for dc in range(2):
                                i = t.matmul(lps[:, dc, :], osb[:, pr, qc, ts * 128:(ts + 1) * 128],
                                             wosb[:, pr, dc * 512:(dc + 1) * 512],
                                             start=(pr == 0), stop=(pr == NP - 1),
                                             skip_group_check=True)
                        return i
                    au.op("tensor", opfn, reads=OS + [("wosb",), ("p2lps",)], writes=[("opp",)])
                    au.op("gpsimd", lambda g, c=c, tb=tb: g.dma_start(
                        out=r2t[:, tb, :], in_=r2_dram[c * 128:(c + 1) * 128, :]),
                        reads=[("r2d", c)],
                        writes=[("r2t", tb), ("p2rk", tb, 0), ("p2rk", tb, 1)], dma=True,
                        stream=f"p2rb{tb}")
                    au.op("scalar", lambda s, tb=tb: s.activation(
                        tob[:, tb, :], lps.ap().rearrange("p a x -> p (a x)"), AF.Identity),
                        xreads=[("opp",)], writes=[("tob", tb)])
                    au.op("vector", lambda v, tb=tb: v.tensor_add(tob[:, tb, :], tob[:, tb, :],
                                                                  bobc[:, :]),
                          reads=[("tob", tb), ("bobc", 0), ("bobc", 1)], writes=[("tob", tb)])
                    au.op("vector", lambda v, tb=tb: v.tensor_add(
                        tob[:, tb, :], tob[:, tb, :], r2t[:, tb, :]),
                        reads=[("tob", tb), ("r2t", tb), ("p2rk", tb, 0)],
                        writes=[("tob", tb), ("p2rk", tb, 0), ("p2rk", tb, 1)])
                    rkeys = [("tob", tb), ("p2rk", tb, 0), ("p2rk", tb, 1)]
                    emit_ln_inplace(ctx, au, tob[:, tb, :], rkeys, lnb2[tb], tag=20 + tb)
                    au.op("sync", lambda sy, c=c, tb=tb: sy.dma_start(
                        out=r3_dram[c * 128:(c + 1) * 128, :], in_=tob[:, tb, :]),
                        reads=rkeys, writes=[("r3d", c)], dma=True, stream=f"p2o{tb}")
        return [("r3d", c) for c in range(NT)]

    if tail_hook_reg is not None:
        return _cont
    return _cont()


def emit_p3(ctx, au, r3_dram, w1_dram, b1_dram, w2_dram, b2_dram, out_dram):
    nc = ctx.nc
    identB, identF = ctx.identB, ctx.identF
    INTER = 4096
    NF = INTER // 128

    w1sb = ctx.sbuf("w1sb", [128, ND, INTER], BF16)
    w2sb = ctx.sbuf("w2sb", [128, NF, D], BF16)
    b1c = ctx.sbuf("b1c", [128, NF], F32)
    b2bc = ctx.sbuf("b2bc", [128, D], F32)
    r3t = ctx.sbuf("r3t", [128, 2, D], F32)
    r3t2 = ctx.sbuf("r3t2", [128, 2, D], F32)
    r3Tb = ctx.sbuf("r3Tb", [128, 2, ND, 256], BF16)
    hck = ctx.sbuf("hck", [128, NF, 256], BF16)
    yt3 = ctx.sbuf("yt3", [128, 2, D], F32)
    onesrF3 = ctx.sbuf("onesrF3", [1, 128], F32)
    sqs3 = ctx.sbuf("p3sqs", [128, 1, D], BF16)
    mu3 = ctx.sbuf("p3mu", [128, 2], F32)
    ssq3 = ctx.sbuf("p3ssq", [128, 2], F32)
    rstd3 = ctx.sbuf("p3rstd", [128, 2], F32)
    mb3 = ctx.sbuf("p3mb", [128, 2], F32)
    lnb3 = [(sqs3[:, 0, :], mu3[:, b:b + 1], ssq3[:, b:b + 1], rstd3[:, b:b + 1],
             mb3[:, b:b + 1]) for b in range(2)]

    au.op("gpsimd", lambda g: g.iota(onesrF3[:, :], pattern=[[0, 128]], base=1,
                                     channel_multiplier=0, allow_small_or_imprecise_dtypes=True),
          writes=[("onesrF3",)])
    au.op("gpsimd", lambda g: g.dma_start(
        out=w1sb[:, :, :], in_=w1_dram.rearrange("(dt p) f -> p dt f", p=128)),
        writes=[("w1sb",)], dma=True, stream="p3w1")
    au.op("gpsimd", lambda g: g.dma_start(
        out=w2sb[:, :, :], in_=w2_dram.rearrange("(ft p) d -> p ft d", p=128)),
        writes=[("w2sb",)], dma=True, stream="p3w2")
    _p3b = ExitStack()
    b1row = ctx.sbuf("b1row", [1, INTER], F32, stack=_p3b)
    b2row = ctx.sbuf("b2row", [1, D], F32, stack=_p3b)
    au.op("gpsimd", lambda g: g.dma_start(out=b1row[:, :], in_=b1_dram[None, :]),
          writes=[("b1row",)], dma=True, stream="p3b1")
    au.op("gpsimd", lambda g: g.dma_start(out=b2row[:, :], in_=b2_dram[None, :]),
          writes=[("b2row",)], dma=True, stream="p3b2")

    with ExitStack() as ps3:
        tp3 = ctx.psum("p3tp", [128, 512], F32, stack=ps3)
        hps3 = [ctx.psum("p3hA", [128, 256], F32, stack=ps3),
                ctx.psum("p3hB", [128, 256], F32, stack=ps3)]
        yps3 = [ctx.psum("p3yA", [128, 2, 512], F32, stack=ps3),
                ctx.psum("p3yB", [128, 2, 512], F32, stack=ps3)]

        def b1cfn(t):
            for ft in range(NF):
                i = t.transpose(tp3[:, ft:ft + 1], b1row[0:1, ft * 128:(ft + 1) * 128],
                                identF[0:1, 0:1])
            return i
        au.op("tensor", b1cfn, reads=[("b1row",), ("identF",)], writes=[("p3tp",)])
        au.op("vector", lambda v: v.tensor_copy(b1c[:, :], tp3[:, 0:NF]),
              reads=[("p3tp",)], writes=[("b1c",)])
        au.op("tensor", lambda t: t.matmul(tp3[:, 0:512], onesrF3[:, :], b2row[0:1, 0:512],
                                           start=True, stop=True),
              reads=[("onesrF3",), ("b2row",)], writes=[("p3tp",)])
        au.op("vector", lambda v: v.tensor_copy(b2bc[:, 0:512], tp3[:, 0:512]),
              reads=[("p3tp",)], writes=[("b2bc", 0)])
        au.op("tensor", lambda t: t.matmul(tp3[:, 0:512], onesrF3[:, :], b2row[0:1, 512:1024],
                                           start=True, stop=True),
              reads=[("onesrF3",), ("b2row",)], writes=[("p3tp",)])
        au.op("vector", lambda v: v.tensor_copy(b2bc[:, 512:1024], tp3[:, 0:512]),
              reads=[("p3tp",)], writes=[("b2bc", 1)])
        au.fence()
        _p3b.close()

        def emit_chunk_T(tc):
            for tt in range(2):
                c = tc * 2 + tt
                b = c % 2
                au.op("gpsimd", lambda g, c=c, b=b: g.dma_start(
                    out=r3t[:, b, :], in_=r3_dram[c * 128:(c + 1) * 128, :]),
                    writes=[("r3t", b)], dma=True, stream=f"p3r{b}")
                for dp in range(ND // 2):
                    def t3fn(t, b=b, dp=dp):
                        t.transpose(tp3[:, 0:128], r3t[:, b, (2 * dp) * 128:(2 * dp + 1) * 128],
                                    identF[:, :])
                        return t.transpose(tp3[:, 128:256],
                                           r3t[:, b, (2 * dp + 1) * 128:(2 * dp + 2) * 128],
                                           identF[:, :])
                    au.op("tensor", t3fn, reads=[("r3t", b), ("identF",)], writes=[("p3tp",)])

                    def t3ev(v, tt=tt, dp=dp, tc=tc):
                        v.tensor_copy(r3Tb[:, tc % 2, 2 * dp, tt * 128:(tt + 1) * 128],
                                      tp3[:, 0:128])
                        return v.tensor_copy(r3Tb[:, tc % 2, 2 * dp + 1, tt * 128:(tt + 1) * 128],
                                             tp3[:, 128:256])
                    au.op("vector", t3ev, reads=[("p3tp",)],
                          writes=[("r3Tb", tc % 2, 2 * dp, tt), ("r3Tb", tc % 2, 2 * dp + 1, tt)])

        emit_chunk_T(0)
        for tc in range(16):
            R3T = [("r3Tb", tc % 2, d, tt) for d in range(ND) for tt in range(2)]
            for ft in range(NF):
                fb = ft % 2

                def hfn(t, ft=ft, fb=fb, tc=tc):
                    for d in range(ND):
                        i = t.matmul(hps3[fb][:, :], w1sb[:, d, ft * 128:(ft + 1) * 128],
                                     r3Tb[:, tc % 2, d, :], start=(d == 0), stop=(d == ND - 1))
                    return i
                au.op("tensor", hfn, reads=R3T + [("w1sb",)], writes=[("p3h", fb)])
                au.op("scalar", lambda s, ft=ft, fb=fb: s.activation(
                    hck[:, ft, :], hps3[fb][:, :], AF.Relu, bias=b1c[:, ft:ft + 1]),
                    reads=[("b1c",)], xreads=[("p3h", fb)], writes=[("hck", ft)])
            HCK = [("hck", ft) for ft in range(NF)]
            if tc + 1 < 16:
                emit_chunk_T(tc + 1)
            for tt in range(2):
                c = tc * 2 + tt
                yb = tt % 2

                def yfn(t, tt=tt, yb=yb):
                    for ft in range(NF):
                        for qc in range(2):
                            i = t.matmul(yps3[yb][:, qc, :], hck[:, ft, tt * 128:(tt + 1) * 128],
                                         w2sb[:, ft, qc * 512:(qc + 1) * 512],
                                         start=(ft == 0), stop=(ft == NF - 1))
                    return i
                au.op("tensor", yfn, reads=HCK + [("w2sb",)], writes=[("p3y", yb)])
                b = (c % 2) + 2
                au.op("gpsimd", lambda g, c=c, b=b: g.dma_start(
                    out=r3t2[:, b - 2, :], in_=r3_dram[c * 128:(c + 1) * 128, :]),
                    writes=[("r3t2", b - 2), ("p3rk", b, 0), ("p3rk", b, 1)], dma=True,
                    stream=f"p3rr{b - 2}")
                au.op("scalar", lambda s, yb=yb, b=b: s.activation(
                    yt3[:, b - 2, :], yps3[yb].ap().rearrange("p a x -> p (a x)"), AF.Identity),
                    xreads=[("p3y", yb)], writes=[("yt3", b - 2)])
                au.op("vector", lambda v, b=b: v.tensor_add(yt3[:, b - 2, :], yt3[:, b - 2, :],
                                                            b2bc[:, :]),
                      reads=[("yt3", b - 2), ("b2bc", 0), ("b2bc", 1)], writes=[("yt3", b - 2)])
                au.op("vector", lambda v, b=b: v.tensor_add(yt3[:, b - 2, :], yt3[:, b - 2, :],
                                                            r3t2[:, b - 2, :]),
                      reads=[("yt3", b - 2), ("r3t2", b - 2), ("p3rk", b, 0)],
                      writes=[("yt3", b - 2), ("p3rk", b, 0), ("p3rk", b, 1)])
                rkeys = [("yt3", b - 2), ("p3rk", b, 0), ("p3rk", b, 1)]
                emit_ln_inplace(ctx, au, yt3[:, b - 2, :], rkeys, lnb3[b - 2], tag=30 + b)
                au.op("sync", lambda sy, c=c, b=b: sy.dma_start(
                    out=out_dram[c * 128:(c + 1) * 128, :], in_=yt3[:, b - 2, :]),
                    reads=rkeys, writes=[("outd", c)], dma=True, stream=f"p3o{b - 2}")
    return [("outd", c) for c in range(NT)]


from concourse.bass_utils import run_bass_kernel_spmd

INTER = 4096


def _rope_tables():
    d = 64
    inv = 1.0 / 10000.0 ** (np.arange(0, d, 2, dtype=np.float64) / d)
    pos = np.arange(T, dtype=np.float64)
    ang = pos[:, None] * inv[None, :]
    ct = np.tile(np.cos(ang), (1, 16)).astype(np.float32)
    st = np.tile(np.sin(ang), (1, 16)).astype(np.float32)
    return ct, st


def _build(phase):
    nc = bass.Bass("TRN2", target_bir_lowering=False)
    with ExitStack() as stack:
        block = stack.enter_context(nc.Block())
        ctx = Ctx(nc, block, stack)
        au = AS(nc, block, stack)
        emit_ident(ctx, au)
        if phase == 0:
            x = nc.dram_tensor("x", [T, D], F32, kind="ExternalInput").ap()
            wqk = nc.dram_tensor("wqk", [2048, D], F32, kind="ExternalInput").ap()
            bqk = nc.dram_tensor("bqk", [2048], F32, kind="ExternalInput").ap()
            r1 = nc.dram_tensor("r1", [T, D], F32, kind="ExternalOutput").ap()
            xb = nc.dram_tensor("xb", [T, D], BF16, kind="Internal").ap()
            attnT = nc.dram_tensor("attnT", [NP, 128, T], BF16, kind="Internal").ap()
            outk = emit_p0(ctx, au, x, wqk, bqk, xb, attnT, r1)
        elif phase == 1:
            r1 = nc.dram_tensor("r1", [T, D], F32, kind="ExternalInput").ap()
            gw = nc.dram_tensor("gw", [8, D], F32, kind="ExternalInput").ap()
            w1 = nc.dram_tensor("w1", [8, D, INTER], F32, kind="ExternalInput").ap()
            w2 = nc.dram_tensor("w2", [8, INTER, D], F32, kind="ExternalInput").ap()
            r2 = nc.dram_tensor("r2", [T, D], F32, kind="ExternalOutput").ap()
            cx = nc.dram_tensor("cx", [8, KCAP, D], BF16, kind="Internal").ap()
            yd = nc.dram_tensor("yd", [8, KCAP, D], F32, kind="Internal").ap()
            probs, mprob, idxu = emit_p1(ctx, au, r1, gw, w1, w2, cx, yd, r2)
            emit_p1_experts(ctx, au, w1, w2, cx, yd, r1, idxu)
            outk = emit_p1_tail(ctx, au, r1, yd, r2, mprob, idxu)
        elif phase == 2:
            r2 = nc.dram_tensor("r2", [T, D], F32, kind="ExternalInput").ap()
            wq = nc.dram_tensor("wq", [D, D], F32, kind="ExternalInput").ap()
            wk = nc.dram_tensor("wk", [D, D], F32, kind="ExternalInput").ap()
            wv = nc.dram_tensor("wv", [D, D], F32, kind="ExternalInput").ap()
            wo = nc.dram_tensor("wo", [D, D], F32, kind="ExternalInput").ap()
            bo = nc.dram_tensor("bo", [D], F32, kind="ExternalInput").ap()
            ct = nc.dram_tensor("ct", [T, 512], F32, kind="ExternalInput").ap()
            st = nc.dram_tensor("st", [T, 512], F32, kind="ExternalInput").ap()
            r3 = nc.dram_tensor("r3", [T, D], F32, kind="ExternalOutput").ap()
            qtm = nc.dram_tensor("qtm", [T, D], BF16, kind="Internal").ap()
            ktm = nc.dram_tensor("ktm", [T, D], BF16, kind="Internal").ap()
            vtm = nc.dram_tensor("vtm", [T, D], BF16, kind="Internal").ap()
            outk = emit_p2(ctx, au, r2, wq, wk, wv, wo, bo, ct, st, qtm, ktm, vtm, r3)
        else:
            r3 = nc.dram_tensor("r3", [T, D], F32, kind="ExternalInput").ap()
            w1 = nc.dram_tensor("w1", [D, INTER], F32, kind="ExternalInput").ap()
            b1 = nc.dram_tensor("b1", [INTER], F32, kind="ExternalInput").ap()
            w2 = nc.dram_tensor("w2", [INTER, D], F32, kind="ExternalInput").ap()
            b2 = nc.dram_tensor("b2", [D], F32, kind="ExternalInput").ap()
            out = nc.dram_tensor("out", [T, D], F32, kind="ExternalOutput").ap()
            outk = emit_p3(ctx, au, r3, w1, b1, w2, b2, out)
        au.op("sync", lambda sy: sy.nop(), reads=outk)
        au.flush()
    return nc


def _build_fused():
    nc = bass.Bass("TRN2", target_bir_lowering=False)
    x = nc.dram_tensor("x", [T, D], F32, kind="ExternalInput").ap()
    wqk = nc.dram_tensor("wqk", [2048, D], F32, kind="ExternalInput").ap()
    bqk = nc.dram_tensor("bqk", [2048], F32, kind="ExternalInput").ap()
    gw = nc.dram_tensor("gw", [8, D], F32, kind="ExternalInput").ap()
    w1 = nc.dram_tensor("w1", [8, D, INTER], F32, kind="ExternalInput").ap()
    w2 = nc.dram_tensor("w2", [8, INTER, D], F32, kind="ExternalInput").ap()
    wq = nc.dram_tensor("wq", [D, D], F32, kind="ExternalInput").ap()
    wk = nc.dram_tensor("wk", [D, D], F32, kind="ExternalInput").ap()
    wv = nc.dram_tensor("wv", [D, D], F32, kind="ExternalInput").ap()
    wo = nc.dram_tensor("wo", [D, D], F32, kind="ExternalInput").ap()
    bo = nc.dram_tensor("bo", [D], F32, kind="ExternalInput").ap()
    fw1 = nc.dram_tensor("fw1", [D, INTER], F32, kind="ExternalInput").ap()
    fb1 = nc.dram_tensor("fb1", [INTER], F32, kind="ExternalInput").ap()
    fw2 = nc.dram_tensor("fw2", [INTER, D], F32, kind="ExternalInput").ap()
    fb2 = nc.dram_tensor("fb2", [D], F32, kind="ExternalInput").ap()
    ct = nc.dram_tensor("ct", [T, 512], F32, kind="ExternalInput").ap()
    st = nc.dram_tensor("st", [T, 512], F32, kind="ExternalInput").ap()
    out = nc.dram_tensor("out", [T, D], F32, kind="ExternalOutput").ap()
    r1 = nc.dram_tensor("r1", [T, D], F32, kind="Internal").ap()
    r2 = nc.dram_tensor("r2", [T, D], F32, kind="Internal").ap()
    r3 = nc.dram_tensor("r3", [T, D], F32, kind="Internal").ap()
    xb = nc.dram_tensor("xb", [T, D], BF16, kind="Internal").ap()
    attnT = nc.dram_tensor("attnT", [NP, 128, T], BF16, kind="Internal").ap()
    cx = nc.dram_tensor("cx", [8, KCAP, D], BF16, kind="Internal").ap()
    yd = nc.dram_tensor("yd", [8, KCAP, D], F32, kind="Internal").ap()
    qtm = nc.dram_tensor("qtm", [T, D], BF16, kind="Internal").ap()
    ktm = nc.dram_tensor("ktm", [T, D], BF16, kind="Internal").ap()
    vtm = nc.dram_tensor("vtm", [T, D], BF16, kind="Internal").ap()
    with ExitStack() as stack:
        block = stack.enter_context(nc.Block())
        ctx = Ctx(nc, block, stack)
        au = AS(nc, block, stack)
        main_stack = ctx.stack
        emit_ident(ctx, au)
        ctx.stack = ExitStack()
        emit_p0(ctx, au, x, wqk, bqk, xb, attnT, r1)
        ctx.stack.close()
        au.fence()
        p1stack = ExitStack()
        ctx.stack = p1stack
        probs, mprob, idxu = emit_p1(ctx, au, r1, gw, w1, w2, cx, yd, r2)
        estack = ExitStack()
        ctx.stack = estack
        emit_p1_experts(ctx, au, w1, w2, cx, yd, r1, idxu)
        estack.close()
        au.fence()
        tailstack = ExitStack()
        ctx.stack = tailstack
        hookbox = {}
        def tail_hook_reg(tileA):
            hookbox["tileA"] = tileA
        ctx.stack = ExitStack()
        p2stack = ctx.stack
        emit_p2_pending = emit_p2(ctx, au, r2, wq, wk, wv, wo, bo, ct, st, qtm, ktm, vtm, r3,
                                  tail_hook_reg=tail_hook_reg)
        ctx.stack = tailstack
        emit_p1_tail(ctx, au, r1, yd, r2, mprob, idxu,
                     after_tile=lambda c, rt, rk: hookbox["tileA"](c, rt, rk))
        tailstack.close()
        ctx.stack = p2stack
        emit_p2_pending()
        p2stack.close()
        p1stack.close()
        au.fence()
        ctx.stack = ExitStack()
        outk = emit_p3(ctx, au, r3, fw1, fb1, fw2, fb2, out)
        ctx.stack.close()
        ctx.stack = main_stack
        au.op("sync", lambda sy: sy.nop(), reads=outk)
        au.flush()
    return nc


def kernel(L0_attn_x, L0_attn_W_qk, L0_attn_b_qk, L0_ffn_gate_w, L0_ffn_w1, L0_ffn_w2,
           L1_attn_Wq, L1_attn_Wk, L1_attn_Wv, L1_attn_Wo, L1_attn_bo,
           L1_ffn_W1, L1_ffn_b1, L1_ffn_W2, L1_ffn_b2):
    n = 8
    cores = list(range(n))
    ct, st = _rope_tables()
    A = np.ascontiguousarray
    base = {
        "wqk": A(np.asarray(L0_attn_W_qk)), "bqk": A(np.asarray(L0_attn_b_qk)),
        "gw": A(np.asarray(L0_ffn_gate_w)), "w1": A(np.asarray(L0_ffn_w1)),
        "w2": A(np.asarray(L0_ffn_w2)),
        "wq": A(np.asarray(L1_attn_Wq)), "wk": A(np.asarray(L1_attn_Wk)),
        "wv": A(np.asarray(L1_attn_Wv)), "wo": A(np.asarray(L1_attn_Wo)),
        "bo": A(np.asarray(L1_attn_bo)),
        "fw1": A(np.asarray(L1_ffn_W1)), "fb1": A(np.asarray(L1_ffn_b1)),
        "fw2": A(np.asarray(L1_ffn_W2)), "fb2": A(np.asarray(L1_ffn_b2)),
        "ct": ct, "st": st,
    }
    in_maps = []
    for i in range(n):
        m = dict(base)
        m["x"] = A(np.asarray(L0_attn_x)[i])
        in_maps.append(m)
    nc = _build_fused()
    res = run_bass_kernel_spmd(nc, in_maps, core_ids=cores)
    return np.stack([res.results[i]["out"] for i in range(n)], axis=0).astype(np.float32)
